# version 04 (final)
# speedup vs baseline: 1.3179x; 1.0513x over previous
import contextlib
import numpy as np
import ml_dtypes
import concourse.bass as bass
import concourse.mybir as mybir
from concourse.bass_utils import run_bass_kernel_spmd

F32 = mybir.dt.float32
BF16 = mybir.dt.bfloat16
AF = mybir.ActivationFunctionType
ALU = mybir.AluOpType
AX = mybir.AxisListType

D = 1024
T = 2048
NB = 8
DC = D // 128
TT = T // 512

ENGS = ("pe", "act", "dve", "pool", "sp")
SEMS = (("pe", "act", "dve", "pool", "sp", "dcc")
        + tuple(f"dsp{i}" for i in range(8)) + tuple(f"dpool{i}" for i in range(8)))


class Buf:
    __slots__ = ("name", "tensor", "w", "readers")

    def __init__(self, name, tensor=None):
        self.name = name
        self.tensor = tensor if tensor is not None else name.split("_")[0]
        self.w = None
        self.readers = {}


class Ctx:
    def __init__(self, nc):
        self.nc = nc
        self.sem = {}
        self.cnt = {s: 0 for s in SEMS}
        self.dma_rot = {"dsp": 0, "dpool": 0}
        self.strict_same_engine = True
        self.known = {e: {} for e in ENGS}
        self.q = {e: [] for e in ENGS}
        self.bufs = {}
        self.tensor_act = {}
        self.whitelist = set()
        self.stack = contextlib.ExitStack()
        self.sb = {}
        self.ps = []
        self.psb = []
        self.dram = {}

    def buf(self, name, tensor=None):
        if name not in self.bufs:
            self.bufs[name] = Buf(name, tensor)
        return self.bufs[name]

    def alloc_sems(self):
        for s in SEMS:
            self.sem[s] = self.stack.enter_context(self.nc.semaphore("s_" + s))

    def sbuf(self, name, shape, dtype):
        t = self.stack.enter_context(self.nc.sbuf_tensor(name, shape, dtype))
        self.sb[name] = t
        return t

    def dram_scratch(self, name, shape, dtype):
        t = self.nc.dram_tensor(name, shape, dtype)
        self.dram[name] = t
        return t

    def op(self, eng, fn, reads=(), writes=(), kind="compute", nd=1):
        if kind == "compute":
            csem, inc = eng, 1
        elif kind == "dma":
            q = "dsp" if eng in ("sp", "act") else "dpool"
            j = self.dma_rot[q]
            self.dma_rot[q] = (j + 1) % 8
            csem, inc = f"{q}{j}", 16 * nd
        elif kind == "cc":
            csem, inc = "dcc", 1
        else:
            raise ValueError(kind)

        need = {}
        if kind == "dma":
            prev = self.cnt[csem]
            if prev > 0 and self.known[eng].get(csem, 0) < prev:
                need[csem] = prev
        def _need(s, v):
            if kind == "compute" and s == eng and (
                    not self.strict_same_engine or eng == "pe"):
                return
            if self.known[eng].get(s, 0) >= v:
                return
            need[s] = max(need.get(s, 0), v)
        for b in reads:
            if b.w:
                _need(*b.w)
        for b in writes:
            if b.w:
                _need(*b.w)
            for s, v in b.readers.items():
                _need(s, v)
            if b.tensor not in self.whitelist:
                for s, v in self.tensor_act.get(b.tensor, {}).items():
                    _need(s, v)
        for s, v in need.items():
            self.known[eng][s] = v

        cval = self.cnt[csem] + inc
        self.cnt[csem] = cval
        for b in reads:
            b.readers[csem] = max(b.readers.get(csem, 0), cval)
        for b in writes:
            b.w = (csem, cval)
            b.readers = {}
        for b in list(reads) + list(writes):
            ta = self.tensor_act.setdefault(b.tensor, {})
            ta[csem] = max(ta.get(csem, 0), cval)

        sem = self.sem
        waits = sorted(need.items())

        def emit(e):
            for s, v in waits:
                e.wait_ge(sem[s], v)
            inst = fn(e)
            if isinstance(inst, list):
                for ii in inst:
                    ii.then_inc(sem[csem], 16)
            else:
                (inst if inst is not None else e.nop()).then_inc(sem[csem], inc)

        self.q[eng].append(emit)

    def barrier_val(self):
        return dict(self.cnt)

    def wait_counts(self, eng, counts):
        for s, v in counts.items():
            if v <= 0 or (s == eng):
                continue
            if self.known[eng].get(s, 0) >= v:
                continue
            self.known[eng][s] = v
            sem = self.sem

            def emit(e, s=s, v=v):
                e.wait_ge(sem[s], v)

            self.q[eng].append(emit)



def build_consts():
    bf = {}
    f32c = {}
    bf["id128"] = np.eye(128, dtype=np.float32)
    bf["ones_col"] = np.ones((128, 1), np.float32)
    onesr = np.zeros((128, 128), np.float32)
    onesr[0, :] = 1.0
    bf["ones_row"] = onesr
    f32c["id128f"] = np.eye(128, dtype=np.float32)
    f32c["ones_col"] = np.ones((128, 1), np.float32)
    f32c["ones_row"] = onesr.copy()
    f32c["eps1e5"] = np.full((128, 1), 1e-5, np.float32)
    f32c["eps1e6"] = np.full((128, 1), 1e-6, np.float32)
    f32c["one"] = np.ones((128, 1), np.float32)
    UNIV_BF = 257
    UNIV_F32 = 260
    s8 = np.zeros((128, 8 * 128), np.float32)
    for e in range(8):
        s8[e, e * 128:(e + 1) * 128] = 1.0
    bf["sel8"] = s8
    s16 = np.zeros((128, 16 * 128), np.float32)
    for n in range(16):
        s16[n, n * 128:(n + 1) * 128] = 1.0
    bf["sel16"] = s16
    i = np.arange(128)[:, None]
    v = np.arange(896)[None, :]
    bf["cmask_ext"] = (i <= v - 384).astype(np.float32)
    jc = np.arange(256)[None, :]
    f32c["wmask"] = np.where((jc >= i) & (jc <= i + 128), 0.0, -1e9).astype(np.float32)
    jk = np.arange(128)[None, :]
    f32c["wmask0"] = np.where(jk <= i, 0.0, -1e9).astype(np.float32)

    def pack(d, dtype):
        offs = {}
        cols = 0
        out_parts = []
        for k, vv in d.items():
            a = np.zeros((128, vv.shape[1]), np.float32)
            a[:vv.shape[0]] = vv
            offs[k] = (cols, vv.shape[1])
            cols += vv.shape[1]
            out_parts.append(a.astype(dtype))
        return np.concatenate(out_parts, axis=1), offs

    cb, ob = pack(bf, ml_dtypes.bfloat16)
    cf, of = pack(f32c, np.float32)
    return cb, ob, cf, of, UNIV_BF, UNIV_F32


CONST_BF, CONST_BF_OFF, CONST_F32, CONST_F32_OFF, UNIV_BF, UNIV_F32 = build_consts()


def cbf(ctx, name, rows=128):
    o, w = CONST_BF_OFF[name]
    assert o + w <= UNIV_BF, name
    return ctx.sb["cbf"][:rows, o:o + w]


def load_lconsts(ctx, names_bf=(), names_f32=()):
    out = {}
    col = 0
    CL = ctx.sb["CONSTL"]
    for nm in names_bf:
        o, w = CONST_BF_OFF[nm]
        b = ctx.buf(f"CONSTL_{nm}")
        ap = CL[:, col:col + w]
        ctx.op("sp", lambda e, ap=ap, o=o, w=w: e.dma_start(
            out=ap, in_=ctx.cbf_dram[:, o:o + w]),
            reads=[ctx.buf("cbfd", tensor="cbfd")], writes=[b], kind="dma")
        out[nm] = (col, w, b)
        col += w
    fcol = 1024
    CLF = CL[:, :].bitcast(F32)
    for nm in names_f32:
        o, w = CONST_F32_OFF[nm]
        fcol -= w
        assert fcol * 2 >= col
        b = ctx.buf(f"CONSTL_f{nm}")
        ap = CLF[:, fcol:fcol + w]
        ctx.op("sp", lambda e, ap=ap, o=o, w=w: e.dma_start(
            out=ap, in_=ctx.cf32_dram[:, o:o + w]),
            reads=[ctx.buf("cf32d", tensor="cf32d")], writes=[b], kind="dma")
        out[nm] = (fcol, w, b)
    return out


def lc_bf(ctx, entry, rows, c0, w):
    col, ww, b = entry
    assert c0 + w <= ww
    return ctx.sb["CONSTL"][0:rows, col + c0:col + c0 + w]


def lc_f32(ctx, entry, rows, c0, w):
    col, ww, b = entry
    assert c0 + w <= ww
    return ctx.sb["CONSTL"][:, :].bitcast(F32)[0:rows, col + c0:col + c0 + w]


def cf32(ctx, name, rows=128):
    o, w = CONST_F32_OFF[name]
    assert o + w <= UNIV_F32, name
    return ctx.sb["cf32"][:rows, o:o + w]


def cbf_part(ctx, name, rows, c0, w):
    o, _ = CONST_BF_OFF[name]
    return ctx.sb["cbf"][0:rows, o + c0:o + c0 + w]


def cf32_part(ctx, name, rows, c0, w):
    o, _ = CONST_F32_OFF[name]
    return ctx.sb["cf32"][0:rows, o + c0:o + c0 + w]



def alloc_memory(ctx):
    sb = ctx.sbuf
    for c in range(DC):
        sb(f"XN{c}", [128, T], BF16)
    for c in range(DC):
        sb(f"G{c}", [128, T], BF16)
    for i in range(10):
        sb(f"BIG{i}", [128, T], BF16)
    for i in range(3):
        sb(f"SF{i}", [128, T], F32)
    for i in range(4):
        sb(f"SB{i}", [128, T], BF16)
    sb("WA", [128, 2048], BF16)
    sb("WB", [128, 2048], BF16)
    sb("WC", [128, 2048], BF16)
    sb("WD", [128, 2048], BF16)
    sb("RWA", [1, T], F32)
    sb("RWB", [1, T], F32)
    sb("RWC", [1, T], F32)
    sb("R16A", [16, T], BF16)
    sb("R16B", [16, T], BF16)
    sb("R64", [64, T], BF16)
    sb("R8B", [8, T], BF16)
    sb("SCL", [128, 256], F32)
    sb("TOK8", [128, 128], F32)
    sb("cbf", [128, UNIV_BF], BF16)
    sb("cf32", [128, UNIV_F32], F32)
    sb("CONSTL", [128, 2048], BF16)
    for i in range(8):
        t = ctx.stack.enter_context(ctx.nc.psum_tensor(f"ps{i}", [128, 512], F32))
        ctx.ps.append(t)
        ctx.psb.append(ctx.buf(f"ps{i}"))


def load_consts(ctx):
    nc = ctx.nc
    ctx.cbf_dram = nc.dram_tensor("c_bf", list(CONST_BF.shape), BF16, kind="ExternalInput").ap()
    ctx.cf32_dram = nc.dram_tensor("c_f32", list(CONST_F32.shape), F32,
                                   kind="ExternalInput").ap()
    ctx.op("sp", lambda e: e.dma_start(out=ctx.sb["cbf"][:, :], in_=ctx.cbf_dram[:, 0:UNIV_BF]),
           writes=[ctx.buf("cbf")], kind="dma")
    ctx.op("sp", lambda e: e.dma_start(out=ctx.sb["cf32"][:, :],
                                       in_=ctx.cf32_dram[:, 0:UNIV_F32]),
           writes=[ctx.buf("cf32")], kind="dma")



def mm(ctx, out_ap, pieces, out_buf, read_bufs, start=True):
    def fn(e):
        inst = None
        n = len(pieces)
        for i, (l, r) in enumerate(pieces):
            inst = e.matmul(out_ap() if callable(out_ap) else out_ap,
                            l() if callable(l) else l,
                            r() if callable(r) else r,
                            start=(start and i == 0), stop=(i == n - 1))
        return inst
    ctx.op("pe", fn, reads=list(read_bufs), writes=[out_buf])


def col_sums(ctx, srcs, out_psum_i, lhsT_col, accumulate_chunks=True):
    raise NotImplementedError


def bcast_row_to_plane(ctx, row_ap, row_buf, plane_name, plane_buf, width=T, dtype_plane=None):
    plane = ctx.sb[plane_name]
    for tt in range(width // 512):
        pi = tt % 2
        pb = ctx.psb[pi]
        ctx.op("pe",
               lambda e, tt=tt, pi=pi: e.matmul(
                   ctx.ps[pi][:, :], cf32(ctx, "ones_row", 1),
                   row_ap[:, tt * 512:(tt + 1) * 512], start=True, stop=True),
               reads=[row_buf, ctx.buf("cf32")], writes=[pb])
        ctx.op("dve",
               lambda e, tt=tt, pi=pi: e.tensor_copy(
                   plane[:, tt * 512:(tt + 1) * 512], ctx.ps[pi][:, :]),
               reads=[pb], writes=[plane_buf])


def scalar_to_col(ctx, src_ap, src_buf, col_slice, col_buf, scale=1.0):
    pb = ctx.psb[7]
    ctx.op("pe", lambda e: e.matmul(ctx.ps[7][:, 0:1], cf32(ctx, "ones_row", 1),
                                    src_ap, start=True, stop=True),
           reads=[src_buf, ctx.buf("cf32")], writes=[pb])
    ctx.op("act", lambda e: e.activation(col_slice, ctx.ps[7][:, 0:1], AF.Copy, scale=scale),
           reads=[pb], writes=[col_buf])


class ColPool:

    def __init__(self, ctx):
        self.ctx = ctx
        self.off = 0

    def alloc(self, name, n):
        off = self.off
        self.off += n
        assert self.off <= 256, "SCL overflow"
        return off, self.ctx.buf("SCL_" + name)


def load_feature_cols(ctx, cp, name, src, src_buf, nfeat, scale=None, reuse=None):
    dram_row_ap = src
    ncols = nfeat // 128
    if reuse is not None:
        off, b = reuse
    else:
        off, b = cp.alloc(name, ncols)
    scl = ctx.sb["SCL"]
    row = ctx.sb["RWC"]
    rb = ctx.buf("RWC")
    dma_b = src_buf
    nhalf = (nfeat + 2047) // 2048
    for h in range(nhalf):
        w = min(2048, nfeat - h * 2048)
        ctx.op("sp", lambda e, h=h, w=w: e.dma_start(
            out=row[:, 0:w], in_=dram_row_ap[h * 2048:h * 2048 + w]),
            reads=[dma_b], writes=[rb], kind="dma")
        for c in range(w // 128):
            pi = 6 + (c % 2)
            pb = ctx.psb[pi]
            ctx.op("pe", lambda e, c=c, pi=pi: e.transpose(
                ctx.ps[pi][:, 0:1], row[:, c * 128:(c + 1) * 128],
                cf32_part(ctx, "id128f", 1, 0, 1)),
                reads=[rb, ctx.buf("cf32")], writes=[pb])
            cc = off + h * 16 + c
            if scale is None:
                ctx.op("dve", lambda e, cc=cc, pi=pi: e.tensor_copy(
                    scl[:, cc:cc + 1], ctx.ps[pi][:, 0:1]),
                    reads=[pb], writes=[b])
            else:
                ctx.op("dve", lambda e, cc=cc, pi=pi: e.tensor_scalar(
                    scl[:, cc:cc + 1], ctx.ps[pi][:, 0:1], float(scale), None, ALU.mult),
                    reads=[pb], writes=[b])
    return off, b


def cast_load(ctx, dest_ap, src2d, dest_buf, src_buf, nchunks, eng="pool"):
    src3 = src2d.rearrange("(c p) m -> p c m", p=128)
    dst3 = dest_ap.rearrange("p (c m) -> p c m", c=nchunks)
    ctx.op(eng, lambda e: e.dma_start(out=dst3, in_=src3),
           reads=[src_buf], writes=[dest_buf], kind="dma")



def emit_load_x(ctx, xin_ap, xin_buf, xd, xdram_buf):
    stage = ctx.sb["SF0"]
    out_st = ctx.sb["SF1"]
    sbufs = [ctx.buf("SF0_ld_a"), ctx.buf("SF0_ld_b")]
    obufs = [ctx.buf("SF1_ld_a"), ctx.buf("SF1_ld_b")]
    for t16 in range(T // 128):
        half = t16 % 2
        st = stage[:, half * 1024:half * 1024 + 1024]
        sb_ = sbufs[half]
        ctx.op("sp", lambda e, t16=t16, st=st: e.dma_start(
            out=st, in_=xin_ap[t16 * 128:(t16 + 1) * 128, :]),
            reads=[xin_buf], writes=[sb_], kind="dma")
        ost = out_st[:, half * 1024:half * 1024 + 1024]
        ob = obufs[half]
        for c in range(DC):
            pi = c % 4
            pb = ctx.psb[pi]
            ctx.op("pe", lambda e, st=st, c=c, pi=pi: e.transpose(
                ctx.ps[pi][:, 0:128], st[:, c * 128:(c + 1) * 128],
                cf32(ctx, "id128f")),
                reads=[sb_, ctx.buf("cf32")], writes=[pb])
            ctx.op("dve", lambda e, ost=ost, c=c, pi=pi: e.tensor_copy(
                ost[:, c * 128:(c + 1) * 128], ctx.ps[pi][:, 0:128]),
                reads=[pb], writes=[ob])
        ctx.op("sp", lambda e, t16=t16, ost=ost: e.dma_start(
            out=xd.rearrange("(c p) t -> p c t", p=128)[:, :, t16 * 128:(t16 + 1) * 128],
            in_=ost.rearrange("p (c u) -> p c u", c=DC)),
            reads=[ob], writes=[xdram_buf], kind="dma")


def emit_store_x(ctx, xd, xdram_buf, xout_ap, xout_buf):
    stage = ctx.sb["SF0"]
    out_st = ctx.sb["SF1"]
    sbufs = [ctx.buf("SF0_stx_a"), ctx.buf("SF0_stx_b")]
    obufs = [ctx.buf("SF1_stx_a"), ctx.buf("SF1_stx_b")]
    for t16 in range(T // 128):
        half = t16 % 2
        st = stage[:, half * 1024:half * 1024 + 1024]
        sb_ = sbufs[half]
        ctx.op("sp", lambda e, t16=t16, st=st: e.dma_start(
            out=st.rearrange("p (c u) -> p c u", c=DC),
            in_=xd.rearrange("(c p) t -> p c t", p=128)[:, :, t16 * 128:(t16 + 1) * 128]),
            reads=[xdram_buf], writes=[sb_], kind="dma")
        ost = out_st[:, half * 1024:half * 1024 + 1024]
        ob = obufs[half]
        for c in range(DC):
            pi = 4 + c % 4
            pb = ctx.psb[pi]
            ctx.op("pe", lambda e, st=st, c=c, pi=pi: e.transpose(
                ctx.ps[pi][:, 0:128], st[:, c * 128:(c + 1) * 128],
                cf32(ctx, "id128f")),
                reads=[sb_, ctx.buf("cf32")], writes=[pb])
            ctx.op("dve", lambda e, ost=ost, c=c, pi=pi: e.tensor_copy(
                ost[:, c * 128:(c + 1) * 128], ctx.ps[pi][:, 0:128]),
                reads=[pb], writes=[ob])
        ctx.op("sp", lambda e, t16=t16, ost=ost: e.dma_start(
            out=xout_ap[t16 * 128:(t16 + 1) * 128, :], in_=ost),
            reads=[ob], writes=[xout_buf], kind="dma")



def emit_ln_xn(ctx, xd, xdram_buf):
    stage = ctx.sb["SF0"]
    sq = ctx.sb["SB0"]
    mu_row, var_row = ctx.sb["RWA"], ctx.sb["RWB"]
    mub_r, varb_r = ctx.buf("RWA"), ctx.buf("RWB")
    st_bufs = [ctx.buf("SF0_ln_a"), ctx.buf("SF0_ln_b")]
    sq_bufs = [ctx.buf("SB0_a"), ctx.buf("SB0_b")]
    for c in range(DC):
        for h in range(2):
            buf_i = h
            st = stage[:, buf_i * 1024:buf_i * 1024 + 1024]
            sb_ = st_bufs[buf_i]
            ctx.op("sp", lambda e, c=c, h=h, st=st: e.dma_start(
                out=st, in_=xd[c * 128:(c + 1) * 128, h * 1024:(h + 1) * 1024]),
                reads=[xdram_buf], writes=[sb_], kind="dma")
            qa = sq[:, buf_i * 1024:buf_i * 1024 + 1024]
            qb = sq_bufs[buf_i]
            ctx.op("act", lambda e, st=st, qa=qa: e.activation(qa, st, AF.Square),
                   reads=[sb_], writes=[qb])
            for u in range(2):
                tt = h * 2 + u
                ctx.op("pe", lambda e, st=st, u=u, c=c, tt=tt: e.matmul(
                    ctx.ps[tt][0:1, :], cf32(ctx, "ones_col"),
                    st[:, u * 512:(u + 1) * 512],
                    start=(c == 0), stop=(c == DC - 1)),
                    reads=[sb_, ctx.buf("cf32")], writes=[ctx.psb[tt]])
                ctx.op("pe", lambda e, qa=qa, u=u, c=c, tt=tt: e.matmul(
                    ctx.ps[4 + tt][0:1, :], cbf(ctx, "ones_col"),
                    qa[:, u * 512:(u + 1) * 512],
                    start=(c == 0), stop=(c == DC - 1)),
                    reads=[qb, ctx.buf("cbf")], writes=[ctx.psb[4 + tt]])
    musq = ctx.sb["RWC"]
    musq_b = ctx.buf("RWC")
    for tt in range(4):
        sl = slice(tt * 512, (tt + 1) * 512)
        ctx.op("act", lambda e, sl=sl, tt=tt: e.activation(
            mu_row[:, sl], ctx.ps[tt][0:1, :], AF.Copy, scale=1.0 / D),
            reads=[ctx.psb[tt]], writes=[mub_r])
        ctx.op("act", lambda e, sl=sl: e.activation(musq[:, sl], mu_row[:, sl], AF.Square),
               reads=[mub_r], writes=[musq_b])
        ctx.op("dve", lambda e, sl=sl, tt=tt: e.scalar_tensor_tensor(
            var_row[:, sl], ctx.ps[4 + tt][0:1, :], 1.0 / D, musq[:, sl],
            ALU.mult, ALU.subtract),
            reads=[ctx.psb[4 + tt], musq_b], writes=[varb_r])
    sd = ctx.sb["RWC"]
    sdb = ctx.buf("RWC")
    ctx.op("act", lambda e: e.activation(sd[:, :], var_row[:, :], AF.Sqrt, bias=cf32_part(ctx, "eps1e5", 1, 0, 1)),
           reads=[varb_r], writes=[sdb])
    ctx.op("dve", lambda e: e.reciprocal(var_row[:, :], sd[:, :]),
           reads=[sdb], writes=[varb_r])
    bcast_row_to_plane(ctx, mu_row, mub_r, "SF1", ctx.buf("SF1"))
    bcast_row_to_plane(ctx, var_row, varb_r, "SF2", ctx.buf("SF2"))
    MU, RS = ctx.sb["SF1"], ctx.sb["SF2"]
    mupb, rspb = ctx.buf("SF1"), ctx.buf("SF2")
    tmpf = sq[:, :].bitcast(F32)
    for c in range(DC):
        buf_i = c % 2
        st = stage[:, buf_i * 1024:buf_i * 1024 + 1024]
        stb2 = ctx.buf(f"SF0_ln2_{buf_i}")
        XN = ctx.sb[f"XN{c}"]
        xnb = ctx.buf(f"XN{c}")
        for h in range(2):
            ctx.op("sp", lambda e, c=c, h=h, st=st: e.dma_start(
                out=st, in_=xd[c * 128:(c + 1) * 128, h * 1024:(h + 1) * 1024]),
                reads=[xdram_buf], writes=[stb2], kind="dma")
            for u in range(2):
                usl = slice(h * 1024 + u * 512, h * 1024 + u * 512 + 512)
                tsl = slice(buf_i * 512, buf_i * 512 + 512)
                tb = ctx.buf(f"SB0_xntmp_{buf_i}")
                ctx.op("dve", lambda e, st=st, u=u, usl=usl, tsl=tsl: e.tensor_tensor(
                    tmpf[:, tsl], st[:, u * 512:(u + 1) * 512], MU[:, usl], ALU.subtract),
                    reads=[stb2, mupb], writes=[tb])
                ctx.op("dve", lambda e, XN=XN, usl=usl, tsl=tsl: e.tensor_tensor(
                    XN[:, usl], tmpf[:, tsl], RS[:, usl], ALU.mult),
                    reads=[tb, rspb], writes=[xnb])






def hw_name(fc):
    return f"BIG{fc}" if fc < 10 else "SB3"


def hw_tensor(ctx, fc):
    return ctx.sb[hw_name(fc)]

DBG_ROUTER = False


def emit_l0_ffn(ctx, W, cp, LC):
    nc = ctx.nc
    if DBG_ROUTER:
        ctx.dram["dbg_logits"] = nc.dram_tensor("dbg_logits", [T, 56], F32,
                                                kind="ExternalOutput")
    cbf_b, cf_b = ctx.buf("cbf"), ctx.buf("cf32")
    wd = ctx.sb["WD"]
    wdb = ctx.buf("WD_wg")
    cast_load(ctx, wd[:, 0:64], W["L0_ffn_Wg"][0][:, :], wdb, W["L0_ffn_Wg"][1], 8)

    tok = ctx.sb["TOK8"]
    tokb = ctx.buf("TOK8")
    w8 = ctx.sb["R8B"]
    w8b = ctx.buf("R8B")
    for pt in range(16):
        bank = pt % 2
        pb = ctx.psb[bank]
        def rmm(e, pt=pt, bank=bank):
            inst = None
            for c in range(DC):
                XN = ctx.sb[f"XN{c}"]
                inst = e.matmul(ctx.ps[bank][:, 0:8], XN[:, pt * 128:(pt + 1) * 128],
                                wd[:, c * 8:(c + 1) * 8], start=(c == 0), stop=(c == DC - 1))
            return inst
        ctx.op("pe", rmm, reads=[wdb] + [ctx.buf(f"XN{c}") for c in range(DC)], writes=[pb])
        ctx.op("dve", lambda e, bank=bank: e.tensor_copy(tok[:, 0:8], ctx.ps[bank][:, 0:8]),
               reads=[pb], writes=[tokb])
        ctx.op("dve", lambda e: e.reduce_max(tok[:, 16:17], tok[:, 0:8], AX.X),
               reads=[tokb], writes=[tokb])
        ctx.op("dve", lambda e: e.tensor_scalar(tok[:, 8:16], tok[:, 0:8], tok[:, 16:17],
                                                None, ALU.is_equal),
               reads=[tokb], writes=[tokb])
        ctx.op("dve", lambda e: e.scalar_tensor_tensor(tok[:, 24:32], tok[:, 8:16], -1e30,
                                                       tok[:, 0:8], ALU.mult, ALU.add),
               reads=[tokb], writes=[tokb])
        ctx.op("dve", lambda e: e.reduce_max(tok[:, 17:18], tok[:, 24:32], AX.X),
               reads=[tokb], writes=[tokb])
        ctx.op("dve", lambda e: e.tensor_scalar(tok[:, 32:40], tok[:, 24:32], tok[:, 17:18],
                                                None, ALU.is_equal),
               reads=[tokb], writes=[tokb])
        ctx.op("dve", lambda e: e.tensor_tensor(tok[:, 18:19], tok[:, 17:18], tok[:, 16:17],
                                                ALU.subtract),
               reads=[tokb], writes=[tokb])
        ctx.op("act", lambda e: e.activation(tok[:, 19:20], tok[:, 18:19], AF.Exp),
               reads=[tokb], writes=[tokb])
        ctx.op("dve", lambda e: e.tensor_scalar(tok[:, 20:21], tok[:, 19:20], 1.0, None, ALU.add),
               reads=[tokb], writes=[tokb])
        ctx.op("dve", lambda e: e.reciprocal(tok[:, 21:22], tok[:, 20:21]),
               reads=[tokb], writes=[tokb])
        ctx.op("dve", lambda e: e.tensor_tensor(tok[:, 22:23], tok[:, 19:20], tok[:, 21:22],
                                                ALU.mult),
               reads=[tokb], writes=[tokb])
        ctx.op("dve", lambda e: e.tensor_scalar(tok[:, 40:48], tok[:, 8:16], tok[:, 21:22],
                                                None, ALU.mult),
               reads=[tokb], writes=[tokb])
        ctx.op("dve", lambda e: e.scalar_tensor_tensor(tok[:, 48:56], tok[:, 32:40],
                                                       tok[:, 22:23], tok[:, 40:48],
                                                       ALU.mult, ALU.add),
               reads=[tokb], writes=[tokb])
        if DBG_ROUTER:
            dl = ctx.dram["dbg_logits"]
            dlb = ctx.buf("dbg_logits", tensor="dbg_logits")
            ctx.op("sp", lambda e, pt=pt: e.dma_start(
                out=dl.ap()[pt * 128:(pt + 1) * 128, :], in_=tok[:, 0:56]),
                reads=[tokb], writes=[dlb], kind="dma")
        tp = ctx.psb[2 + pt % 2]
        ctx.op("pe", lambda e, pt=pt: e.transpose(ctx.ps[2 + pt % 2][0:8, 0:128],
                                                  tok[:, 48:56], cf32(ctx, "id128f")),
               reads=[tokb, cf_b], writes=[tp])
        ctx.op("act", lambda e, pt=pt: e.activation(
            w8[0:8, pt * 128:(pt + 1) * 128], ctx.ps[2 + pt % 2][0:8, 0:128], AF.Copy),
            reads=[tp], writes=[w8b])

    H0 = 1408
    FC = H0 // 128
    wa, wb_ = ctx.sb["WA"], ctx.sb["WB"]
    sb1, sb2, sb3 = ctx.sb["SB1"], ctx.sb["SB2"], ctx.sb["SB3"]
    wbc_b = ctx.buf("SB1_wbc")
    for e_i in range(10):
        routed = e_i < 8
        if routed:
            off = 0
            def up1s(sl, e_i=e_i):
                return W["L0_ffn_W1"][0][e_i, :, sl]
            def up2s(sl, e_i=e_i):
                return W["L0_ffn_W2"][0][e_i, :, sl]
            def dns(rsl, csl, e_i=e_i):
                return W["L0_ffn_W3"][0][e_i, rsl, csl]
            up1b, up2b, dnb = W["L0_ffn_W1"][1], W["L0_ffn_W2"][1], W["L0_ffn_W3"][1]
        else:
            off = (e_i - 8) * H0
            def up1s(sl):
                return W["L0_ffn_Ws1"][0][:, sl]
            def up2s(sl):
                return W["L0_ffn_Ws2"][0][:, sl]
            def dns(rsl, csl):
                return W["L0_ffn_Ws3"][0][rsl, csl]
            up1b, up2b, dnb = W["L0_ffn_Ws1"][1], W["L0_ffn_Ws2"][1], W["L0_ffn_Ws3"][1]
        if routed:
            for tt2 in range(TT):
                pbb = ctx.psb[6]
                ctx.op("pe", lambda e, e_i=e_i, tt2=tt2: e.matmul(
                    ctx.ps[6][:, :], lc_bf(ctx, LC["sel8"], 8, e_i * 128, 128),
                    w8[0:8, tt2 * 512:(tt2 + 1) * 512], start=True, stop=True),
                    reads=[w8b, LC["sel8"][2]], writes=[pbb])
                ctx.op("act", lambda e, tt2=tt2: e.activation(
                    sb1[:, tt2 * 512:(tt2 + 1) * 512], ctx.ps[6][:, :], AF.Copy),
                    reads=[pbb], writes=[wbc_b])
        for fc in range(FC):
            wt = wa if fc % 2 == 0 else ctx.sb["WC"]
            wab = ctx.buf(("WA" if fc % 2 == 0 else "WC") + "_up")
            m0 = off + fc * 128
            cast_load(ctx, wt[:, 0:1024], up1s(slice(m0, m0 + 128)), wab, up1b, 8)
            cast_load(ctx, wt[:, 1024:2048], up2s(slice(m0, m0 + 128)), wab, up2b, 8)
            for tt2 in range(TT):
                tsl = slice(tt2 * 512, (tt2 + 1) * 512)
                pA, pB = ctx.psb[0 + (tt2 % 2)], ctx.psb[2 + (tt2 % 2)]
                iA, iB = 0 + (tt2 % 2), 2 + (tt2 % 2)
                def upmm(e, tsl=tsl, iA=iA, iB=iB, wt=wt):
                    inst = None
                    for kc in range(DC):
                        XN = ctx.sb[f"XN{kc}"]
                        e.matmul(ctx.ps[iA][:, :], wt[:, kc * 128:kc * 128 + 128],
                                 XN[:, tsl], start=(kc == 0), stop=(kc == DC - 1))
                        inst = e.matmul(ctx.ps[iB][:, :],
                                        wt[:, 1024 + kc * 128:1024 + kc * 128 + 128],
                                        XN[:, tsl], start=(kc == 0), stop=(kc == DC - 1))
                    return inst
                ctx.op("pe", upmm,
                       reads=[wab] + [ctx.buf(f"XN{c}") for c in range(DC)],
                       writes=[pA, pB])
                silu_t = sb2[:, tt2 * 512:(tt2 + 1) * 512]
                silu_b = ctx.buf(f"SB2_{tt2}")
                ctx.op("act", lambda e, silu_t=silu_t, iA=iA: e.activation(
                    silu_t, ctx.ps[iA][:, :], AF.Silu),
                    reads=[pA], writes=[silu_b])
                hw_t = hw_tensor(ctx, fc)[:, tt2 * 512:(tt2 + 1) * 512]
                hw_b = ctx.buf(hw_name(fc) + f"_{tt2}")
                if routed:
                    t2 = sb3[:, tt2 * 512:(tt2 + 1) * 512]
                    t2b = ctx.buf(f"SB3_{tt2}")
                    ctx.op("dve", lambda e, t2=t2, iB=iB, tsl=tsl: e.tensor_tensor(
                        t2, ctx.ps[iB][:, :], sb1[:, tsl], ALU.mult),
                        reads=[pB, wbc_b], writes=[t2b])
                    ctx.op("dve", lambda e, hw_t=hw_t, silu_t=silu_t, t2=t2: e.tensor_tensor(
                        hw_t, silu_t, t2, ALU.mult),
                        reads=[silu_b, t2b], writes=[hw_b])
                else:
                    ctx.op("dve", lambda e, hw_t=hw_t, silu_t=silu_t, iB=iB: e.tensor_tensor(
                        hw_t, silu_t, ctx.ps[iB][:, :], ALU.mult),
                        reads=[silu_b, pB], writes=[hw_b])
        for mo in range(DC):
            wt2 = wb_ if mo % 2 == 0 else ctx.sb["WD"]
            wbb = ctx.buf(("WB" if mo % 2 == 0 else "WD") + "_dn")
            cast_load(ctx, wt2[:, 0:FC * 128],
                      dns(slice(off, off + H0), slice(mo * 128, (mo + 1) * 128)),
                      wbb, dnb, FC)
            for tt2 in range(TT):
                tsl = slice(tt2 * 512, (tt2 + 1) * 512)
                iY = 4 + (tt2 % 2)
                pY = ctx.psb[iY]
                def dnmm(e, tsl=tsl, iY=iY, wt2=wt2):
                    inst = None
                    for fc in range(FC):
                        inst = e.matmul(ctx.ps[iY][:, :],
                                        wt2[:, fc * 128:fc * 128 + 128],
                                        hw_tensor(ctx, fc)[:, tsl],
                                        start=(fc == 0), stop=(fc == FC - 1))
                    return inst
                ctx.op("pe", dnmm,
                       reads=[wbb] + [ctx.buf(hw_name(fc) + f"_{tt2}") for fc in range(FC)],
                       writes=[pY])
                G = ctx.sb[f"G{mo}"]
                gb = ctx.buf(f"G{mo}_{tt2}")
                if e_i == 0:
                    ctx.op("dve", lambda e, G=G, tsl=tsl, iY=iY: e.tensor_copy(
                        G[:, tsl], ctx.ps[iY][:, :]),
                        reads=[pY], writes=[gb])
                else:
                    ctx.op("dve", lambda e, G=G, tsl=tsl, iY=iY: e.tensor_tensor(
                        G[:, tsl], G[:, tsl], ctx.ps[iY][:, :], ALU.add),
                        reads=[pY, gb], writes=[gb])
    for mo in range(DC):
        G = ctx.sb[f"G{mo}"]
        for tt2 in range(TT):
            gb = ctx.buf(f"G{mo}_{tt2}")
            tsl = slice(tt2 * 512, (tt2 + 1) * 512)
            ctx.op("act", lambda e, G=G, tsl=tsl: e.activation(G[:, tsl], G[:, tsl], AF.Silu),
                   reads=[gb], writes=[gb])



def emit_l0_attn(ctx, W, cp, LC, xd_in, xdin_buf, xd_out, xdout_buf):
    cbf_b, cf_b = ctx.buf("cbf"), ctx.buf("cf32")
    sb0, sb1, sb2, sb3 = ctx.sb["SB0"], ctx.sb["SB1"], ctx.sb["SB2"], ctx.sb["SB3"]
    SF0, SF1, SF2 = ctx.sb["SF0"], ctx.sb["SF1"], ctx.sb["SF2"]
    wa, wb_, wc, wdt = ctx.sb["WA"], ctx.sb["WB"], ctx.sb["WC"], ctx.sb["WD"]
    rmsb = ctx.buf("SF0_rms")
    scl = ctx.sb["SCL"]
    tok = ctx.sb["TOK8"]

    sqb = [ctx.buf("SB0_sq0"), ctx.buf("SB0_sq1")]
    for tt in range(TT):
        for c in range(DC):
            XN = ctx.sb[f"XN{c}"]
            xnb = ctx.buf(f"XN{c}")
            qa = sb0[:, (c % 2) * 512:(c % 2) * 512 + 512]
            qb = sqb[c % 2]
            ctx.op("act", lambda e, XN=XN, qa=qa, tt=tt: e.activation(
                qa, XN[:, tt * 512:(tt + 1) * 512], AF.Square),
                reads=[xnb], writes=[qb])
            ctx.op("pe", lambda e, qa=qa, c=c, tt=tt: e.matmul(
                ctx.ps[tt][0:1, :], cbf(ctx, "ones_col"), qa,
                start=(c == 0), stop=(c == DC - 1)),
                reads=[qb, cbf_b], writes=[ctx.psb[tt]])
    rrow = ctx.sb["RWA"]
    rrow_b = ctx.buf("RWA_rms")
    tmp_r = ctx.sb["RWB"]
    tmp_rb = ctx.buf("RWB_rms")
    for tt in range(TT):
        sl = slice(tt * 512, (tt + 1) * 512)
        ctx.op("act", lambda e, sl=sl, tt=tt: e.activation(
            tmp_r[:, sl], ctx.ps[tt][0:1, :], AF.Sqrt, scale=1.0 / D,
            bias=cf32_part(ctx, "eps1e6", 1, 0, 1)),
            reads=[ctx.psb[tt]], writes=[tmp_rb])
    ctx.op("dve", lambda e: e.reciprocal(rrow[:, :], tmp_r[:, :]),
           reads=[tmp_rb], writes=[rrow_b])
    bcast_row_to_plane(ctx, rrow, rrow_b, "SF0", rmsb)

    nw_off, nw_b = load_feature_cols(ctx, cp, "l0nw", W["L0_attn_norm_w"][0],
                                     W["L0_attn_norm_w"][1], D)
    tsc_b = ctx.buf("TOK8_scal")
    ctx.op("sp", lambda e: e.dma_start(out=tok[0:1, 120:121], in_=W["L0_attn_gate"][0][:, :]),
           reads=[W["L0_attn_gate"][1]], writes=[tsc_b], kind="dma")
    ctx.op("sp", lambda e: e.dma_start(out=tok[0:1, 121:122],
                                       in_=W["L0_attn_log_temperature"][0][:, :]),
           reads=[W["L0_attn_log_temperature"][1]], writes=[tsc_b], kind="dma")
    ctx.op("act", lambda e: e.activation(tok[0:1, 122:123], tok[0:1, 121:122], AF.Exp),
           reads=[tsc_b], writes=[tsc_b])
    co_gate, cb_gate = cp.alloc("l0gate", 1)
    co_sc, cb_sc = cp.alloc("l0scale", 1)
    co_nsc, cb_nsc = cp.alloc("l0nscale", 1)
    scalar_to_col(ctx, tok[0:1, 120:121], tsc_b, scl[:, co_gate:co_gate + 1], cb_gate)
    scalar_to_col(ctx, tok[0:1, 122:123], tsc_b, scl[:, co_sc:co_sc + 1], cb_sc)
    scalar_to_col(ctx, tok[0:1, 122:123], tsc_b, scl[:, co_nsc:co_nsc + 1], cb_nsc, scale=-1.0)

    def h_tile(dest_ap, dest_b, kc, tsl):
        XN = ctx.sb[f"XN{kc}"]
        ctx.op("dve", lambda e: e.scalar_tensor_tensor(
            dest_ap, XN[:, tsl], scl[:, nw_off + kc:nw_off + kc + 1], SF0[:, tsl],
            ALU.mult, ALU.mult),
            reads=[ctx.buf(f"XN{kc}"), nw_b, rmsb], writes=[dest_b])

    hbuf = ctx.buf("SB3_h")

    for g in range(2):
        wkb, wvb = ctx.buf("WA_wk"), ctx.buf("WB_wv")
        cast_load(ctx, wa[:, :], W["L0_attn_Wk"][0][:, g * 256:(g + 1) * 256],
                  wkb, W["L0_attn_Wk"][1], 8)
        cast_load(ctx, wb_[:, :], W["L0_attn_Wv"][0][:, g * 256:(g + 1) * 256],
                  wvb, W["L0_attn_Wv"][1], 8)
        for tt in range(TT):
            tsl = slice(tt * 512, (tt + 1) * 512)
            for kc in range(DC):
                ht = sb3[:, 0:512]
                h_tile(ht, hbuf, kc, tsl)
                def kmm(e, kc=kc, ht=ht):
                    inst = None
                    for m in range(2):
                        inst = e.matmul(ctx.ps[m][:, :],
                                        wa[:, kc * 256 + m * 128:kc * 256 + m * 128 + 128],
                                        ht, start=(kc == 0), stop=(kc == DC - 1))
                    return inst
                ctx.op("pe", kmm, reads=[wkb, hbuf], writes=[ctx.psb[0], ctx.psb[1]])
                def vmm(e, kc=kc, ht=ht):
                    inst = None
                    for ts in range(4):
                        inst = e.matmul(ctx.ps[2 + ts][:, 0:256], ht[:, ts * 128:(ts + 1) * 128],
                                        wb_[:, kc * 256:(kc + 1) * 256],
                                        start=(kc == 0), stop=(kc == DC - 1))
                    return inst
                ctx.op("pe", vmm, reads=[wvb, hbuf],
                       writes=[ctx.psb[2], ctx.psb[3], ctx.psb[4], ctx.psb[5]])
            for m in range(2):
                KB = ctx.sb[f"BIG{2 * g + m}"]
                kb_ = ctx.buf(f"BIG{2 * g + m}_k")
                ctx.op("act", lambda e, KB=KB, tsl=tsl, m=m: e.activation(
                    KB[:, tsl], ctx.ps[m][:, :], AF.Copy),
                    reads=[ctx.psb[m]], writes=[kb_])
            for ts in range(4):
                gts = tt * 4 + ts
                VB = ctx.sb[f"BIG{4 + 2 * g + gts // 8}"]
                vb_ = ctx.buf(f"BIG{4 + 2 * g + gts // 8}_v")
                ctx.op("act", lambda e, VB=VB, gts=gts, ts=ts: e.activation(
                    VB[:, (gts % 8) * 256:(gts % 8) * 256 + 256], ctx.ps[2 + ts][:, 0:256],
                    AF.Copy),
                    reads=[ctx.psb[2 + ts]], writes=[vb_])
        for tt in range(TT):
            for dc in range(2):
                KB = ctx.sb[f"BIG{2 * g + dc}"]
                kb_ = ctx.buf(f"BIG{2 * g + dc}_k")
                qa = sb0[:, (dc % 2) * 512:(dc % 2) * 512 + 512]
                qb = sqb[dc % 2]
                ctx.op("act", lambda e, KB=KB, qa=qa, tt=tt: e.activation(
                    qa, KB[:, tt * 512:(tt + 1) * 512], AF.Square),
                    reads=[kb_], writes=[qb])
                ctx.op("pe", lambda e, qa=qa, dc=dc, tt=tt: e.matmul(
                    ctx.ps[6][0:1, tt * 128:tt * 128 + 128] if False else ctx.ps[6 + tt % 2][0:1, :],
                    cbf(ctx, "ones_col"), qa,
                    start=(dc == 0), stop=(dc == 1)),
                    reads=[qb, cbf_b], writes=[ctx.psb[6 + tt % 2]])
            sl = slice(tt * 512, (tt + 1) * 512)
            ctx.op("act", lambda e, sl=sl, tt=tt: e.activation(
                tmp_r[:, sl], ctx.ps[6 + tt % 2][0:1, :], AF.Sqrt),
                reads=[ctx.psb[6 + tt % 2]], writes=[tmp_rb])
        ctx.op("dve", lambda e: e.tensor_scalar(tmp_r[:, :], tmp_r[:, :], 1e-12, None, ALU.max),
               reads=[tmp_rb], writes=[tmp_rb])
        ctx.op("dve", lambda e: e.reciprocal(rrow[:, :], tmp_r[:, :]),
               reads=[tmp_rb], writes=[rrow_b])
        bcast_row_to_plane(ctx, rrow, rrow_b, "SF1", ctx.buf("SF1_l2"))
        for dc in range(2):
            KB = ctx.sb[f"BIG{2 * g + dc}"]
            kb_ = ctx.buf(f"BIG{2 * g + dc}_k")
            ctx.op("dve", lambda e, KB=KB: e.tensor_tensor(KB[:, :], KB[:, :], SF1[:, :],
                                                           ALU.mult),
                   reads=[kb_, ctx.buf("SF1_l2")], writes=[kb_])

    aod = ctx.dram_scratch("l0_attn_out", [D, T], BF16)
    aod_b = ctx.buf("l0_attn_out", tensor="l0_attn_out")

    E_slots = [ctx.sb["BIG8"], ctx.sb["BIG9"], wa, sb0]
    E_names = ["BIG8", "BIG9", "WA", "SB0"]

    for hh in range(4):
        g = hh // 2
        wqt = wc if hh % 2 == 0 else wdt
        wqb = ctx.buf(("WC" if hh % 2 == 0 else "WD") + "_q")
        cast_load(ctx, wqt[:, :], W["L0_attn_Wq"][0][:, hh * 256:(hh + 1) * 256],
                  wqb, W["L0_attn_Wq"][1], 8)
        qtiles = [sb1, sb2]
        qb_ = [ctx.buf("SB1_q"), ctx.buf("SB2_q")]
        for tt in range(TT):
            tsl = slice(tt * 512, (tt + 1) * 512)
            for kc in range(DC):
                ht = sb3[:, 0:512]
                h_tile(ht, hbuf, kc, tsl)
                def qmm(e, kc=kc, ht=ht, wqt=wqt):
                    inst = None
                    for m in range(2):
                        inst = e.matmul(ctx.ps[m][:, :],
                                        wqt[:, kc * 256 + m * 128:kc * 256 + m * 128 + 128],
                                        ht, start=(kc == 0), stop=(kc == DC - 1))
                    return inst
                ctx.op("pe", qmm, reads=[wqb, hbuf], writes=[ctx.psb[0], ctx.psb[1]])
            for m in range(2):
                ctx.op("act", lambda e, m=m, tsl=tsl, qt_=qtiles[m]: e.activation(
                    qt_[:, tsl], ctx.ps[m][:, :], AF.Copy),
                    reads=[ctx.psb[m]], writes=[qb_[m]])
        for tt in range(TT):
            for dc in range(2):
                qa = sb0[:, (dc % 2) * 512:(dc % 2) * 512 + 512]
                qqb = sqb[dc % 2]
                ctx.op("act", lambda e, dc=dc, qa=qa, tt=tt, qt_=qtiles[dc]: e.activation(
                    qa, qt_[:, tt * 512:(tt + 1) * 512], AF.Square),
                    reads=[qb_[dc]], writes=[qqb])
                ctx.op("pe", lambda e, qa=qa, dc=dc, tt=tt: e.matmul(
                    ctx.ps[6 + tt % 2][0:1, :], cbf(ctx, "ones_col"), qa,
                    start=(dc == 0), stop=(dc == 1)),
                    reads=[qqb, cbf_b], writes=[ctx.psb[6 + tt % 2]])
            sl = slice(tt * 512, (tt + 1) * 512)
            ctx.op("act", lambda e, sl=sl, tt=tt: e.activation(
                tmp_r[:, sl], ctx.ps[6 + tt % 2][0:1, :], AF.Sqrt),
                reads=[ctx.psb[6 + tt % 2]], writes=[tmp_rb])
        ctx.op("dve", lambda e: e.tensor_scalar(tmp_r[:, :], tmp_r[:, :], 1e-12, None, ALU.max),
               reads=[tmp_rb], writes=[tmp_rb])
        ctx.op("dve", lambda e: e.reciprocal(rrow[:, :], tmp_r[:, :]),
               reads=[tmp_rb], writes=[rrow_b])
        bcast_row_to_plane(ctx, rrow, rrow_b, "SF1", ctx.buf("SF1_l2"))
        for dc in range(2):
            ctx.op("dve", lambda e, dc=dc, qt_=qtiles[dc]: e.tensor_tensor(
                qt_[:, :], qt_[:, :], SF1[:, :], ALU.mult),
                reads=[qb_[dc], ctx.buf("SF1_l2")], writes=[qb_[dc]])

        for qt in range(TT):
            qsl = slice(qt * 512, (qt + 1) * 512)
            nkt = 4 * qt + 4
            def mk_post(kt, et, eb, nkt=nkt, g=g):
                def post():
                    ctx.op("pe", lambda e, et=et, kt=kt, nkt=nkt: e.matmul(
                        ctx.ps[6][0:1, :], cbf(ctx, "ones_col"), et,
                        start=(kt == 0), stop=(kt == nkt - 1)),
                        reads=[eb, cbf_b], writes=[ctx.psb[6]])
                    def pvmm(e, kt=kt, et=et, g=g, nkt=nkt):
                        inst = None
                        VB = ctx.sb[f"BIG{4 + 2 * g + kt // 8}"]
                        for dc in range(2):
                            inst = e.matmul(ctx.ps[3 + dc][:, :],
                                            VB[:, (kt % 8) * 256 + dc * 128:
                                               (kt % 8) * 256 + dc * 128 + 128],
                                            et, start=(kt == 0), stop=(kt == nkt - 1))
                        return inst
                    ctx.op("pe", pvmm,
                           reads=[eb, ctx.buf(f"BIG{4 + 2 * g + kt // 8}_v")],
                           writes=[ctx.psb[3], ctx.psb[4]])
                return post
            pending = None
            for kt in range(nkt):
                isc = kt % 2
                psc = ctx.psb[isc]
                def smm(e, kt=kt, isc=isc, qsl=qsl, g=g):
                    inst = None
                    for dc in range(2):
                        KB = ctx.sb[f"BIG{2 * g + dc}"]
                        inst = e.matmul(ctx.ps[isc][:, :], KB[:, kt * 128:(kt + 1) * 128],
                                        qtiles[dc][:, qsl], start=(dc == 0), stop=(dc == 1))
                    return inst
                ctx.op("pe", smm,
                       reads=[ctx.buf(f"BIG{2 * g}_k"), ctx.buf(f"BIG{2 * g + 1}_k"),
                              qb_[0], qb_[1]],
                       writes=[psc])
                et = E_slots[kt % 4][:, (kt // 4) * 512:(kt // 4) * 512 + 512]
                eb = ctx.buf(E_names[kt % 4] + f"_E{kt // 4}")
                ctx.op("act", lambda e, et=et, isc=isc: e.activation(
                    et, ctx.ps[isc][:, :], AF.Exp,
                    scale=scl[:, co_sc:co_sc + 1], bias=scl[:, co_nsc:co_nsc + 1]),
                    reads=[psc, cb_sc, cb_nsc], writes=[eb])
                if kt >= 4 * qt:
                    j = kt - 4 * qt
                    ctx.op("dve", lambda e, et=et, j=j: e.tensor_tensor(
                        et, et, lc_bf(ctx, LC["cmask_ext"], 128, 384 - 128 * j, 512),
                        ALU.mult),
                        reads=[eb, LC["cmask_ext"][2]], writes=[eb])
                if pending is not None:
                    pending()
                pending = mk_post(kt, et, eb)
            pending()
            den_r = ctx.sb["RWC"]
            den_rb = ctx.buf("RWC_den")
            ctx.op("act", lambda e: e.activation(den_r[0:1, 0:512], ctx.ps[6][0:1, :], AF.Copy),
                   reads=[ctx.psb[6]], writes=[den_rb])
            ctx.op("dve", lambda e: e.reciprocal(den_r[0:1, 0:512], den_r[0:1, 0:512]),
                   reads=[den_rb], writes=[den_rb])
            pb7 = ctx.psb[7]
            ctx.op("pe", lambda e: e.matmul(ctx.ps[7][:, :], cf32(ctx, "ones_row", 1),
                                            den_r[0:1, 0:512], start=True, stop=True),
                   reads=[den_rb, cf_b], writes=[pb7])
            dplane_b = ctx.buf("SF2_den")
            ctx.op("dve", lambda e, qsl=qsl: e.tensor_copy(SF2[:, qsl], ctx.ps[7][:, :]),
                   reads=[pb7], writes=[dplane_b])
            for dc in range(2):
                stg = wb_[:, dc * 512:dc * 512 + 512]
                stgb = ctx.buf(f"WB_o{dc}")
                ctx.op("dve", lambda e, stg=stg, dc=dc, qsl=qsl: e.tensor_tensor(
                    stg, ctx.ps[3 + dc][:, :], SF2[:, qsl], ALU.mult),
                    reads=[ctx.psb[3 + dc], dplane_b], writes=[stgb])
                ctx.op("sp", lambda e, stg=stg, hh=hh, dc=dc, qsl=qsl: e.dma_start(
                    out=aod.ap()[hh * 256 + dc * 128:hh * 256 + dc * 128 + 128, qsl], in_=stg),
                    reads=[stgb], writes=[aod_b], kind="dma")

    for qt in range(TT):
        qsl = slice(qt * 512, (qt + 1) * 512)
        ao_slots = []
        for kc in range(DC):
            tens = E_slots[kc // 2]
            apo = tens[:, (kc % 2) * 512:(kc % 2) * 512 + 512]
            bo = ctx.buf(E_names[kc // 2] + f"_ao{kc % 2}")
            ao_slots.append((apo, bo))
            ctx.op("sp", lambda e, apo=apo, kc=kc, qsl=qsl: e.dma_start(
                out=apo, in_=aod.ap()[kc * 128:(kc + 1) * 128, qsl]),
                reads=[aod_b], writes=[bo], kind="dma")
        for mo in range(DC):
            wot = wdt if mo % 2 == 0 else wc
            wob = ctx.buf(("WD" if mo % 2 == 0 else "WC") + "_wo")
            cast_load(ctx, wot[:, 0:1024],
                      W["L0_attn_Wo"][0][:, mo * 128:(mo + 1) * 128],
                      wob, W["L0_attn_Wo"][1], 8)
            iY = mo % 2
            pY = ctx.psb[iY]
            def womm(e, iY=iY, ao_slots=ao_slots, wot=wot):
                inst = None
                for kc in range(DC):
                    inst = e.matmul(ctx.ps[iY][:, :],
                                    wot[:, kc * 128:kc * 128 + 128],
                                    ao_slots[kc][0], start=(kc == 0), stop=(kc == DC - 1))
                return inst
            ctx.op("pe", womm, reads=[wob] + [b for _, b in ao_slots], writes=[pY])
            XN = ctx.sb[f"XN{mo}"]
            st_ = sb3[:, 0:512]
            stb = ctx.buf("SB3_s")
            ctx.op("dve", lambda e, st_=st_, iY=iY, XN=XN, qsl=qsl: e.scalar_tensor_tensor(
                st_, ctx.ps[iY][:, :], scl[:, co_gate:co_gate + 1], XN[:, qsl],
                ALU.mult, ALU.add),
                reads=[pY, cb_gate, ctx.buf(f"XN{mo}")], writes=[stb])
            G = ctx.sb[f"G{mo}"]
            tt_ = sb3[:, 512:1024]
            ttb = ctx.buf("SB3_t")
            ctx.op("dve", lambda e, tt_=tt_, st_=st_, G=G, qsl=qsl: e.tensor_tensor(
                tt_, st_, G[:, qsl], ALU.mult),
                reads=[stb, ctx.buf(f"G{mo}_{qt}")], writes=[ttb])
            xst = SF1[:, (mo % 2) * 512:(mo % 2) * 512 + 512]
            xstb = ctx.buf(f"SF1_x{mo % 2}")
            ctx.op("sp", lambda e, xst=xst, mo=mo, qsl=qsl: e.dma_start(
                out=xst, in_=xd_in[mo * 128:(mo + 1) * 128, qsl]),
                reads=[xdin_buf], writes=[xstb], kind="dma")
            xon = SF2[:, (mo % 2) * 512:(mo % 2) * 512 + 512]
            xonb = ctx.buf(f"SF2_xo{mo % 2}")
            ctx.op("dve", lambda e, xon=xon, xst=xst, tt_=tt_: e.tensor_tensor(
                xon, xst, tt_, ALU.add),
                reads=[xstb, ttb], writes=[xonb])
            ctx.op("sp", lambda e, xon=xon, mo=mo, qsl=qsl: e.dma_start(
                out=xd_out[mo * 128:(mo + 1) * 128, qsl], in_=xon),
                reads=[xonb], writes=[xdout_buf], kind="dma")



def alloc_memory(ctx):
    sb = ctx.sbuf
    for c in range(DC):
        sb(f"XN{c}", [128, T], BF16)
    for c in range(DC):
        sb(f"G{c}", [128, T], BF16)
    for i in range(10):
        sb(f"BIG{i}", [128, T], BF16)
    for i in range(3):
        sb(f"SF{i}", [128, T], F32)
    for i in range(4):
        sb(f"SB{i}", [128, T], BF16)
    sb("WA", [128, 2048], BF16)
    sb("WB", [128, 2048], BF16)
    sb("WC", [128, 2048], BF16)
    sb("WD", [128, 2048], BF16)
    sb("RWA", [1, T], F32)
    sb("RWB", [1, T], F32)
    sb("RWC", [1, T], F32)
    sb("R16A", [16, T], BF16)
    sb("R16B", [16, T], BF16)
    sb("R64", [64, T], BF16)
    sb("R8B", [8, T], BF16)
    sb("SCL", [128, 256], F32)
    sb("TOK8", [128, 128], F32)
    sb("cbf", [128, UNIV_BF], BF16)
    sb("cf32", [128, UNIV_F32], F32)
    sb("CONSTL", [128, 2048], BF16)
    for i in range(8):
        t = ctx.stack.enter_context(ctx.nc.psum_tensor(f"ps{i}", [128, 512], F32))
        ctx.ps.append(t)
        ctx.psb.append(ctx.buf(f"ps{i}"))


def load_consts(ctx):
    nc = ctx.nc
    ctx.cbf_dram = nc.dram_tensor("c_bf", list(CONST_BF.shape), BF16, kind="ExternalInput").ap()
    ctx.cf32_dram = nc.dram_tensor("c_f32", list(CONST_F32.shape), F32,
                                   kind="ExternalInput").ap()
    ctx.op("sp", lambda e: e.dma_start(out=ctx.sb["cbf"][:, :], in_=ctx.cbf_dram[:, 0:UNIV_BF]),
           writes=[ctx.buf("cbf")], kind="dma")
    ctx.op("sp", lambda e: e.dma_start(out=ctx.sb["cf32"][:, :],
                                       in_=ctx.cf32_dram[:, 0:UNIV_F32]),
           writes=[ctx.buf("cf32")], kind="dma")



def mm(ctx, out_ap, pieces, out_buf, read_bufs, start=True):
    def fn(e):
        inst = None
        n = len(pieces)
        for i, (l, r) in enumerate(pieces):
            inst = e.matmul(out_ap() if callable(out_ap) else out_ap,
                            l() if callable(l) else l,
                            r() if callable(r) else r,
                            start=(start and i == 0), stop=(i == n - 1))
        return inst
    ctx.op("pe", fn, reads=list(read_bufs), writes=[out_buf])


def col_sums(ctx, srcs, out_psum_i, lhsT_col, accumulate_chunks=True):
    raise NotImplementedError


def bcast_row_to_plane(ctx, row_ap, row_buf, plane_name, plane_buf, width=T, dtype_plane=None):
    plane = ctx.sb[plane_name]
    for tt in range(width // 512):
        pi = tt % 2
        pb = ctx.psb[pi]
        ctx.op("pe",
               lambda e, tt=tt, pi=pi: e.matmul(
                   ctx.ps[pi][:, :], cf32(ctx, "ones_row", 1),
                   row_ap[:, tt * 512:(tt + 1) * 512], start=True, stop=True),
               reads=[row_buf, ctx.buf("cf32")], writes=[pb])
        ctx.op("dve",
               lambda e, tt=tt, pi=pi: e.tensor_copy(
                   plane[:, tt * 512:(tt + 1) * 512], ctx.ps[pi][:, :]),
               reads=[pb], writes=[plane_buf])


def scalar_to_col(ctx, src_ap, src_buf, col_slice, col_buf, scale=1.0):
    pb = ctx.psb[7]
    ctx.op("pe", lambda e: e.matmul(ctx.ps[7][:, 0:1], cf32(ctx, "ones_row", 1),
                                    src_ap, start=True, stop=True),
           reads=[src_buf, ctx.buf("cf32")], writes=[pb])
    ctx.op("act", lambda e: e.activation(col_slice, ctx.ps[7][:, 0:1], AF.Copy, scale=scale),
           reads=[pb], writes=[col_buf])


class ColPool:

    def __init__(self, ctx):
        self.ctx = ctx
        self.off = 0

    def alloc(self, name, n):
        off = self.off
        self.off += n
        assert self.off <= 256, "SCL overflow"
        return off, self.ctx.buf("SCL_" + name)


def load_feature_cols(ctx, cp, name, src, src_buf, nfeat, scale=None, reuse=None):
    dram_row_ap = src
    ncols = nfeat // 128
    if reuse is not None:
        off, b = reuse
    else:
        off, b = cp.alloc(name, ncols)
    scl = ctx.sb["SCL"]
    row = ctx.sb["RWC"]
    rb = ctx.buf("RWC")
    dma_b = src_buf
    nhalf = (nfeat + 2047) // 2048
    for h in range(nhalf):
        w = min(2048, nfeat - h * 2048)
        ctx.op("sp", lambda e, h=h, w=w: e.dma_start(
            out=row[:, 0:w], in_=dram_row_ap[h * 2048:h * 2048 + w]),
            reads=[dma_b], writes=[rb], kind="dma")
        for c in range(w // 128):
            pi = 6 + (c % 2)
            pb = ctx.psb[pi]
            ctx.op("pe", lambda e, c=c, pi=pi: e.transpose(
                ctx.ps[pi][:, 0:1], row[:, c * 128:(c + 1) * 128],
                cf32_part(ctx, "id128f", 1, 0, 1)),
                reads=[rb, ctx.buf("cf32")], writes=[pb])
            cc = off + h * 16 + c
            if scale is None:
                ctx.op("dve", lambda e, cc=cc, pi=pi: e.tensor_copy(
                    scl[:, cc:cc + 1], ctx.ps[pi][:, 0:1]),
                    reads=[pb], writes=[b])
            else:
                ctx.op("dve", lambda e, cc=cc, pi=pi: e.tensor_scalar(
                    scl[:, cc:cc + 1], ctx.ps[pi][:, 0:1], float(scale), None, ALU.mult),
                    reads=[pb], writes=[b])
    return off, b


def cast_load(ctx, dest_ap, src2d, dest_buf, src_buf, nchunks, eng="pool"):
    src3 = src2d.rearrange("(c p) m -> p c m", p=128)
    dst3 = dest_ap.rearrange("p (c m) -> p c m", c=nchunks)
    ctx.op(eng, lambda e: e.dma_start(out=dst3, in_=src3),
           reads=[src_buf], writes=[dest_buf], kind="dma")



def emit_load_x(ctx, xin_ap, xin_buf, xd, xdram_buf):
    stage = ctx.sb["SF0"]
    out_st = ctx.sb["SF1"]
    sbufs = [ctx.buf("SF0_ld_a"), ctx.buf("SF0_ld_b")]
    obufs = [ctx.buf("SF1_ld_a"), ctx.buf("SF1_ld_b")]
    for t16 in range(T // 128):
        half = t16 % 2
        st = stage[:, half * 1024:half * 1024 + 1024]
        sb_ = sbufs[half]
        ctx.op("sp", lambda e, t16=t16, st=st: e.dma_start(
            out=st, in_=xin_ap[t16 * 128:(t16 + 1) * 128, :]),
            reads=[xin_buf], writes=[sb_], kind="dma")
        ost = out_st[:, half * 1024:half * 1024 + 1024]
        ob = obufs[half]
        for c in range(DC):
            pi = c % 4
            pb = ctx.psb[pi]
            ctx.op("pe", lambda e, st=st, c=c, pi=pi: e.transpose(
                ctx.ps[pi][:, 0:128], st[:, c * 128:(c + 1) * 128],
                cf32(ctx, "id128f")),
                reads=[sb_, ctx.buf("cf32")], writes=[pb])
            ctx.op("dve", lambda e, ost=ost, c=c, pi=pi: e.tensor_copy(
                ost[:, c * 128:(c + 1) * 128], ctx.ps[pi][:, 0:128]),
                reads=[pb], writes=[ob])
        ctx.op("sp", lambda e, t16=t16, ost=ost: e.dma_start(
            out=xd.rearrange("(c p) t -> p c t", p=128)[:, :, t16 * 128:(t16 + 1) * 128],
            in_=ost.rearrange("p (c u) -> p c u", c=DC)),
            reads=[ob], writes=[xdram_buf], kind="dma")


def emit_store_x(ctx, xd, xdram_buf, xout_ap, xout_buf):
    stage = ctx.sb["SF0"]
    out_st = ctx.sb["SF1"]
    sbufs = [ctx.buf("SF0_stx_a"), ctx.buf("SF0_stx_b")]
    obufs = [ctx.buf("SF1_stx_a"), ctx.buf("SF1_stx_b")]
    for t16 in range(T // 128):
        half = t16 % 2
        st = stage[:, half * 1024:half * 1024 + 1024]
        sb_ = sbufs[half]
        ctx.op("sp", lambda e, t16=t16, st=st: e.dma_start(
            out=st.rearrange("p (c u) -> p c u", c=DC),
            in_=xd.rearrange("(c p) t -> p c t", p=128)[:, :, t16 * 128:(t16 + 1) * 128]),
            reads=[xdram_buf], writes=[sb_], kind="dma")
        ost = out_st[:, half * 1024:half * 1024 + 1024]
        ob = obufs[half]
        for c in range(DC):
            pi = 4 + c % 4
            pb = ctx.psb[pi]
            ctx.op("pe", lambda e, st=st, c=c, pi=pi: e.transpose(
                ctx.ps[pi][:, 0:128], st[:, c * 128:(c + 1) * 128],
                cf32(ctx, "id128f")),
                reads=[sb_, ctx.buf("cf32")], writes=[pb])
            ctx.op("dve", lambda e, ost=ost, c=c, pi=pi: e.tensor_copy(
                ost[:, c * 128:(c + 1) * 128], ctx.ps[pi][:, 0:128]),
                reads=[pb], writes=[ob])
        ctx.op("sp", lambda e, t16=t16, ost=ost: e.dma_start(
            out=xout_ap[t16 * 128:(t16 + 1) * 128, :], in_=ost),
            reads=[ob], writes=[xout_buf], kind="dma")



def emit_ln_xn(ctx, xd, xdram_buf):
    stage = ctx.sb["SF0"]
    sq = ctx.sb["SB0"]
    mu_row, var_row = ctx.sb["RWA"], ctx.sb["RWB"]
    mub_r, varb_r = ctx.buf("RWA"), ctx.buf("RWB")
    st_bufs = [ctx.buf("SF0_ln_a"), ctx.buf("SF0_ln_b")]
    sq_bufs = [ctx.buf("SB0_a"), ctx.buf("SB0_b")]
    for c in range(DC):
        for h in range(2):
            buf_i = h
            st = stage[:, buf_i * 1024:buf_i * 1024 + 1024]
            sb_ = st_bufs[buf_i]
            ctx.op("sp", lambda e, c=c, h=h, st=st: e.dma_start(
                out=st, in_=xd[c * 128:(c + 1) * 128, h * 1024:(h + 1) * 1024]),
                reads=[xdram_buf], writes=[sb_], kind="dma")
            qa = sq[:, buf_i * 1024:buf_i * 1024 + 1024]
            qb = sq_bufs[buf_i]
            ctx.op("act", lambda e, st=st, qa=qa: e.activation(qa, st, AF.Square),
                   reads=[sb_], writes=[qb])
            for u in range(2):
                tt = h * 2 + u
                ctx.op("pe", lambda e, st=st, u=u, c=c, tt=tt: e.matmul(
                    ctx.ps[tt][0:1, :], cf32(ctx, "ones_col"),
                    st[:, u * 512:(u + 1) * 512],
                    start=(c == 0), stop=(c == DC - 1)),
                    reads=[sb_, ctx.buf("cf32")], writes=[ctx.psb[tt]])
                ctx.op("pe", lambda e, qa=qa, u=u, c=c, tt=tt: e.matmul(
                    ctx.ps[4 + tt][0:1, :], cbf(ctx, "ones_col"),
                    qa[:, u * 512:(u + 1) * 512],
                    start=(c == 0), stop=(c == DC - 1)),
                    reads=[qb, ctx.buf("cbf")], writes=[ctx.psb[4 + tt]])
    musq = ctx.sb["RWC"]
    musq_b = ctx.buf("RWC")
    for tt in range(4):
        sl = slice(tt * 512, (tt + 1) * 512)
        ctx.op("act", lambda e, sl=sl, tt=tt: e.activation(
            mu_row[:, sl], ctx.ps[tt][0:1, :], AF.Copy, scale=1.0 / D),
            reads=[ctx.psb[tt]], writes=[mub_r])
        ctx.op("act", lambda e, sl=sl: e.activation(musq[:, sl], mu_row[:, sl], AF.Square),
               reads=[mub_r], writes=[musq_b])
        ctx.op("dve", lambda e, sl=sl, tt=tt: e.scalar_tensor_tensor(
            var_row[:, sl], ctx.ps[4 + tt][0:1, :], 1.0 / D, musq[:, sl],
            ALU.mult, ALU.subtract),
            reads=[ctx.psb[4 + tt], musq_b], writes=[varb_r])
    sd = ctx.sb["RWC"]
    sdb = ctx.buf("RWC")
    ctx.op("act", lambda e: e.activation(sd[:, :], var_row[:, :], AF.Sqrt, bias=cf32_part(ctx, "eps1e5", 1, 0, 1)),
           reads=[varb_r], writes=[sdb])
    ctx.op("dve", lambda e: e.reciprocal(var_row[:, :], sd[:, :]),
           reads=[sdb], writes=[varb_r])
    bcast_row_to_plane(ctx, mu_row, mub_r, "SF1", ctx.buf("SF1"))
    bcast_row_to_plane(ctx, var_row, varb_r, "SF2", ctx.buf("SF2"))
    MU, RS = ctx.sb["SF1"], ctx.sb["SF2"]
    mupb, rspb = ctx.buf("SF1"), ctx.buf("SF2")
    tmpf = sq[:, :].bitcast(F32)
    for c in range(DC):
        buf_i = c % 2
        st = stage[:, buf_i * 1024:buf_i * 1024 + 1024]
        stb2 = ctx.buf(f"SF0_ln2_{buf_i}")
        XN = ctx.sb[f"XN{c}"]
        xnb = ctx.buf(f"XN{c}")
        for h in range(2):
            ctx.op("sp", lambda e, c=c, h=h, st=st: e.dma_start(
                out=st, in_=xd[c * 128:(c + 1) * 128, h * 1024:(h + 1) * 1024]),
                reads=[xdram_buf], writes=[stb2], kind="dma")
            for u in range(2):
                usl = slice(h * 1024 + u * 512, h * 1024 + u * 512 + 512)
                tsl = slice(buf_i * 512, buf_i * 512 + 512)
                tb = ctx.buf(f"SB0_xntmp_{buf_i}")
                ctx.op("dve", lambda e, st=st, u=u, usl=usl, tsl=tsl: e.tensor_tensor(
                    tmpf[:, tsl], st[:, u * 512:(u + 1) * 512], MU[:, usl], ALU.subtract),
                    reads=[stb2, mupb], writes=[tb])
                ctx.op("dve", lambda e, XN=XN, usl=usl, tsl=tsl: e.tensor_tensor(
                    XN[:, usl], tmpf[:, tsl], RS[:, usl], ALU.mult),
                    reads=[tb, rspb], writes=[xnb])






LAYER_INPUTS = {
    0: {"L0_attn_norm_w": [D], "L0_attn_Wq": [D, D], "L0_attn_Wk": [D, 512],
        "L0_attn_Wv": [D, 512], "L0_attn_Wo": [D, D], "L0_attn_gate": [1, 1],
        "L0_attn_log_temperature": [1, 1], "L0_ffn_Wg": [D, 8],
        "L0_ffn_W1": [8, D, 1408], "L0_ffn_W2": [8, D, 1408], "L0_ffn_W3": [8, 1408, D],
        "L0_ffn_Ws1": [D, 2816], "L0_ffn_Ws2": [D, 2816], "L0_ffn_Ws3": [2816, D]},
    1: {},
    2: {},
    3: {},
}

LAYER_FNS = {}


def emit_layer0(ctx, W, xd_in, xdin_buf, xd_out, xdout_buf):
    cp = ColPool(ctx)
    LC = load_lconsts(ctx, names_bf=["sel8", "cmask_ext"])
    emit_ln_xn(ctx, xd_in, xdin_buf)
    emit_l0_ffn(ctx, W, cp, LC)
    emit_l0_attn(ctx, W, cp, LC, xd_in, xdin_buf, xd_out, xdout_buf)


LAYER_FNS[0] = emit_layer0


def build_program(layers, debug=False):
    nc = bass.Bass("TRN2", target_bir_lowering=False)
    ctx = Ctx(nc)
    xin = nc.dram_tensor("x_in", [T, D], F32, kind="ExternalInput")
    xout = nc.dram_tensor("x_out", [T, D], F32, kind="ExternalOutput")
    W = {}
    for L in layers:
        for name, shape in LAYER_INPUTS[L].items():
            h = nc.dram_tensor(name, shape, F32, kind="ExternalInput")
            W[name] = (h, ctx.buf("w_" + name, tensor="w_" + name))
    xda = ctx.dram_scratch("xd_a", [D, T], F32)
    xdb = ctx.dram_scratch("xd_b", [D, T], F32)
    xds = [(xda.ap(), ctx.buf("xda", tensor="xda")), (xdb.ap(), ctx.buf("xdb", tensor="xdb"))]
    ctx.alloc_sems()
    alloc_memory(ctx)
    load_consts(ctx)
    emit_load_x(ctx, xin[:, :], ctx.buf("xin", tensor="xin"), xds[0][0], xds[0][1])
    cur = 0
    for L in layers:
        LAYER_FNS[L](ctx, W, xds[cur][0], xds[cur][1], xds[1 - cur][0], xds[1 - cur][1])
        cur = 1 - cur
    emit_store_x(ctx, xds[cur][0], xds[cur][1], xout[:, :], ctx.buf("xout", tensor="xout"))
    if debug:
        dbg = {}
        for nm, tens, rows in [("dbg_xn", "XN", 128), ("dbg_g", "G", 128)]:
            od = nc.dram_tensor(nm, [DC * 128, T], BF16, kind="ExternalOutput")
            ob = ctx.buf(nm, tensor=nm)
            for c in range(DC):
                ctx.op("sp", lambda e, od=od, c=c, tens=tens: e.dma_start(
                    out=od.ap()[c * 128:(c + 1) * 128, :], in_=ctx.sb[f"{tens}{c}"][:, :]),
                    reads=[ctx.buf(f"{tens}{c}")], writes=[ob], kind="dma")
        w8d = nc.dram_tensor("dbg_w8", [8, T], BF16, kind="ExternalOutput")
        ctx.op("sp", lambda e: e.dma_start(out=w8d.ap()[:, :], in_=ctx.sb["R8B"][:, :]),
               reads=[ctx.buf("R8B")], writes=[ctx.buf("dbg_w8", tensor="dbg_w8")], kind="dma")
        aodd = nc.dram_tensor("dbg_aod", [D, T], BF16, kind="ExternalOutput")
        ctx.op("sp", lambda e: e.dma_start(out=aodd.ap()[:, :],
                                           in_=ctx.dram["l0_attn_out"].ap()[:, :]),
               reads=[ctx.buf("l0_attn_out", tensor="l0_attn_out")],
               writes=[ctx.buf("dbg_aod", tensor="dbg_aod")], kind="dma")
    fin = ctx.barrier_val()
    dsems = [k for k in SEMS if k.startswith("dsp") or k.startswith("dpool") or k == "dcc"]
    for e in ENGS:
        ctx.wait_counts(e, {k: v for k, v in fin.items() if k in dsems})
    with ctx.stack:
        with nc.Block() as block:
            for name, fns in ctx.q.items():
                def body(eng, fns=fns):
                    for f in fns:
                        f(eng)
                getattr(block, {"pe": "tensor", "act": "scalar", "dve": "vector",
                                "pool": "gpsimd", "sp": "sync"}[name])(body)
    return nc


def make_in_map(inputs, layers, core):
    m = {"c_bf": CONST_BF, "c_f32": CONST_F32}
    for L in layers:
        for name, shape in LAYER_INPUTS[L].items():
            if name == "l1_pos_x":
                pos = np.concatenate([
                    np.asarray(inputs["L1_attn_fwd_pos"], np.float32),
                    np.zeros(1, np.float32),
                    np.asarray(inputs["L1_attn_bwd_pos"], np.float32)])
                idx = np.arange(8) - core + 1024
                m[name] = np.ascontiguousarray(pos[idx])
            else:
                m[name] = np.ascontiguousarray(inputs[name], dtype=np.float32).reshape(shape)
    return m


def kernel(**inputs):
    x = np.ascontiguousarray(inputs["L0_attn_x"], dtype=np.float32)
    layers = [0, 1, 2, 3]
    nc = build_program(layers)
    in_maps = []
    for c in range(NB):
        m = make_in_map(inputs, layers, c)
        m["x_in"] = x[c]
        in_maps.append(m)
    res = run_bass_kernel_spmd(nc, in_maps, core_ids=list(range(NB)))
    return np.stack([r["x_out"] for r in res.results], axis=0)




def prep_transposed(ctx, name, src2d, src_buf, R, C, cast=True):
    nc = ctx.nc
    if cast:
        tmp = ctx.dram_scratch(name + "_c", [R, C], BF16).ap()
        tmpb = ctx.buf(name + "c", tensor=name + "c")
        ctx.op("pool", lambda e: e.dma_start(out=tmp, in_=src2d),
               reads=[src_buf], writes=[tmpb], kind="dma")
    else:
        tmp, tmpb = src2d, src_buf
    wt = ctx.dram_scratch(name + "_T", [C, R], BF16).ap()
    wtb = ctx.buf(name + "T", tensor=name + "T")
    nr = (R + 2047) // 2048
    for kc in range(C // 128):
        for rr in range(nr):
            r0 = rr * 2048
            rw_ = min(2048, R - r0)
            stg = ctx.sb["WA"] if (kc * nr + rr) % 2 == 0 else ctx.sb["WB"]
            sgb = ctx.buf(("WA" if (kc * nr + rr) % 2 == 0 else "WB") + "_prepT")
            ctx.op("sp", lambda e, kc=kc, r0=r0, rw_=rw_, stg=stg: e.dma_start_transpose(
                stg[:, 0:rw_], tmp[r0:r0 + rw_, kc * 128:(kc + 1) * 128]),
                reads=[tmpb], writes=[sgb], kind="dma")
            ctx.op("sp", lambda e, kc=kc, r0=r0, rw_=rw_, stg=stg: e.dma_start(
                out=wt[kc * 128:(kc + 1) * 128, r0:r0 + rw_], in_=stg[:, 0:rw_]),
                reads=[sgb], writes=[wtb], kind="dma")
    return wt, wtb


def load_lhsT(ctx, dest_ap, wt_ap, wt_buf, dest_buf, m0, mw, kchunks, k0=0, eng="sp"):
    src = wt_ap[k0:k0 + kchunks * 128, m0:m0 + mw].rearrange("(kc p) m -> p kc m", p=128)
    dst = dest_ap.rearrange("p (kc m) -> p kc m", kc=kchunks)
    ctx.op(eng, lambda e: e.dma_start(out=dst, in_=src),
           reads=[wt_buf], writes=[dest_buf], kind="dma")



def quantize_absmean(ctx, cp, tag, src, src_buf, R, C, qdram, qdram_buf):
    SF0, SF1 = ctx.sb["SF0"], ctx.sb["SF1"]
    sb0, sb1 = ctx.sb["SB0"], ctx.sb["SB1"]
    scl = ctx.sb["SCL"]
    acc_off, acc_b = cp.alloc(tag + "acc", 1)
    thr_off, thr_b = cp.alloc(tag + "thr", 1)
    rs_off, rs_b = cp.alloc(tag + "rs", 1)
    ctx.op("dve", lambda e: e.memset(scl[:, acc_off:acc_off + 1], 0.0),
           writes=[acc_b])
    nh = C // 2048 if C >= 2048 else 1
    cw = min(C, 2048)
    for r in range(R // 128):
        for hh in range(nh):
            st = SF0[:, 0:cw]
            stb = ctx.buf("SF0_q1")
            ctx.op("sp", lambda e, r=r, hh=hh, st=st: e.dma_start(
                out=st, in_=src[r * 128:(r + 1) * 128, hh * cw:(hh + 1) * cw]),
                reads=[src_buf], writes=[stb], kind="dma")
            ab = SF1[:, 0:cw]
            abb = ctx.buf("SF1_q1")
            ctx.op("act", lambda e, st=st, ab=ab: e.activation(ab, st, AF.Abs),
                   reads=[stb], writes=[abb])
            rs = scl[:, rs_off:rs_off + 1]
            rsb = rs_b
            ctx.op("dve", lambda e, ab=ab, rs=rs: e.reduce_sum(rs, ab, AX.X),
                   reads=[abb], writes=[rsb])
            ctx.op("dve", lambda e, rs=rs: e.tensor_tensor(
                scl[:, acc_off:acc_off + 1], scl[:, acc_off:acc_off + 1], rs, ALU.add),
                reads=[rsb, acc_b], writes=[acc_b])
    pb = ctx.psb[6]
    ctx.op("pe", lambda e: e.matmul(ctx.ps[6][0:1, 0:1], cf32(ctx, "ones_col"),
                                    scl[:, acc_off:acc_off + 1], start=True, stop=True),
           reads=[acc_b, ctx.buf("cf32")], writes=[pb])
    rw = ctx.sb["RWC"]
    rwb = ctx.buf("RWC_thr")
    ctx.op("act", lambda e: e.activation(rw[0:1, 0:1], ctx.ps[6][0:1, 0:1], AF.Copy,
                                         scale=1.0 / (R * C)),
           reads=[pb], writes=[rwb])
    ctx.op("dve", lambda e: e.tensor_scalar(rw[0:1, 0:1], rw[0:1, 0:1], 1e-6, 0.5,
                                            ALU.add, ALU.mult),
           reads=[rwb], writes=[rwb])
    scalar_to_col(ctx, rw[0:1, 0:1], rwb, scl[:, thr_off:thr_off + 1], thr_b)
    for r in range(R // 128):
        for hh in range(nh):
            st = SF0[:, 0:cw]
            stb = ctx.buf("SF0_q2")
            ctx.op("sp", lambda e, r=r, hh=hh, st=st: e.dma_start(
                out=st, in_=src[r * 128:(r + 1) * 128, hh * cw:(hh + 1) * cw]),
                reads=[src_buf], writes=[stb], kind="dma")
            ab = SF1[:, 0:cw]
            abb = ctx.buf("SF1_q2")
            ctx.op("act", lambda e, st=st, ab=ab: e.activation(ab, st, AF.Abs),
                   reads=[stb], writes=[abb])
            mk = sb0[:, 0:cw]
            mkb = ctx.buf("SB0_q2m")
            ctx.op("dve", lambda e, ab=ab, mk=mk: e.tensor_scalar(
                mk, ab, scl[:, thr_off:thr_off + 1], None, ALU.is_ge),
                reads=[abb, thr_b], writes=[mkb])
            sg = sb1[:, 0:cw]
            sgb = ctx.buf("SB1_q2s")
            ctx.op("act", lambda e, st=st, sg=sg: e.activation(sg, st, AF.Sign),
                   reads=[stb], writes=[sgb])
            ctx.op("dve", lambda e, mk=mk, sg=sg: e.tensor_tensor(mk, mk, sg, ALU.mult),
                   reads=[sgb, mkb], writes=[mkb])
            ctx.op("sp", lambda e, r=r, hh=hh, mk=mk: e.dma_start(
                out=qdram[r * 128:(r + 1) * 128, hh * cw:(hh + 1) * cw], in_=mk),
                reads=[mkb], writes=[qdram_buf], kind="dma")


def emit_l1_ffn(ctx, W, cp):
    cbf_b, cf_b = ctx.buf("cbf"), ctx.buf("cf32")
    SF0, SF1 = ctx.sb["SF0"], ctx.sb["SF1"]
    sb2 = ctx.sb["SB2"]
    wc = ctx.sb["WC"]
    wa, wb_ = ctx.sb["WA"], ctx.sb["WB"]
    scl = ctx.sb["SCL"]
    qfc = ctx.dram_scratch("l1_qfc", [4096, D], BF16).ap()
    qfc_b = ctx.buf("l1qfc", tensor="l1qfc")
    qpr = ctx.dram_scratch("l1_qpr", [D, 4096], BF16).ap()
    qpr_b = ctx.buf("l1qpr", tensor="l1qpr")
    hdram = ctx.dram_scratch("l1_h", [4096, T], BF16).ap()
    hdram_b = ctx.buf("l1h", tensor="l1h")
    quantize_absmean(ctx, cp, "fc", W["L1_ffn_W_fc"][0][:, :], W["L1_ffn_W_fc"][1],
                     4096, D, qfc, qfc_b)
    quantize_absmean(ctx, cp, "pr", W["L1_ffn_W_proj"][0][:, :], W["L1_ffn_W_proj"][1],
                     D, 4096, qpr, qpr_b)
    qfcT, qfcT_b = prep_transposed(ctx, "l1_qfc", qfc, qfc_b, 4096, D, cast=False)
    qprT, qprT_b = prep_transposed(ctx, "l1_qpr", qpr, qpr_b, D, 4096, cast=False)
    rsum, rsq = ctx.sb["RWA"], ctx.sb["RWB"]
    rsum_b, rsq_b = ctx.buf("RWA_l1"), ctx.buf("RWB_l1")
    ctx.op("dve", lambda e: e.memset(rsum[:, :], 0.0), writes=[rsum_b])
    ctx.op("dve", lambda e: e.memset(rsq[:, :], 0.0), writes=[rsq_b])
    for mo in range(32):
        wht = wc if mo % 2 == 0 else ctx.sb["WD"]
        wcb = ctx.buf(("WC" if mo % 2 == 0 else "WD") + "_l1")
        load_lhsT(ctx, wht[:, 0:1024], qfcT, qfcT_b, wcb, mo * 128, 128, DC)
        for tt in range(TT):
            tsl = slice(tt * 512, (tt + 1) * 512)
            iY = tt % 2
            pY = ctx.psb[iY]
            def hmm(e, iY=iY, tsl=tsl, wht=wht):
                inst = None
                for kc in range(DC):
                    XN = ctx.sb[f"XN{kc}"]
                    inst = e.matmul(ctx.ps[iY][:, :], wht[:, kc * 128:(kc + 1) * 128],
                                    XN[:, tsl], start=(kc == 0), stop=(kc == DC - 1))
                return inst
            ctx.op("pe", hmm, reads=[wcb] + [ctx.buf(f"XN{c}") for c in range(DC)],
                   writes=[pY])
            ht = sb2[:, (tt % 2) * 512:(tt % 2) * 512 + 512]
            htb = ctx.buf(f"SB2_h{tt % 2}")
            ctx.op("act", lambda e, ht=ht, iY=iY: e.activation(ht, ctx.ps[iY][:, :], AF.Gelu),
                   reads=[pY], writes=[htb])
            ctx.op("sp", lambda e, ht=ht, mo=mo, tsl=tsl: e.dma_start(
                out=hdram[mo * 128:(mo + 1) * 128, tsl], in_=ht),
                reads=[htb], writes=[hdram_b], kind="dma")
            ctx.op("pe", lambda e, ht=ht, iY=iY: e.matmul(
                ctx.ps[2 + iY][0:1, :], cbf(ctx, "ones_col"), ht, start=True, stop=True),
                reads=[htb, cbf_b], writes=[ctx.psb[2 + iY]])
            ctx.op("dve", lambda e, tsl=tsl, iY=iY: e.tensor_tensor(
                rsum[:, tsl], rsum[:, tsl], ctx.ps[2 + iY][0:1, :], ALU.add),
                reads=[ctx.psb[2 + iY], rsum_b], writes=[rsum_b])
            sqt = sb2[:, 1024 + (tt % 2) * 512:1024 + (tt % 2) * 512 + 512]
            sqtb = ctx.buf(f"SB2_sq{tt % 2}")
            ctx.op("act", lambda e, ht=ht, sqt=sqt: e.activation(sqt, ht, AF.Square),
                   reads=[htb], writes=[sqtb])
            ctx.op("pe", lambda e, sqt=sqt, iY=iY: e.matmul(
                ctx.ps[4 + iY][0:1, :], cbf(ctx, "ones_col"), sqt, start=True, stop=True),
                reads=[sqtb, cbf_b], writes=[ctx.psb[4 + iY]])
            ctx.op("dve", lambda e, tsl=tsl, iY=iY: e.tensor_tensor(
                rsq[:, tsl], rsq[:, tsl], ctx.ps[4 + iY][0:1, :], ALU.add),
                reads=[ctx.psb[4 + iY], rsq_b], writes=[rsq_b])
    H1 = 4096
    mu = ctx.sb["RWA"]
    ctx.op("dve", lambda e: e.tensor_scalar(mu[:, :], rsum[:, :], 1.0 / H1, None, ALU.mult),
           reads=[rsum_b], writes=[rsum_b])
    msq = ctx.sb["RWC"]
    msq_b = ctx.buf("RWC_l1m")
    ctx.op("act", lambda e: e.activation(msq[:, :], mu[:, :], AF.Square),
           reads=[rsum_b], writes=[msq_b])
    ctx.op("dve", lambda e: e.scalar_tensor_tensor(rsq[:, :], rsq[:, :], 1.0 / H1,
                                                   msq[:, :], ALU.mult, ALU.subtract),
           reads=[rsq_b, msq_b], writes=[rsq_b])
    ctx.op("act", lambda e: e.activation(rsq[:, :], rsq[:, :], AF.Sqrt,
                                         bias=cf32_part(ctx, "eps1e5", 1, 0, 1)),
           reads=[rsq_b], writes=[rsq_b])
    ctx.op("dve", lambda e: e.reciprocal(rsq[:, :], rsq[:, :]),
           reads=[rsq_b], writes=[rsq_b])
    bcast_row_to_plane(ctx, mu, rsum_b, "SF1", ctx.buf("SF1_l1mu"))
    bcast_row_to_plane(ctx, rsq, rsq_b, "SF2", ctx.buf("SF2_l1rs"))
    lw_off, lw_b = load_feature_cols(ctx, cp, "l1lw", W["L1_ffn_ln_w"][0],
                                     W["L1_ffn_ln_w"][1], 4096)
    lb_off, lb_b = load_feature_cols(ctx, cp, "l1lb", W["L1_ffn_ln_b"][0],
                                     W["L1_ffn_ln_b"][1], 4096)
    MUp, RSp = SF1, ctx.sb["SF2"]
    for tt in range(TT):
        tsl = slice(tt * 512, (tt + 1) * 512)
        for kc in range(32):
            slot = ctx.sb[f"BIG{kc // 4}"]
            slot_ap = slot[:, (kc % 4) * 512:(kc % 4) * 512 + 512]
            slot_b = ctx.buf(f"BIG{kc // 4}_hn{kc % 4}")
            st = ctx.sb["SB0"][:, (kc % 2) * 512:(kc % 2) * 512 + 512]
            stb = ctx.buf(f"SB0_l1h{kc % 2}")
            ctx.op("sp", lambda e, kc=kc, tsl=tsl, st=st: e.dma_start(
                out=st, in_=hdram[kc * 128:(kc + 1) * 128, tsl]),
                reads=[hdram_b], writes=[stb], kind="dma")
            t1 = SF0[:, 1024 + (kc % 2) * 512:1024 + (kc % 2) * 512 + 512]
            t1b = ctx.buf(f"SF0_l1t{kc % 2}")
            ctx.op("dve", lambda e, st=st, t1=t1, tsl=tsl: e.tensor_tensor(
                t1, st, MUp[:, tsl], ALU.subtract),
                reads=[stb, ctx.buf("SF1_l1mu")], writes=[t1b])
            ctx.op("dve", lambda e, t1=t1, tsl=tsl: e.tensor_tensor(
                t1, t1, RSp[:, tsl], ALU.mult),
                reads=[t1b, ctx.buf("SF2_l1rs")], writes=[t1b])
            ctx.op("dve", lambda e, t1=t1, slot_ap=slot_ap, kc=kc: e.tensor_scalar(
                slot_ap, t1, scl[:, lw_off + kc:lw_off + kc + 1],
                scl[:, lb_off + kc:lb_off + kc + 1], ALU.mult, ALU.add),
                reads=[t1b, lw_b, lb_b], writes=[slot_b])
        for mo2 in range(DC):
            iY = 6 + mo2 % 2
            pY = ctx.psb[iY]
            for kh in range(2):
                wpt = wa if kh == 0 else wb_
                wab = ctx.buf(("WA" if kh == 0 else "WB") + "_l1p")
                load_lhsT(ctx, wpt[:, 0:2048], qprT, qprT_b, wab, mo2 * 128, 128, 16,
                          k0=kh * 2048)
                def pmm(e, kh=kh, iY=iY, tt=tt, wpt=wpt):
                    inst = None
                    for kk in range(16):
                        kc = kh * 16 + kk
                        slot = ctx.sb[f"BIG{kc // 4}"]
                        inst = e.matmul(ctx.ps[iY][:, :], wpt[:, kk * 128:(kk + 1) * 128],
                                        slot[:, (kc % 4) * 512:(kc % 4) * 512 + 512],
                                        start=(kc == 0), stop=(kc == 31))
                    return inst
                ctx.op("pe", pmm,
                       reads=[wab] + [ctx.buf(f"BIG{(kh * 16 + kk) // 4}_hn{(kh * 16 + kk) % 4}")
                                      for kk in range(16)],
                       writes=[pY])
            G = ctx.sb[f"G{mo2}"]
            gb = ctx.buf(f"G{mo2}_{tt}")
            ctx.op("act", lambda e, G=G, tsl=tsl, iY=iY: e.activation(
                G[:, tsl], ctx.ps[iY][:, :], AF.Silu),
                reads=[pY], writes=[gb])



def emit_l1_attn(ctx, W, cp, xd_in, xdin_buf, xd_out, xdout_buf):
    cbf_b, cf_b = ctx.buf("cbf"), ctx.buf("cf32")
    SF0, SF1, SF2 = ctx.sb["SF0"], ctx.sb["SF1"], ctx.sb["SF2"]
    sb0, sb1, sb2, sb3 = ctx.sb["SB0"], ctx.sb["SB1"], ctx.sb["SB2"], ctx.sb["SB3"]
    wa, wb_, wc, wdt = ctx.sb["WA"], ctx.sb["WB"], ctx.sb["WC"], ctx.sb["WD"]
    scl = ctx.sb["SCL"]
    tok = ctx.sb["TOK8"]

    ipwT, ipwT_b = prep_transposed(ctx, "l1_ipw", W["L1_attn_in_proj_weight"][0][:, :],
                                   W["L1_attn_in_proj_weight"][1], 3072, D)
    owT, owT_b = prep_transposed(ctx, "l1_ow", W["L1_attn_out_w"][0][:, :],
                                 W["L1_attn_out_w"][1], D, D)
    bias_off, bias_b = load_feature_cols(ctx, cp, "l1bias", W["L1_attn_in_proj_bias"][0],
                                         W["L1_attn_in_proj_bias"][1], 3072)
    ob_off, ob_b = load_feature_cols(ctx, cp, "l1ob", W["L1_attn_out_b"][0],
                                     W["L1_attn_out_b"][1], D)

    tmd = {}
    for nm in ("q", "k", "v"):
        tmd[nm] = (ctx.dram_scratch(f"l1_{nm}tm", [T, D], BF16).ap(),
                   ctx.buf(f"l1{nm}tm", tensor=f"l1{nm}tm"))
    kall = ctx.dram_scratch("l1_kall", [NB * T, D], BF16)
    vall = ctx.dram_scratch("l1_vall", [NB * T, D], BF16)
    kall_b = ctx.buf("l1kall", tensor="l1kall")
    vall_b = ctx.buf("l1vall", tensor="l1vall")
    for pi_, nm in [(1, "k"), (2, "v"), (0, "q")]:
        dst, dstb = tmd[nm]
        for mo in range(DC):
            wqt = wc if mo % 2 == 0 else wdt
            wcb = ctx.buf(("WC" if mo % 2 == 0 else "WD") + "_l1a")
            load_lhsT(ctx, wqt[:, 0:1024], ipwT, ipwT_b, wcb,
                      pi_ * D + mo * 128, 128, DC)
            for tt in range(TT):
                tsl = slice(tt * 512, (tt + 1) * 512)
                iY = tt % 2
                pY = ctx.psb[iY]
                def qmm(e, iY=iY, tsl=tsl, wqt=wqt):
                    inst = None
                    for kc in range(DC):
                        XN = ctx.sb[f"XN{kc}"]
                        inst = e.matmul(ctx.ps[iY][:, :], wqt[:, kc * 128:(kc + 1) * 128],
                                        XN[:, tsl], start=(kc == 0), stop=(kc == DC - 1))
                    return inst
                ctx.op("pe", qmm, reads=[wcb] + [ctx.buf(f"XN{c}") for c in range(DC)],
                       writes=[pY])
                fm = sb1[:, (tt % 2) * 512:(tt % 2) * 512 + 512]
                fmb = ctx.buf(f"SB1_fm{tt % 2}")
                cc = bias_off + pi_ * 8 + mo
                ctx.op("dve", lambda e, fm=fm, iY=iY, cc=cc: e.tensor_scalar(
                    fm, ctx.ps[iY][:, :], scl[:, cc:cc + 1], None, ALU.add),
                    reads=[pY, bias_b], writes=[fmb])
                for ts in range(4):
                    ip2 = 2 + ts % 2
                    pT = ctx.psb[ip2]
                    ctx.op("pe", lambda e, fm=fm, ts=ts, ip2=ip2: e.transpose(
                        ctx.ps[ip2][:, 0:256].bitcast(BF16)[:, 0:128],
                        fm[:, ts * 128:(ts + 1) * 128], cbf(ctx, "id128")),
                        reads=[fmb, cbf_b], writes=[pT])
                    stg = sb2[:, (ts % 2) * 128:(ts % 2) * 128 + 128]
                    stgb = ctx.buf(f"SB2_tm{ts % 2}")
                    ctx.op("dve", lambda e, stg=stg, ip2=ip2: e.tensor_copy(
                        stg, ctx.ps[ip2][:, 0:256].bitcast(BF16)[:, 0:128]),
                        reads=[pT], writes=[stgb])
                    pt_g = tt * 4 + ts
                    ctx.op("sp", lambda e, stg=stg, pt_g=pt_g, mo=mo, dst=dst: e.dma_start(
                        out=dst[pt_g * 128:(pt_g + 1) * 128, mo * 128:(mo + 1) * 128], in_=stg),
                        reads=[stgb], writes=[dstb], kind="dma")
        if nm == "k":
            ctx.op("pool", lambda e: e.collective_compute(
                "AllGather", ALU.bypass, replica_groups=[list(range(NB))],
                ins=[ctx.dram["l1_ktm"].ap().opt()], outs=[kall.ap().opt()]),
                reads=[tmd["k"][1]], writes=[kall_b], kind="cc")
        elif nm == "v":
            ctx.op("pool", lambda e: e.collective_compute(
                "AllGather", ALU.bypass, replica_groups=[list(range(NB))],
                ins=[ctx.dram["l1_vtm"].ap().opt()], outs=[vall.ap().opt()]),
                reads=[tmd["v"][1]], writes=[vall_b], kind="cc")

    rwc = ctx.sb["RWC"]
    rwc_b = ctx.buf("RWC_l1c")
    ctx.op("sp", lambda e: e.dma_start(out=rwc[0:1, 0:D], in_=W["L1_attn_ctx_w"][0][0, :]),
           reads=[W["L1_attn_ctx_w"][1]], writes=[rwc_b], kind="dma")
    cxc_b = ctx.buf("WD_cx")
    for c in range(DC):
        pT = ctx.psb[4 + c % 2]
        ctx.op("pe", lambda e, c=c: e.transpose(
            ctx.ps[4 + c % 2][:, 0:1], rwc[0:1, c * 128:(c + 1) * 128],
            cf32_part(ctx, "id128f", 1, 0, 1)),
            reads=[rwc_b, cf_b], writes=[pT])
        ctx.op("act", lambda e, c=c: e.activation(wdt[:, c:c + 1], ctx.ps[4 + c % 2][:, 0:1],
                                                  AF.Copy),
               reads=[pT], writes=[cxc_b])
    rwa = ctx.sb["RWA"]
    rwa_b = ctx.buf("RWA_l1ctx")
    for tt in range(TT):
        tsl = slice(tt * 512, (tt + 1) * 512)
        iY = 6
        def cmm(e, tsl=tsl):
            inst = None
            for kc in range(DC):
                XN = ctx.sb[f"XN{kc}"]
                inst = e.matmul(ctx.ps[6][0:1, :], wdt[:, kc:kc + 1], XN[:, tsl],
                                start=(kc == 0), stop=(kc == DC - 1))
            return inst
        ctx.op("pe", cmm, reads=[cxc_b] + [ctx.buf(f"XN{c}") for c in range(DC)],
               writes=[ctx.psb[6]])
        ctx.op("act", lambda e, tsl=tsl: e.activation(rwa[0:1, tsl], ctx.ps[6][0:1, :], AF.Copy),
               reads=[ctx.psb[6]], writes=[rwa_b])

    rwb = ctx.sb["RWB"]
    rwb_b = ctx.buf("RWB_l1g")
    ctx.op("sp", lambda e: e.dma_start(out=rwb[0:1, 0:16], in_=W["L1_attn_head_x"][0][:]),
           reads=[W["L1_attn_head_x"][1]], writes=[rwb_b], kind="dma")
    ctx.op("sp", lambda e: e.dma_start(out=rwb[0:1, 16:24], in_=W["l1_pos_x"][0][:]),
           reads=[W["l1_pos_x"][1]], writes=[rwb_b], kind="dma")
    ctx.op("dve", lambda e: e.tensor_tensor(
        bass.AP(rwb, 32, [[2048, 1], [8, 16], [1, 8]]),
        bass.AP(rwb, 0, [[2048, 1], [1, 16], [0, 8]]),
        bass.AP(rwb, 16, [[2048, 1], [0, 16], [1, 8]]), ALU.add),
        reads=[rwb_b], writes=[rwb_b])
    pT = ctx.psb[7]
    ctx.op("pe", lambda e: e.matmul(ctx.ps[7][:, 0:128], cf32(ctx, "ones_row", 1),
                                    rwb[0:1, 32:160], start=True, stop=True),
           reads=[rwb_b, cf_b], writes=[pT])
    tokb = ctx.buf("TOK8_l1g")
    ctx.op("dve", lambda e: e.tensor_copy(tok[:, :], ctx.ps[7][:, 0:128]),
           reads=[pT], writes=[tokb])

    aod = ctx.dram_scratch("l1_attn_out", [D, T], BF16)
    aod_b = ctx.buf("l1_attn_out", tensor="l1_attn_out")

    for pt in range(16):
        psl = slice(pt * 128, (pt + 1) * 128)
        qtb = ctx.buf("SB1_l1q")
        ctx.op("sp", lambda e, psl=psl: e.dma_start(out=sb1[:, 0:1024],
                                                    in_=tmd["q"][0][psl, :]),
               reads=[tmd["q"][1]], writes=[qtb], kind="dma")
        co_ctx, cb_ctx = (cp.alloc("l1ctxcol", 1) if pt == 0 else (ctx._l1ctx))
        ctx._l1ctx = (co_ctx, cb_ctx)
        pT2 = ctx.psb[6]
        ctx.op("pe", lambda e, pt=pt: e.transpose(
            ctx.ps[6][:, 0:1], rwa[0:1, pt * 128:(pt + 1) * 128],
            cf32_part(ctx, "id128f", 1, 0, 1)),
            reads=[rwa_b, cf_b], writes=[pT2])
        ctx.op("dve", lambda e: e.tensor_copy(scl[:, co_ctx:co_ctx + 1], ctx.ps[6][:, 0:1]),
               reads=[pT2], writes=[cb_ctx])
        lwb = ctx.buf("SF1_l1lw")
        ctx.op("act", lambda e: e.activation(SF1[:, 0:128], tok[:, :], AF.Sigmoid,
                                             scale=1.0, bias=scl[:, co_ctx:co_ctx + 1]),
               reads=[tokb, cb_ctx], writes=[lwb])
        ctx.op("dve", lambda e: e.reduce_max(
            SF1[:, 128:144], SF1[:, 0:128].rearrange("p (h s) -> p h s", s=8), AX.X),
            reads=[lwb], writes=[lwb])
        ctx.op("dve", lambda e: e.reciprocal(SF1[:, 144:160], SF1[:, 128:144]),
               reads=[lwb], writes=[lwb])
        ctx.op("dve", lambda e: e.tensor_tensor(
            SF1[:, 160:288].rearrange("p (h s) -> p h s", s=8),
            SF1[:, 0:128].rearrange("p (h s) -> p h s", s=8),
            bass.AP(SF1, 144, [[2048, 128], [1, 16], [0, 8]]), ALU.mult),
            reads=[lwb], writes=[lwb])
        ctx.op("dve", lambda e: e.tensor_scalar(SF1[:, 160:288], SF1[:, 160:288],
                                                1e-10, None, ALU.max),
               reads=[lwb], writes=[lwb])
        ctx.op("act", lambda e: e.activation(SF1[:, 288:416], SF1[:, 160:288], AF.Ln),
               reads=[lwb], writes=[lwb])
        scb = ctx.buf("SF2_l1s")
        for s_ in range(NB):
            ksl = sb2[:, (s_ % 2) * 1024:(s_ % 2) * 1024 + 1024]
            kslb = ctx.buf(f"SB2_l1k{s_ % 2}")
            ctx.op("sp", lambda e, s_=s_, pt=pt, ksl=ksl: e.dma_start(
                out=ksl, in_=kall.ap()[s_ * T + pt * 128:s_ * T + (pt + 1) * 128, :]),
                reads=[kall_b], writes=[kslb], kind="dma")
            t1 = sb3[:, 0:1024]
            t1b = ctx.buf("SB3_l1t1")
            ctx.op("dve", lambda e, ksl=ksl, t1=t1: e.tensor_tensor(
                t1, sb1[:, 0:1024], ksl, ALU.mult),
                reads=[qtb, kslb], writes=[t1b])
            ctx.op("dve", lambda e, s_=s_, t1=t1: e.reduce_sum(
                bass.AP(SF2, s_, [[2048, 128], [8, 16]]),
                t1.rearrange("p (h d) -> p h d", d=64), AX.X),
                reads=[t1b], writes=[scb])
        ctx.op("dve", lambda e: e.tensor_tensor(SF2[:, 0:128], SF2[:, 0:128],
                                                SF1[:, 288:416], ALU.add),
               reads=[scb, lwb], writes=[scb])
        ctx.op("dve", lambda e: e.reduce_max(
            SF2[:, 128:144], SF2[:, 0:128].rearrange("p (h s) -> p h s", s=8), AX.X),
            reads=[scb], writes=[scb])
        ctx.op("dve", lambda e: e.tensor_tensor(
            SF2[:, 160:288].rearrange("p (h s) -> p h s", s=8),
            SF2[:, 0:128].rearrange("p (h s) -> p h s", s=8),
            bass.AP(SF2, 128, [[2048, 128], [1, 16], [0, 8]]), ALU.subtract),
            reads=[scb], writes=[scb])
        ctx.op("act", lambda e: e.activation(SF2[:, 160:288], SF2[:, 160:288], AF.Exp),
               reads=[scb], writes=[scb])
        ctx.op("dve", lambda e: e.reduce_sum(
            SF2[:, 128:144], SF2[:, 160:288].rearrange("p (h s) -> p h s", s=8), AX.X),
            reads=[scb], writes=[scb])
        ctx.op("dve", lambda e: e.reciprocal(SF2[:, 144:160], SF2[:, 128:144]),
               reads=[scb], writes=[scb])
        ctx.op("dve", lambda e: e.tensor_tensor(
            SF2[:, 160:288].rearrange("p (h s) -> p h s", s=8),
            SF2[:, 160:288].rearrange("p (h s) -> p h s", s=8),
            bass.AP(SF2, 144, [[2048, 128], [1, 16], [0, 8]]), ALU.mult),
            reads=[scb], writes=[scb])
        prb = ctx.buf("SB3_l1pr")
        ctx.op("dve", lambda e: e.tensor_copy(sb3[:, 1024:1152], SF2[:, 160:288]),
               reads=[scb], writes=[prb])
        accb = ctx.buf("SF2_l1acc")
        ctx.op("dve", lambda e: e.memset(SF2[:, 1024:2048], 0.0), writes=[accb])
        for s_ in range(NB):
            vsl = sb2[:, (s_ % 2) * 1024:(s_ % 2) * 1024 + 1024]
            vslb = ctx.buf(f"SB2_l1v{s_ % 2}")
            ctx.op("sp", lambda e, s_=s_, pt=pt, vsl=vsl: e.dma_start(
                out=vsl, in_=vall.ap()[s_ * T + pt * 128:s_ * T + (pt + 1) * 128, :]),
                reads=[vall_b], writes=[vslb], kind="dma")
            t2 = sb3[:, 0:1024]
            t2b = ctx.buf("SB3_l1t2")
            ctx.op("dve", lambda e, vsl=vsl, s_=s_, t2=t2: e.tensor_tensor(
                t2.rearrange("p (h d) -> p h d", d=64),
                vsl[:, :].rearrange("p (h d) -> p h d", d=64),
                bass.AP(sb3, 1024 + s_, [[2048, 128], [8, 16], [0, 64]]), ALU.mult),
                reads=[vslb, prb], writes=[t2b])
            ctx.op("dve", lambda e, t2=t2: e.tensor_tensor(
                SF2[:, 1024:2048], SF2[:, 1024:2048], t2, ALU.add),
                reads=[t2b, accb], writes=[accb])
        ob16 = ctx.buf("SB3_l1o")
        ctx.op("dve", lambda e: e.tensor_copy(sb3[:, 0:1024], SF2[:, 1024:2048]),
               reads=[accb], writes=[ob16])
        for c in range(DC):
            ip2 = 2 + c % 2
            ctx.op("pe", lambda e, c=c, ip2=ip2: e.transpose(
                ctx.ps[ip2][:, 0:256].bitcast(BF16)[:, 0:128],
                sb3[:, c * 128:(c + 1) * 128], cbf(ctx, "id128")),
                reads=[ob16, cbf_b], writes=[ctx.psb[ip2]])
            stg = sb2[:, (c % 2) * 128:(c % 2) * 128 + 128]
            stgb = ctx.buf(f"SB2_l1o{c % 2}")
            ctx.op("dve", lambda e, stg=stg, ip2=ip2: e.tensor_copy(
                stg, ctx.ps[ip2][:, 0:256].bitcast(BF16)[:, 0:128]),
                reads=[ctx.psb[ip2]], writes=[stgb])
            ctx.op("sp", lambda e, stg=stg, c=c, pt=pt: e.dma_start(
                out=aod.ap()[c * 128:(c + 1) * 128, pt * 128:(pt + 1) * 128], in_=stg),
                reads=[stgb], writes=[aod_b], kind="dma")

    for qt in range(TT):
        qsl = slice(qt * 512, (qt + 1) * 512)
        ao_slots = []
        for kc in range(DC):
            tens = ctx.sb[f"BIG{kc // 2}"]
            apo = tens[:, (kc % 2) * 512:(kc % 2) * 512 + 512]
            bo = ctx.buf(f"BIG{kc // 2}_l1ao{kc % 2}")
            ao_slots.append((apo, bo))
            ctx.op("sp", lambda e, apo=apo, kc=kc, qsl=qsl: e.dma_start(
                out=apo, in_=aod.ap()[kc * 128:(kc + 1) * 128, qsl]),
                reads=[aod_b], writes=[bo], kind="dma")
        for mo in range(DC):
            wot = wdt if mo % 2 == 0 else wc
            woff = 1024 if mo % 2 == 0 else 0
            wob = ctx.buf(("WD" if mo % 2 == 0 else "WC") + "_l1wo")
            load_lhsT(ctx, wot[:, woff:woff + 1024], owT, owT_b, wob, mo * 128, 128, DC)
            iY = mo % 2
            pY = ctx.psb[iY]
            def womm(e, iY=iY, ao_slots=ao_slots, wot=wot, woff=woff):
                inst = None
                for kc in range(DC):
                    inst = e.matmul(ctx.ps[iY][:, :],
                                    wot[:, woff + kc * 128:woff + (kc + 1) * 128],
                                    ao_slots[kc][0], start=(kc == 0), stop=(kc == DC - 1))
                return inst
            ctx.op("pe", womm, reads=[wob] + [b for _, b in ao_slots], writes=[pY])
            st_ = sb3[:, 0:512]
            stb = ctx.buf("SB3_l1s")
            cc = ob_off + mo
            ctx.op("dve", lambda e, st_=st_, iY=iY, cc=cc: e.tensor_scalar(
                st_, ctx.ps[iY][:, :], scl[:, cc:cc + 1], None, ALU.add),
                reads=[pY, ob_b], writes=[stb])
            G = ctx.sb[f"G{mo}"]
            tt_ = sb3[:, 512:1024]
            ttb = ctx.buf("SB3_l1tt")
            ctx.op("dve", lambda e, tt_=tt_, st_=st_, G=G, qsl=qsl: e.tensor_tensor(
                tt_, st_, G[:, qsl], ALU.mult),
                reads=[stb, ctx.buf(f"G{mo}_{qt}")], writes=[ttb])
            xst = SF1[:, (mo % 2) * 512:(mo % 2) * 512 + 512]
            xstb = ctx.buf(f"SF1_x{mo % 2}")
            ctx.op("sp", lambda e, xst=xst, mo=mo, qsl=qsl: e.dma_start(
                out=xst, in_=xd_in[mo * 128:(mo + 1) * 128, qsl]),
                reads=[xdin_buf], writes=[xstb], kind="dma")
            xon = SF2[:, (mo % 2) * 512:(mo % 2) * 512 + 512]
            xonb = ctx.buf(f"SF2_xo{mo % 2}")
            ctx.op("dve", lambda e, xon=xon, xst=xst, tt_=tt_: e.tensor_tensor(
                xon, xst, tt_, ALU.add),
                reads=[xstb, ttb], writes=[xonb])
            ctx.op("sp", lambda e, xon=xon, mo=mo, qsl=qsl: e.dma_start(
                out=xd_out[mo * 128:(mo + 1) * 128, qsl], in_=xon),
                reads=[xonb], writes=[xdout_buf], kind="dma")


def emit_layer1(ctx, W, xd_in, xdin_buf, xd_out, xdout_buf):
    cp = ColPool(ctx)
    emit_ln_xn(ctx, xd_in, xdin_buf)
    emit_l1_ffn(ctx, W, cp)
    emit_l1_attn(ctx, W, cp, xd_in, xdin_buf, xd_out, xdout_buf)


LAYER_FNS[1] = emit_layer1
LAYER_INPUTS[1] = {
    "L1_attn_in_proj_weight": [3072, D], "L1_attn_in_proj_bias": [3072],
    "L1_attn_fwd_pos": [1024], "L1_attn_bwd_pos": [1024], "L1_attn_head_x": [16],
    "L1_attn_ctx_w": [1, D], "L1_attn_out_w": [D, D], "L1_attn_out_b": [D],
    "l1_pos_x": [8],
    "L1_ffn_W_fc": [4096, D], "L1_ffn_ln_w": [4096], "L1_ffn_ln_b": [4096],
    "L1_ffn_W_proj": [D, 4096],
}



def emit_router(ctx, get_lhsT, rhs_ap, rhs_buf, w8, w8b, tokname="TOK8"):
    tok = ctx.sb[tokname]
    tokb = ctx.buf(tokname + "_r")
    cf_b = ctx.buf("cf32")
    for pt in range(16):
        bank = pt % 2
        pb = ctx.psb[bank]
        def rmm(e, pt=pt, bank=bank):
            inst = None
            for c in range(DC):
                lap = get_lhsT(pt, c)
                inst = e.matmul(ctx.ps[bank][:, 0:8], lap, rhs_ap[:, c * 8:(c + 1) * 8],
                                start=(c == 0), stop=(c == DC - 1))
            return inst
        ctx.op("pe", rmm, reads=[rhs_buf] + [get_lhsT(pt, c, buf=True) for c in range(DC)],
               writes=[pb])
        ctx.op("dve", lambda e, bank=bank: e.tensor_copy(tok[:, 0:8], ctx.ps[bank][:, 0:8]),
               reads=[pb], writes=[tokb])
        ctx.op("dve", lambda e: e.reduce_max(tok[:, 16:17], tok[:, 0:8], AX.X),
               reads=[tokb], writes=[tokb])
        ctx.op("dve", lambda e: e.tensor_scalar(tok[:, 8:16], tok[:, 0:8], tok[:, 16:17],
                                                None, ALU.is_equal),
               reads=[tokb], writes=[tokb])
        ctx.op("dve", lambda e: e.scalar_tensor_tensor(tok[:, 24:32], tok[:, 8:16], -1e30,
                                                       tok[:, 0:8], ALU.mult, ALU.add),
               reads=[tokb], writes=[tokb])
        ctx.op("dve", lambda e: e.reduce_max(tok[:, 17:18], tok[:, 24:32], AX.X),
               reads=[tokb], writes=[tokb])
        ctx.op("dve", lambda e: e.tensor_scalar(tok[:, 32:40], tok[:, 24:32], tok[:, 17:18],
                                                None, ALU.is_equal),
               reads=[tokb], writes=[tokb])
        ctx.op("dve", lambda e: e.tensor_tensor(tok[:, 18:19], tok[:, 17:18], tok[:, 16:17],
                                                ALU.subtract),
               reads=[tokb], writes=[tokb])
        ctx.op("act", lambda e: e.activation(tok[:, 19:20], tok[:, 18:19], AF.Exp),
               reads=[tokb], writes=[tokb])
        ctx.op("dve", lambda e: e.tensor_scalar(tok[:, 20:21], tok[:, 19:20], 1.0, None,
                                                ALU.add),
               reads=[tokb], writes=[tokb])
        ctx.op("dve", lambda e: e.reciprocal(tok[:, 21:22], tok[:, 20:21]),
               reads=[tokb], writes=[tokb])
        ctx.op("dve", lambda e: e.tensor_tensor(tok[:, 22:23], tok[:, 19:20], tok[:, 21:22],
                                                ALU.mult),
               reads=[tokb], writes=[tokb])
        ctx.op("dve", lambda e: e.tensor_scalar(tok[:, 40:48], tok[:, 8:16], tok[:, 21:22],
                                                None, ALU.mult),
               reads=[tokb], writes=[tokb])
        ctx.op("dve", lambda e: e.scalar_tensor_tensor(tok[:, 48:56], tok[:, 32:40],
                                                       tok[:, 22:23], tok[:, 40:48],
                                                       ALU.mult, ALU.add),
               reads=[tokb], writes=[tokb])
        tp = ctx.psb[2 + pt % 2]
        ctx.op("pe", lambda e, pt=pt: e.transpose(ctx.ps[2 + pt % 2][0:8, 0:128],
                                                  tok[:, 48:56], cf32(ctx, "id128f")),
               reads=[tokb, cf_b], writes=[tp])
        ctx.op("act", lambda e, pt=pt: e.activation(
            w8[0:8, pt * 128:(pt + 1) * 128], ctx.ps[2 + pt % 2][0:8, 0:128], AF.Copy),
            reads=[tp], writes=[w8b])



def emit_l2_ffn(ctx, W, cp, LC):
    cbf_b, cf_b = ctx.buf("cbf"), ctx.buf("cf32")
    tok = ctx.sb["TOK8"]
    SF0, SF1 = ctx.sb["SF0"], ctx.sb["SF1"]
    sb0, sb1, sb2, sb3 = ctx.sb["SB0"], ctx.sb["SB1"], ctx.sb["SB2"], ctx.sb["SB3"]
    wa, wb_, wc, wdt = ctx.sb["WA"], ctx.sb["WB"], ctx.sb["WC"], ctx.sb["WD"]
    scl = ctx.sb["SCL"]
    w8 = ctx.sb["R8B"]
    w8b = ctx.buf("R8B_l2")
    R16A = ctx.sb["R16A"]

    preps = {}
    for nm, src, R, C in [("w1", "L2_ffn_w1", 4096, D), ("w3", "L2_ffn_w3", 4096, D),
                          ("w2", "L2_ffn_w2", D, 4096)]:
        preps[nm] = prep_transposed(ctx, "l2_" + nm, W[src][0][:, :], W[src][1], R, C)
    b1te = ctx.dram_scratch("l2_b1te", [8, 128, 4096], BF16)
    b3te = ctx.dram_scratch("l2_b3te", [8, 128, 4096], BF16)
    a2te = ctx.dram_scratch("l2_a2te", [4096, 128], BF16)
    b1te_b = ctx.buf("l2b1te", tensor="l2b1te")
    b3te_b = ctx.buf("l2b3te", tensor="l2b3te")
    a2te_b = ctx.buf("l2a2te", tensor="l2a2te")
    zt = sb0[:, :]
    ztb = ctx.buf("SB0_zero")
    ctx.op("dve", lambda e: e.memset(zt, 0.0), writes=[ztb])
    for te, teb in [(b1te, b1te_b), (b3te, b3te_b)]:
        for e_i in range(8):
            for h in range(2):
                ctx.op("sp", lambda e, te=te, e_i=e_i, h=h: e.dma_start(
                    out=te.ap()[e_i, :, h * 2048:(h + 1) * 2048], in_=zt),
                    reads=[ztb], writes=[teb], kind="dma")
    big8 = ctx.sb["BIG8"]
    a1s_b = ctx.buf("BIG8_a1s")
    for nm, tgt_col, src in [("a1", 0, "L2_ffn_a1"), ("a3", 1024, "L2_ffn_a3")]:
        for e_i in range(8):
            at = sb1[0:16, 0:1024]
            atb = ctx.buf("SB1_l2a")
            ctx.op("pool", lambda e, src=src, e_i=e_i, at=at: e.dma_start(
                out=at, in_=W[src][0][e_i, :, :]),
                reads=[W[src][1]], writes=[atb], kind="dma")
            for kc in range(DC):
                ip = 4 + kc % 2
                ctx.op("pe", lambda e, at=at, kc=kc, ip=ip: e.transpose(
                    ctx.ps[ip][:, 0:8].bitcast(BF16), at[0:16, kc * 128:(kc + 1) * 128],
                    cbf_part(ctx, "id128", 16, 0, 16)),
                    reads=[atb, cbf_b], writes=[ctx.psb[ip]])
                ctx.op("dve", lambda e, tgt_col=tgt_col, kc=kc, e_i=e_i, ip=ip: e.tensor_copy(
                    big8[:, tgt_col + kc * 128 + e_i * 16:tgt_col + kc * 128 + e_i * 16 + 16],
                    ctx.ps[ip][:, 0:8].bitcast(BF16)),
                    reads=[ctx.psb[ip]], writes=[a1s_b])
    for te, teb, src in [(b1te, b1te_b, "L2_ffn_b1"), (b3te, b3te_b, "L2_ffn_b3")]:
        for e_i in range(8):
            for m in range(32):
                bt = sb1[:, 0:16]
                btb = ctx.buf("SB1_l2b")
                ctx.op("pool", lambda e, src=src, e_i=e_i, m=m, bt=bt: e.dma_start(
                    out=bt, in_=W[src][0][e_i, m * 128:(m + 1) * 128, :]),
                    reads=[W[src][1]], writes=[btb], kind="dma")
                ip = 4 + m % 2
                ctx.op("pe", lambda e, bt=bt, ip=ip: e.transpose(
                    ctx.ps[ip][0:16, 0:64].bitcast(BF16), bt, cbf(ctx, "id128")),
                    reads=[btb, cbf_b], writes=[ctx.psb[ip]])
                st16 = sb1[0:16, 16:144]
                st16b = ctx.buf("SB1_l2bt")
                ctx.op("dve", lambda e, ip=ip, st16=st16: e.tensor_copy(
                    st16, ctx.ps[ip][0:16, 0:64].bitcast(BF16)),
                    reads=[ctx.psb[ip]], writes=[st16b])
                ctx.op("sp", lambda e, te=te, e_i=e_i, m=m, st16=st16: e.dma_start(
                    out=te.ap()[e_i, e_i * 16:e_i * 16 + 16, m * 128:(m + 1) * 128], in_=st16),
                    reads=[st16b], writes=[teb], kind="dma")
    for m in range(32):
        at = sb1[:, 0:128]
        atb = ctx.buf("SB1_l2a2")
        def lda2(e, m=m):
            return [e.dma_start(
                out=sb1[e_i * 16:(e_i + 1) * 16, 0:128],
                in_=W["L2_ffn_a2"][0][e_i, :, m * 128:(m + 1) * 128])
                for e_i in range(8)]
        ctx.op("pool", lda2, reads=[W["L2_ffn_a2"][1]], writes=[atb], kind="dma", nd=8)
        ip = 4 + m % 2
        ctx.op("pe", lambda e, at=at, ip=ip: e.transpose(
            ctx.ps[ip][:, 0:64].bitcast(BF16), at, cbf(ctx, "id128")),
            reads=[atb, cbf_b], writes=[ctx.psb[ip]])
        st = sb1[:, 128:256]
        stb = ctx.buf("SB1_l2a2t")
        ctx.op("dve", lambda e, ip=ip, st=st: e.tensor_copy(
            st, ctx.ps[ip][:, 0:64].bitcast(BF16)),
            reads=[ctx.psb[ip]], writes=[stb])
        ctx.op("sp", lambda e, m=m, st=st: e.dma_start(
            out=a2te.ap()[m * 128:(m + 1) * 128, :], in_=st),
            reads=[stb], writes=[a2te_b], kind="dma")
    gt_b = ctx.buf("WD_l2g")
    for kc in range(DC):
        gtile = sb1[0:8, 0:128]
        gtileb = ctx.buf("SB1_l2g")
        ctx.op("pool", lambda e, kc=kc, gtile=gtile: e.dma_start(
            out=gtile, in_=W["L2_ffn_gate_w"][0][:, kc * 128:(kc + 1) * 128]),
            reads=[W["L2_ffn_gate_w"][1]], writes=[gtileb], kind="dma")
        ip = 4 + kc % 2
        ctx.op("pe", lambda e, gtile=gtile, kc=kc, ip=ip: e.transpose(
            ctx.ps[ip][:, 0:4].bitcast(BF16), gtile, cbf_part(ctx, "id128", 8, 0, 8)),
            reads=[gtileb, cbf_b], writes=[ctx.psb[ip]])
        ctx.op("dve", lambda e, kc=kc, ip=ip: e.tensor_copy(
            wdt[:, 1024 + kc * 8:1024 + (kc + 1) * 8], ctx.ps[ip][:, 0:4].bitcast(BF16)),
            reads=[ctx.psb[ip]], writes=[gt_b])

    nw_off, nw_b = load_feature_cols(ctx, cp, "l2nw", W["L2_ffn_norm_w"][0],
                                     W["L2_ffn_norm_w"][1], D)
    sqb = [ctx.buf("SB2_sq0"), ctx.buf("SB2_sq1")]
    for tt in range(TT):
        for c in range(DC):
            XN = ctx.sb[f"XN{c}"]
            qa = sb2[:, (c % 2) * 512:(c % 2) * 512 + 512]
            qb = sqb[c % 2]
            ctx.op("act", lambda e, XN=XN, qa=qa, tt=tt: e.activation(
                qa, XN[:, tt * 512:(tt + 1) * 512], AF.Square),
                reads=[ctx.buf(f"XN{c}")], writes=[qb])
            ctx.op("pe", lambda e, qa=qa, c=c, tt=tt: e.matmul(
                ctx.ps[tt][0:1, :], cbf(ctx, "ones_col"), qa,
                start=(c == 0), stop=(c == DC - 1)),
                reads=[qb, cbf_b], writes=[ctx.psb[tt]])
    rrow = ctx.sb["RWA"]
    rrow_b = ctx.buf("RWA_l2rms")
    for tt in range(TT):
        sl = slice(tt * 512, (tt + 1) * 512)
        ctx.op("act", lambda e, sl=sl, tt=tt: e.activation(
            rrow[:, sl], ctx.ps[tt][0:1, :], AF.Sqrt, scale=1.0 / D,
            bias=cf32_part(ctx, "eps1e6", 1, 0, 1)),
            reads=[ctx.psb[tt]], writes=[rrow_b])
    ctx.op("dve", lambda e: e.reciprocal(rrow[:, :], rrow[:, :]),
           reads=[rrow_b], writes=[rrow_b])
    bcast_row_to_plane(ctx, rrow, rrow_b, "SF0", ctx.buf("SF0_l2rms"))
    tdram = ctx.dram_scratch("l2_t", [D, T], BF16).ap()
    tdram_b = ctx.buf("l2t", tensor="l2t")
    for c in range(DC):
        XN = ctx.sb[f"XN{c}"]
        tt_ap = sb2[:, 0:T]
        ttb = ctx.buf("SB2_l2t")
        ctx.op("dve", lambda e, XN=XN, c=c, tt_ap=tt_ap: e.scalar_tensor_tensor(
            tt_ap, XN[:, :], scl[:, nw_off + c:nw_off + c + 1], SF0[:, :],
            ALU.mult, ALU.mult),
            reads=[ctx.buf(f"XN{c}"), nw_b, ctx.buf("SF0_l2rms")], writes=[ttb])
        ctx.op("sp", lambda e, c=c, tt_ap=tt_ap: e.dma_start(
            out=tdram[c * 128:(c + 1) * 128, :], in_=tt_ap),
            reads=[ttb], writes=[tdram_b], kind="dma")

    def get_lhsT(pt, c, buf=False):
        if buf:
            return ctx.buf(f"SB3_l2r{c % 4}")
        return sb3[:, (c % 4) * 128:(c % 4) * 128 + 128]
    for pt in range(16):
        for c in range(DC):
            dst = sb3[:, (c % 4) * 128 + (c // 4) * 512:(c % 4) * 128 + (c // 4) * 512 + 128]
            dstb = ctx.buf(f"SB3_l2r{c}")
            ctx.op("sp", lambda e, pt=pt, c=c, dst=dst: e.dma_start(
                out=dst, in_=tdram[c * 128:(c + 1) * 128, pt * 128:(pt + 1) * 128]),
                reads=[tdram_b], writes=[dstb], kind="dma")
        bank = pt % 2
        pb = ctx.psb[bank]
        def rmm(e, pt=pt, bank=bank):
            inst = None
            for c in range(DC):
                lap = sb3[:, (c % 4) * 128 + (c // 4) * 512:(c % 4) * 128 + (c // 4) * 512 + 128]
                inst = e.matmul(ctx.ps[bank][:, 0:8], lap,
                                wdt[:, 1024 + c * 8:1024 + (c + 1) * 8],
                                start=(c == 0), stop=(c == DC - 1))
            return inst
        ctx.op("pe", rmm, reads=[gt_b] + [ctx.buf(f"SB3_l2r{c}") for c in range(DC)],
               writes=[pb])
        _router_tail(ctx, bank, pb, w8, w8b, pt)

    for tt in range(TT):
        tsl = slice(tt * 512, (tt + 1) * 512)
        tslots = []
        for kc in range(DC):
            tens = sb0 if kc < 4 else sb1
            apo = tens[:, (kc % 4) * 512:(kc % 4) * 512 + 512]
            bo = ctx.buf(("SB0" if kc < 4 else "SB1") + f"_l2t{kc % 4}")
            tslots.append((apo, bo))
            ctx.op("sp", lambda e, kc=kc, apo=apo, tsl=tsl: e.dma_start(
                out=apo, in_=tdram[kc * 128:(kc + 1) * 128, tsl]),
                reads=[tdram_b], writes=[bo], kind="dma")
        for li, (col0, rb_name) in enumerate([(0, "SB3_r1"), (1024, "SB3_r3")]):
            pb = ctx.psb[6]
            def rmm2(e, col0=col0):
                inst = None
                for kc in range(DC):
                    inst = e.matmul(ctx.ps[6][:, :], big8[:, col0 + kc * 128:col0 + (kc + 1) * 128],
                                    tslots[kc][0], start=(kc == 0), stop=(kc == DC - 1))
                return inst
            ctx.op("pe", rmm2, reads=[a1s_b] + [b for _, b in tslots], writes=[pb])
            dst = sb3[:, 1024 + li * 512:1024 + li * 512 + 512]
            ctx.op("act", lambda e, dst=dst: e.activation(dst, ctx.ps[6][:, :], AF.Copy,
                                                          scale=2.0),
                   reads=[pb], writes=[ctx.buf(rb_name)])
        r2sb = []
        for e_i in range(8):
            tens = SF0 if e_i < 4 else SF1
            apo = tens[0:16, (e_i % 4) * 256:(e_i % 4) * 256 + 256].bitcast(BF16)
            bo = ctx.buf(("SF0" if e_i < 4 else "SF1") + f"_l2r2_{e_i % 4}")
            r2sb.append((apo, bo))
        for m in range(32):
            w13t = wa if m % 2 == 0 else wc
            wab = ctx.buf(("WA" if m % 2 == 0 else "WC") + "_l2w13")
            load_lhsT(ctx, w13t[:, 0:1024], preps["w1"][0], preps["w1"][1], wab,
                      m * 128, 128, DC)
            load_lhsT(ctx, w13t[:, 1024:2048], preps["w3"][0], preps["w3"][1], wab,
                      m * 128, 128, DC)
            btet = wb_ if m % 2 == 0 else ctx.sb["BIG9"]
            wbb = ctx.buf(("WB" if m % 2 == 0 else "BIG9") + "_l2bte")
            def tlb(e, m=m, btet=btet):
                return [
                    e.dma_start(
                        out=btet[:, 0:1024].rearrange("p (ee m2) -> p ee m2", ee=8),
                        in_=b1te.ap()[:, :, m * 128:(m + 1) * 128].rearrange(
                            "ee p m2 -> p ee m2")),
                    e.dma_start(
                        out=btet[:, 1024:2048].rearrange("p (ee m2) -> p ee m2", ee=8),
                        in_=b3te.ap()[:, :, m * 128:(m + 1) * 128].rearrange(
                            "ee p m2 -> p ee m2"))]
            ctx.op("sp", tlb, reads=[b1te_b, b3te_b], writes=[wbb], kind="dma", nd=2)
            a2tt = wdt[:, 0:128] if m % 2 == 0 else tok[:, 0:64].bitcast(BF16)
            wdb2 = ctx.buf(("WD" if m % 2 == 0 else "TOK8") + "_l2a2te")
            ctx.op("sp", lambda e, m=m, a2tt=a2tt: e.dma_start(
                out=a2tt, in_=a2te.ap()[m * 128:(m + 1) * 128, :]),
                reads=[a2te_b], writes=[wdb2], kind="dma")
            for bi, wcol in [(0, 0), (1, 1024)]:
                def bmm(e, bi=bi, wcol=wcol, w13t=w13t):
                    inst = None
                    for kc in range(DC):
                        inst = e.matmul(ctx.ps[bi][:, :],
                                        w13t[:, wcol + kc * 128:wcol + kc * 128 + 128],
                                        tslots[kc][0], start=(kc == 0), stop=(kc == DC - 1))
                    return inst
                ctx.op("pe", bmm, reads=[wab] + [b for _, b in tslots], writes=[ctx.psb[bi]])
            b1s = sb2[:, 0:512]
            b3s = sb2[:, 512:1024]
            b1sb, b3sb = ctx.buf("SB2_b1s"), ctx.buf("SB2_b3s")
            ctx.op("act", lambda e, b1s=b1s: e.activation(b1s, ctx.ps[0][:, :], AF.Copy),
                   reads=[ctx.psb[0]], writes=[b1sb])
            ctx.op("act", lambda e, b3s=b3s: e.activation(b3s, ctx.ps[1][:, :], AF.Copy),
                   reads=[ctx.psb[1]], writes=[b3sb])
            for e_i in range(8):
                pw = ctx.psb[5]
                ctx.op("pe", lambda e, e_i=e_i, tt=tt: e.matmul(
                    ctx.ps[5][:, :], lc_bf(ctx, LC["sel8"], 8, e_i * 128, 128),
                    w8[0:8, tt * 512:(tt + 1) * 512], start=True, stop=True),
                    reads=[w8b, LC["sel8"][2]], writes=[pw])
                wbc = sb2[:, 1024:1536]
                wbcb = ctx.buf("SB2_wbc")
                ctx.op("act", lambda e, wbc=wbc: e.activation(wbc, ctx.ps[5][:, :], AF.Copy),
                       reads=[pw], writes=[wbcb])
                def hmm(e, e_i=e_i, btet=btet):
                    e.matmul(ctx.ps[2][:, :], cbf(ctx, "id128"), sb2[:, 0:512],
                             start=True, stop=False)
                    e.matmul(ctx.ps[2][:, :], btet[:, e_i * 128:(e_i + 1) * 128],
                             sb3[:, 1024:1536], start=False, stop=True)
                    e.matmul(ctx.ps[3][:, :], cbf(ctx, "id128"), sb2[:, 512:1024],
                             start=True, stop=False)
                    inst = e.matmul(ctx.ps[3][:, :],
                                    btet[:, 1024 + e_i * 128:1024 + (e_i + 1) * 128],
                                    sb3[:, 1536:2048], start=False, stop=True)
                    return inst
                ctx.op("pe", hmm,
                       reads=[b1sb, b3sb, wbb, ctx.buf("SB3_r1"), ctx.buf("SB3_r3"), cbf_b],
                       writes=[ctx.psb[2], ctx.psb[3]])
                sl1 = sb2[:, 1536:2048]
                sl1b = ctx.buf("SB2_silu")
                ctx.op("act", lambda e, sl1=sl1: e.activation(sl1, ctx.ps[2][:, :], AF.Silu),
                       reads=[ctx.psb[2]], writes=[sl1b])
                t3w = sb3[:, 0:512]
                t3wb = ctx.buf("SB3_t3w")
                ctx.op("dve", lambda e, t3w=t3w, wbc=wbc: e.tensor_tensor(
                    t3w, ctx.ps[3][:, :], wbc, ALU.mult),
                    reads=[ctx.psb[3], wbcb], writes=[t3wb])
                actw = sb3[:, 512:1024]
                actwb = ctx.buf("SB3_actw")
                ctx.op("dve", lambda e, actw=actw, sl1=sl1, t3w=t3w: e.tensor_tensor(
                    actw, sl1, t3w, ALU.mult),
                    reads=[sl1b, t3wb], writes=[actwb])
                mx = ctx.sb[f"BIG{m // 4}"][:, (m % 4) * 512:(m % 4) * 512 + 512]
                mxb = ctx.buf(f"BIG{m // 4}_mx{m % 4}")
                if e_i == 0:
                    ctx.op("dve", lambda e, mx=mx, actw=actw: e.tensor_copy(mx, actw),
                           reads=[actwb], writes=[mxb])
                else:
                    ctx.op("dve", lambda e, mx=mx, actw=actw: e.tensor_tensor(
                        mx, mx, actw, ALU.add),
                        reads=[actwb, mxb], writes=[mxb])
                pr2 = ctx.psb[6]
                a2view = wdt[:, 0:128] if m % 2 == 0 else tok[:, 0:64].bitcast(BF16)
                ctx.op("pe", lambda e, e_i=e_i, actw=actw, a2view=a2view: e.matmul(
                    ctx.ps[6][0:16, :], a2view[:, e_i * 16:e_i * 16 + 16], actw,
                    start=True, stop=True),
                    reads=[wdb2, actwb], writes=[pr2])
                if m == 0:
                    ctx.op("dve", lambda e, e_i=e_i: e.tensor_scalar(
                        r2sb[e_i][0], ctx.ps[6][0:16, :], 2.0, None, ALU.mult),
                        reads=[pr2], writes=[r2sb[e_i][1]])
                else:
                    ctx.op("dve", lambda e, e_i=e_i: e.scalar_tensor_tensor(
                        r2sb[e_i][0], ctx.ps[6][0:16, :], 2.0, r2sb[e_i][0],
                        ALU.mult, ALU.add),
                        reads=[pr2, r2sb[e_i][1]], writes=[r2sb[e_i][1]])
        for dm in range(DC):
            b2t_b = ctx.buf("R16A_b2t")
            for e_i in range(8):
                bt = sb2[:, 1536:1552]
                btb = ctx.buf("SB2_b2l")
                ctx.op("pool", lambda e, e_i=e_i, dm=dm, bt=bt: e.dma_start(
                    out=bt, in_=W["L2_ffn_b2"][0][e_i, dm * 128:(dm + 1) * 128, :]),
                    reads=[W["L2_ffn_b2"][1]], writes=[btb], kind="dma")
                ip = 5
                ctx.op("pe", lambda e, bt=bt: e.transpose(
                    ctx.ps[5][0:16, 0:64].bitcast(BF16), bt, cbf(ctx, "id128")),
                    reads=[btb, cbf_b], writes=[ctx.psb[5]])
                ctx.op("dve", lambda e, e_i=e_i: e.tensor_copy(
                    R16A[0:16, e_i * 128:(e_i + 1) * 128],
                    ctx.ps[5][0:16, 0:64].bitcast(BF16)),
                    reads=[ctx.psb[5]], writes=[b2t_b])
            w2tb = ctx.buf("WA_l2w2t")
            w2tb2 = ctx.buf("WB_l2w2t")
            load_lhsT(ctx, wa[:, 0:2048], preps["w2"][0], preps["w2"][1], w2tb,
                      dm * 128, 128, 16, k0=0)
            load_lhsT(ctx, wb_[:, 0:2048], preps["w2"][0], preps["w2"][1], w2tb2,
                      dm * 128, 128, 16, k0=2048)
            pY = ctx.psb[4]
            def ymm(e, dm=dm):
                inst = None
                for kc in range(32):
                    tgt = wa if kc < 16 else wb_
                    mxs = ctx.sb[f"BIG{kc // 4}"][:, (kc % 4) * 512:(kc % 4) * 512 + 512]
                    inst = e.matmul(ctx.ps[4][:, :], tgt[:, (kc % 16) * 128:(kc % 16 + 1) * 128],
                                    mxs, start=(kc == 0), stop=False)
                for e_i in range(8):
                    inst = e.matmul(ctx.ps[4][:, :], R16A[0:16, e_i * 128:(e_i + 1) * 128],
                                    r2sb[e_i][0], start=False, stop=(e_i == 7))
                return inst
            ctx.op("pe", ymm,
                   reads=[w2tb, w2tb2, b2t_b]
                   + [ctx.buf(f"BIG{kc // 4}_mx{kc % 4}") for kc in range(32)]
                   + [b for _, b in r2sb],
                   writes=[pY])
            G = ctx.sb[f"G{dm}"]
            gb = ctx.buf(f"G{dm}_{tt}")
            ctx.op("act", lambda e, G=G, tsl=tsl: e.activation(G[:, tsl], ctx.ps[4][:, :],
                                                               AF.Silu),
                   reads=[pY], writes=[gb])


def _router_tail(ctx, bank, pb, w8, w8b, pt):
    tok = ctx.sb["TOK8"]
    tokb = ctx.buf("TOK8_r")
    cf_b = ctx.buf("cf32")
    ctx.op("dve", lambda e, bank=bank: e.tensor_copy(tok[:, 0:8], ctx.ps[bank][:, 0:8]),
           reads=[pb], writes=[tokb])
    ctx.op("dve", lambda e: e.reduce_max(tok[:, 16:17], tok[:, 0:8], AX.X),
           reads=[tokb], writes=[tokb])
    ctx.op("dve", lambda e: e.tensor_scalar(tok[:, 8:16], tok[:, 0:8], tok[:, 16:17],
                                            None, ALU.is_equal),
           reads=[tokb], writes=[tokb])
    ctx.op("dve", lambda e: e.scalar_tensor_tensor(tok[:, 24:32], tok[:, 8:16], -1e30,
                                                   tok[:, 0:8], ALU.mult, ALU.add),
           reads=[tokb], writes=[tokb])
    ctx.op("dve", lambda e: e.reduce_max(tok[:, 17:18], tok[:, 24:32], AX.X),
           reads=[tokb], writes=[tokb])
    ctx.op("dve", lambda e: e.tensor_scalar(tok[:, 32:40], tok[:, 24:32], tok[:, 17:18],
                                            None, ALU.is_equal),
           reads=[tokb], writes=[tokb])
    ctx.op("dve", lambda e: e.tensor_tensor(tok[:, 18:19], tok[:, 17:18], tok[:, 16:17],
                                            ALU.subtract),
           reads=[tokb], writes=[tokb])
    ctx.op("act", lambda e: e.activation(tok[:, 19:20], tok[:, 18:19], AF.Exp),
           reads=[tokb], writes=[tokb])
    ctx.op("dve", lambda e: e.tensor_scalar(tok[:, 20:21], tok[:, 19:20], 1.0, None, ALU.add),
           reads=[tokb], writes=[tokb])
    ctx.op("dve", lambda e: e.reciprocal(tok[:, 21:22], tok[:, 20:21]),
           reads=[tokb], writes=[tokb])
    ctx.op("dve", lambda e: e.tensor_tensor(tok[:, 22:23], tok[:, 19:20], tok[:, 21:22],
                                            ALU.mult),
           reads=[tokb], writes=[tokb])
    ctx.op("dve", lambda e: e.tensor_scalar(tok[:, 40:48], tok[:, 8:16], tok[:, 21:22],
                                            None, ALU.mult),
           reads=[tokb], writes=[tokb])
    ctx.op("dve", lambda e: e.scalar_tensor_tensor(tok[:, 48:56], tok[:, 32:40],
                                                   tok[:, 22:23], tok[:, 40:48],
                                                   ALU.mult, ALU.add),
           reads=[tokb], writes=[tokb])
    tp = ctx.psb[2 + pt % 2]
    ctx.op("pe", lambda e, pt=pt: e.transpose(ctx.ps[2 + pt % 2][0:8, 0:128],
                                              tok[:, 48:56], cf32(ctx, "id128f")),
           reads=[tokb, cf_b], writes=[tp])
    ctx.op("act", lambda e, pt=pt: e.activation(
        w8[0:8, pt * 128:(pt + 1) * 128], ctx.ps[2 + pt % 2][0:8, 0:128], AF.Copy),
        reads=[tp], writes=[w8b])



def emit_l2_attn(ctx, W, cp, LC, xd_in, xdin_buf, xd_out, xdout_buf):
    cbf_b, cf_b = ctx.buf("cbf"), ctx.buf("cf32")
    SF0, SF1, SF2 = ctx.sb["SF0"], ctx.sb["SF1"], ctx.sb["SF2"]
    sb0, sb1, sb2, sb3 = ctx.sb["SB0"], ctx.sb["SB1"], ctx.sb["SB2"], ctx.sb["SB3"]
    wa, wb_, wc, wdt = ctx.sb["WA"], ctx.sb["WB"], ctx.sb["WC"], ctx.sb["WD"]
    scl = ctx.sb["SCL"]

    winT, winT_b = prep_transposed(ctx, "l2_win", W["L2_attn_w_in"][0][:, :],
                                   W["L2_attn_w_in"][1], 3072, D)
    woutT, woutT_b = prep_transposed(ctx, "l2_wout", W["L2_attn_w_out"][0][:, :],
                                     W["L2_attn_w_out"][1], D, D)
    bi_off, bi_b = load_feature_cols(ctx, cp, "l2bi", W["L2_attn_b_in"][0],
                                     W["L2_attn_b_in"][1], 3072)
    bo_off, bo_b = load_feature_cols(ctx, cp, "l2bo", W["L2_attn_b_out"][0],
                                     W["L2_attn_b_out"][1], D)

    qfm = ctx.dram_scratch("l2_qfm", [D, T], BF16).ap()
    kfm = ctx.dram_scratch("l2_kfm", [D, T], BF16).ap()
    vtm = ctx.dram_scratch("l2_vtm", [T, D], BF16).ap()
    qfm_b = ctx.buf("l2qfm", tensor="l2qfm")
    kfm_b = ctx.buf("l2kfm", tensor="l2kfm")
    vtm_b = ctx.buf("l2vtm", tensor="l2vtm")
    for pi_, nm in enumerate(("q", "k", "v")):
        for mo in range(DC):
            wqt = wc if mo % 2 == 0 else wdt
            wcb = ctx.buf(("WC" if mo % 2 == 0 else "WD") + "_l2a")
            load_lhsT(ctx, wqt[:, 0:1024], winT, winT_b, wcb, pi_ * D + mo * 128, 128, DC)
            for tt in range(TT):
                tsl = slice(tt * 512, (tt + 1) * 512)
                iY = tt % 2
                pY = ctx.psb[iY]
                def qmm(e, iY=iY, tsl=tsl, wqt=wqt):
                    inst = None
                    for kc in range(DC):
                        XN = ctx.sb[f"XN{kc}"]
                        inst = e.matmul(ctx.ps[iY][:, :], wqt[:, kc * 128:(kc + 1) * 128],
                                        XN[:, tsl], start=(kc == 0), stop=(kc == DC - 1))
                    return inst
                ctx.op("pe", qmm, reads=[wcb] + [ctx.buf(f"XN{c}") for c in range(DC)],
                       writes=[pY])
                fm = sb1[:, (tt % 2) * 512:(tt % 2) * 512 + 512]
                fmb = ctx.buf(f"SB1_fm{tt % 2}")
                cc = bi_off + pi_ * 8 + mo
                ctx.op("dve", lambda e, fm=fm, iY=iY, cc=cc: e.tensor_scalar(
                    fm, ctx.ps[iY][:, :], scl[:, cc:cc + 1], None, ALU.add),
                    reads=[pY, bi_b], writes=[fmb])
                if nm in ("q", "k"):
                    dst = qfm if nm == "q" else kfm
                    dstb = qfm_b if nm == "q" else kfm_b
                    ctx.op("sp", lambda e, fm=fm, mo=mo, tsl=tsl, dst=dst: e.dma_start(
                        out=dst[mo * 128:(mo + 1) * 128, tsl], in_=fm),
                        reads=[fmb], writes=[dstb], kind="dma")
                else:
                    for ts in range(4):
                        ip2 = 2 + ts % 2
                        ctx.op("pe", lambda e, fm=fm, ts=ts, ip2=ip2: e.transpose(
                            ctx.ps[ip2][:, 0:64].bitcast(BF16),
                            fm[:, ts * 128:(ts + 1) * 128], cbf(ctx, "id128")),
                            reads=[fmb, cbf_b], writes=[ctx.psb[ip2]])
                        stg = sb2[:, (ts % 2) * 128:(ts % 2) * 128 + 128]
                        stgb = ctx.buf(f"SB2_tm{ts % 2}")
                        ctx.op("dve", lambda e, stg=stg, ip2=ip2: e.tensor_copy(
                            stg, ctx.ps[ip2][:, 0:64].bitcast(BF16)),
                            reads=[ctx.psb[ip2]], writes=[stgb])
                        pt_g = tt * 4 + ts
                        ctx.op("sp", lambda e, stg=stg, pt_g=pt_g, mo=mo: e.dma_start(
                            out=vtm[pt_g * 128:(pt_g + 1) * 128, mo * 128:(mo + 1) * 128],
                            in_=stg),
                            reads=[stgb], writes=[vtm_b], kind="dma")

    aod = ctx.dram_scratch("l2_attn_out", [D, T], BF16)
    aod_b = ctx.buf("l2_attn_out", tensor="l2_attn_out")
    SCALE = 1.0 / 8.0

    for hh in range(16):
        hc, hb = hh // 2, (hh % 2) * 64
        kh_b = ctx.buf("WA_l2k")
        qh_b = ctx.buf("WB_l2q")
        ctx.op("sp", lambda e, hc=hc, hb=hb: e.dma_start(
            out=wa[0:64, :], in_=kfm[hc * 128 + hb:hc * 128 + hb + 64, :]),
            reads=[kfm_b], writes=[kh_b], kind="dma")
        ctx.op("sp", lambda e, hc=hc, hb=hb: e.dma_start(
            out=wb_[0:64, :], in_=qfm[hc * 128 + hb:hc * 128 + hb + 64, :]),
            reads=[qfm_b], writes=[qh_b], kind="dma")
        vh_b = ctx.buf("SB2_l2v")
        ctx.op("sp", lambda e, hh=hh: e.dma_start(
            out=sb2[:, 0:1024].rearrange("p (ts dd) -> p ts dd", ts=16),
            in_=vtm.rearrange("(ts p) d -> p ts d", p=128)[:, :, hh * 64:(hh + 1) * 64]),
            reads=[vtm_b], writes=[vh_b], kind="dma")
        for qt in range(16):
            nk = 1 if qt == 0 else 2
            k0 = (qt - 1) * 128 if qt > 0 else 0
            kw = nk * 128
            psS = ctx.psb[0 + qt % 2]
            ctx.op("pe", lambda e, qt=qt, k0=k0, kw=kw, i=0 + qt % 2: e.matmul(
                ctx.ps[i][:, 0:kw], wb_[0:64, qt * 128:(qt + 1) * 128],
                wa[0:64, k0:k0 + kw], start=True, stop=True),
                reads=[kh_b, qh_b], writes=[psS])
            sS = SF1[:, 0:256]
            sSb = ctx.buf("SF1_l2s")
            mask = (lc_f32(ctx, LC["wmask0"], 128, 0, 128) if qt == 0
                    else lc_f32(ctx, LC["wmask"], 128, 0, 256))
            mread = LC["wmask0"][2] if qt == 0 else LC["wmask"][2]
            ctx.op("dve", lambda e, kw=kw, i=0 + qt % 2, mask=mask: e.scalar_tensor_tensor(
                sS[:, 0:kw], ctx.ps[i][:, 0:kw], SCALE, mask, ALU.mult, ALU.add),
                reads=[psS, mread], writes=[sSb])
            mx = SF1[:, 256:257]
            ctx.op("dve", lambda e, kw=kw: e.reduce_max(mx, sS[:, 0:kw], AX.X),
                   reads=[sSb], writes=[sSb])
            ctx.op("dve", lambda e: e.tensor_scalar(SF1[:, 257:258], mx, -1.0, None, ALU.mult),
                   reads=[sSb], writes=[sSb])
            P = sb3[:, 0:256]
            Pb = ctx.buf("SB3_l2p")
            ctx.op("act", lambda e, kw=kw: e.activation(P[:, 0:kw], sS[:, 0:kw], AF.Exp,
                                                        bias=SF1[:, 257:258]),
                   reads=[sSb], writes=[Pb])
            den = SF1[:, 258:259]
            ctx.op("dve", lambda e, kw=kw: e.reduce_sum(den, P[:, 0:kw], AX.X),
                   reads=[Pb], writes=[sSb])
            ctx.op("dve", lambda e: e.reciprocal(SF1[:, 259:260], den),
                   reads=[sSb], writes=[sSb])
            ctx.op("dve", lambda e, kw=kw: e.tensor_scalar(
                P[:, 0:kw], P[:, 0:kw], SF1[:, 259:260], None, ALU.mult),
                reads=[sSb, Pb], writes=[Pb])
            for kt in range(nk):
                ip2 = 2 + kt
                ctx.op("pe", lambda e, kt=kt, ip2=ip2: e.transpose(
                    ctx.ps[ip2][:, 0:64].bitcast(BF16), P[:, kt * 128:(kt + 1) * 128],
                    cbf(ctx, "id128")),
                    reads=[Pb, cbf_b], writes=[ctx.psb[ip2]])
                pts = sb3[:, 256 + kt * 128:256 + (kt + 1) * 128]
                ptsb = ctx.buf(f"SB3_l2pt{kt}")
                ctx.op("dve", lambda e, pts=pts, ip2=ip2: e.tensor_copy(
                    pts, ctx.ps[ip2][:, 0:64].bitcast(BF16)),
                    reads=[ctx.psb[ip2]], writes=[ptsb])
            pO = ctx.psb[4 + qt % 2]
            def pvmm(e, qt=qt, nk=nk, i=4 + qt % 2):
                inst = None
                for kt in range(nk):
                    ts = (qt - 1 + kt) if qt > 0 else 0
                    inst = e.matmul(ctx.ps[i][0:64, 0:128],
                                    sb2[:, ts * 64:(ts + 1) * 64],
                                    sb3[:, 256 + kt * 128:256 + (kt + 1) * 128],
                                    start=(kt == 0), stop=(kt == nk - 1))
                return inst
            ctx.op("pe", pvmm,
                   reads=[vh_b] + [ctx.buf(f"SB3_l2pt{kt}") for kt in range(nk)],
                   writes=[pO])
            og = sb3[:, 512:640]
            ogb = ctx.buf("SB3_l2o")
            ctx.op("dve", lambda e, i=4 + qt % 2, og=og: e.tensor_copy(
                og[0:64, :], ctx.ps[i][0:64, 0:128]),
                reads=[pO], writes=[ogb])
            ctx.op("sp", lambda e, hh=hh, qt=qt, og=og: e.dma_start(
                out=aod.ap()[hh * 64:(hh + 1) * 64, qt * 128:(qt + 1) * 128], in_=og[0:64, :]),
                reads=[ogb], writes=[aod_b], kind="dma")

    for qt in range(TT):
        qsl = slice(qt * 512, (qt + 1) * 512)
        ao_slots = []
        for kc in range(DC):
            tens = ctx.sb[f"BIG{kc // 2}"]
            apo = tens[:, (kc % 2) * 512:(kc % 2) * 512 + 512]
            bo = ctx.buf(f"BIG{kc // 2}_l2ao{kc % 2}")
            ao_slots.append((apo, bo))
            ctx.op("sp", lambda e, apo=apo, kc=kc, qsl=qsl: e.dma_start(
                out=apo, in_=aod.ap()[kc * 128:(kc + 1) * 128, qsl]),
                reads=[aod_b], writes=[bo], kind="dma")
        for mo in range(DC):
            wot = wdt if mo % 2 == 0 else wc
            woff = 1024 if mo % 2 == 0 else 0
            wob = ctx.buf(("WD" if mo % 2 == 0 else "WC") + "_l2wo")
            load_lhsT(ctx, wot[:, woff:woff + 1024], woutT, woutT_b, wob, mo * 128, 128, DC)
            iY = mo % 2
            pY = ctx.psb[iY]
            def womm(e, iY=iY, ao_slots=ao_slots, wot=wot, woff=woff):
                inst = None
                for kc in range(DC):
                    inst = e.matmul(ctx.ps[iY][:, :],
                                    wot[:, woff + kc * 128:woff + (kc + 1) * 128],
                                    ao_slots[kc][0], start=(kc == 0), stop=(kc == DC - 1))
                return inst
            ctx.op("pe", womm, reads=[wob] + [b for _, b in ao_slots], writes=[pY])
            st_ = sb3[:, 0:512]
            stb = ctx.buf("SB3_l2cs")
            cc = bo_off + mo
            ctx.op("dve", lambda e, st_=st_, iY=iY, cc=cc: e.tensor_scalar(
                st_, ctx.ps[iY][:, :], scl[:, cc:cc + 1], None, ALU.add),
                reads=[pY, bo_b], writes=[stb])
            G = ctx.sb[f"G{mo}"]
            tt_ = sb3[:, 512:1024]
            ttb = ctx.buf("SB3_l2ct")
            ctx.op("dve", lambda e, tt_=tt_, st_=st_, G=G, qsl=qsl: e.tensor_tensor(
                tt_, st_, G[:, qsl], ALU.mult),
                reads=[stb, ctx.buf(f"G{mo}_{qt}")], writes=[ttb])
            xst = SF1[:, (mo % 2) * 512:(mo % 2) * 512 + 512]
            xstb = ctx.buf(f"SF1_x{mo % 2}")
            ctx.op("sp", lambda e, xst=xst, mo=mo, qsl=qsl: e.dma_start(
                out=xst, in_=xd_in[mo * 128:(mo + 1) * 128, qsl]),
                reads=[xdin_buf], writes=[xstb], kind="dma")
            xon = SF2[:, (mo % 2) * 512:(mo % 2) * 512 + 512]
            xonb = ctx.buf(f"SF2_xo{mo % 2}")
            ctx.op("dve", lambda e, xon=xon, xst=xst, tt_=tt_: e.tensor_tensor(
                xon, xst, tt_, ALU.add),
                reads=[xstb, ttb], writes=[xonb])
            ctx.op("sp", lambda e, xon=xon, mo=mo, qsl=qsl: e.dma_start(
                out=xd_out[mo * 128:(mo + 1) * 128, qsl], in_=xon),
                reads=[xonb], writes=[xdout_buf], kind="dma")


def emit_layer2(ctx, W, xd_in, xdin_buf, xd_out, xdout_buf):
    cp = ColPool(ctx)
    LC = load_lconsts(ctx, names_bf=["sel8"], names_f32=["wmask", "wmask0"])
    emit_ln_xn(ctx, xd_in, xdin_buf)
    emit_l2_ffn(ctx, W, cp, LC)
    emit_l2_attn(ctx, W, cp, LC, xd_in, xdin_buf, xd_out, xdout_buf)


LAYER_FNS[2] = emit_layer2
LAYER_INPUTS[2] = {
    "L2_attn_w_in": [3072, D], "L2_attn_b_in": [3072], "L2_attn_w_out": [D, D],
    "L2_attn_b_out": [D], "L2_ffn_norm_w": [D], "L2_ffn_w1": [4096, D],
    "L2_ffn_w3": [4096, D], "L2_ffn_w2": [D, 4096], "L2_ffn_gate_w": [8, D],
    "L2_ffn_a1": [8, 16, D], "L2_ffn_b1": [8, 4096, 16], "L2_ffn_a3": [8, 16, D],
    "L2_ffn_b3": [8, 4096, 16], "L2_ffn_a2": [8, 16, 4096], "L2_ffn_b2": [8, D, 16],
}



def emit_l3_ffn(ctx, W, cp, LC):
    cbf_b, cf_b = ctx.buf("cbf"), ctx.buf("cf32")
    SF0, SF1, SF2 = ctx.sb["SF0"], ctx.sb["SF1"], ctx.sb["SF2"]
    sb0, sb1, sb2, sb3 = ctx.sb["SB0"], ctx.sb["SB1"], ctx.sb["SB2"], ctx.sb["SB3"]
    wa, wdt = ctx.sb["WA"], ctx.sb["WD"]
    scl = ctx.sb["SCL"]
    tok = ctx.sb["TOK8"]
    w8 = ctx.sb["R8B"]
    oh_b = ctx.buf("R8B_l3oh")

    ct_b = ctx.buf("WD_l3c")
    for kc in range(DC):
        gtile = sb1[0:8, 0:128]
        gtileb = ctx.buf("SB1_l3c")
        ctx.op("pool", lambda e, kc=kc, gtile=gtile: e.dma_start(
            out=gtile, in_=W["L3_ffn_centroids"][0][:, kc * 128:(kc + 1) * 128]),
            reads=[W["L3_ffn_centroids"][1]], writes=[gtileb], kind="dma")
        ip = 4 + kc % 2
        ctx.op("pe", lambda e, gtile=gtile, ip=ip: e.transpose(
            ctx.ps[ip][:, 0:4].bitcast(BF16), gtile, cbf_part(ctx, "id128", 8, 0, 8)),
            reads=[gtileb, cbf_b], writes=[ctx.psb[ip]])
        ctx.op("dve", lambda e, kc=kc, ip=ip: e.tensor_copy(
            wdt[:, 1024 + kc * 8:1024 + (kc + 1) * 8], ctx.ps[ip][:, 0:4].bitcast(BF16)),
            reads=[ctx.psb[ip]], writes=[ct_b])
    alpha_row = ctx.sb["RWA"]
    alpha_b = ctx.buf("RWA_l3a")
    for pt in range(16):
        bank = pt % 2
        pb = ctx.psb[bank]
        def amm(e, pt=pt, bank=bank):
            inst = None
            for c in range(DC):
                XN = ctx.sb[f"XN{c}"]
                inst = e.matmul(ctx.ps[bank][:, 0:8], XN[:, pt * 128:(pt + 1) * 128],
                                wdt[:, 1024 + c * 8:1024 + (c + 1) * 8],
                                start=(c == 0), stop=(c == DC - 1))
            return inst
        ctx.op("pe", amm, reads=[ct_b] + [ctx.buf(f"XN{c}") for c in range(DC)], writes=[pb])
        tokb = ctx.buf("TOK8_l3")
        ctx.op("dve", lambda e, bank=bank: e.tensor_copy(tok[:, 0:8], ctx.ps[bank][:, 0:8]),
               reads=[pb], writes=[tokb])
        ctx.op("dve", lambda e: e.reduce_max(tok[:, 16:17], tok[:, 0:8], AX.X),
               reads=[tokb], writes=[tokb])
        ctx.op("dve", lambda e: e.tensor_scalar(tok[:, 8:16], tok[:, 0:8], tok[:, 16:17],
                                                None, ALU.is_equal),
               reads=[tokb], writes=[tokb])
        ctx.op("act", lambda e: e.activation(tok[:, 17:18], tok[:, 16:17], AF.Sigmoid),
               reads=[tokb], writes=[tokb])
        tp = ctx.psb[2 + pt % 2]
        ctx.op("pe", lambda e, pt=pt: e.transpose(ctx.ps[2 + pt % 2][0:8, 0:128],
                                                  tok[:, 8:16], cf32(ctx, "id128f")),
               reads=[tokb, cf_b], writes=[tp])
        ctx.op("act", lambda e, pt=pt: e.activation(
            w8[0:8, pt * 128:(pt + 1) * 128], ctx.ps[2 + pt % 2][0:8, 0:128], AF.Copy),
            reads=[tp], writes=[oh_b])
        tp2 = ctx.psb[4 + pt % 2]
        ctx.op("pe", lambda e, pt=pt: e.transpose(ctx.ps[4 + pt % 2][0:1, 0:128],
                                                  tok[:, 17:18], cf32(ctx, "id128f")),
               reads=[tokb, cf_b], writes=[tp2])
        ctx.op("act", lambda e, pt=pt: e.activation(
            alpha_row[0:1, pt * 128:(pt + 1) * 128], ctx.ps[4 + pt % 2][0:1, 0:128], AF.Copy),
            reads=[tp2], writes=[alpha_b])
    bcast_row_to_plane(ctx, alpha_row, alpha_b, "SF0", ctx.buf("SF0_l3al"))

    sqb = [ctx.buf("SB2_sq0"), ctx.buf("SB2_sq1")]
    for tt in range(TT):
        for c in range(DC):
            XN = ctx.sb[f"XN{c}"]
            qa = sb2[:, (c % 2) * 512:(c % 2) * 512 + 512]
            qb = sqb[c % 2]
            ctx.op("act", lambda e, XN=XN, qa=qa, tt=tt: e.activation(
                qa, XN[:, tt * 512:(tt + 1) * 512], AF.Square),
                reads=[ctx.buf(f"XN{c}")], writes=[qb])
            ctx.op("pe", lambda e, XN=XN, c=c, tt=tt: e.matmul(
                ctx.ps[tt][0:1, :], cbf(ctx, "ones_col"), XN[:, tt * 512:(tt + 1) * 512],
                start=(c == 0), stop=(c == DC - 1)),
                reads=[ctx.buf(f"XN{c}"), cbf_b], writes=[ctx.psb[tt]])
            ctx.op("pe", lambda e, qa=qa, c=c, tt=tt: e.matmul(
                ctx.ps[4 + tt][0:1, :], cbf(ctx, "ones_col"), qa,
                start=(c == 0), stop=(c == DC - 1)),
                reads=[qb, cbf_b], writes=[ctx.psb[4 + tt]])
    mu_row, var_row = ctx.sb["RWB"], ctx.sb["RWC"]
    mub, varb = ctx.buf("RWB_l3mu"), ctx.buf("RWC_l3v")
    for tt in range(4):
        sl = slice(tt * 512, (tt + 1) * 512)
        ctx.op("act", lambda e, sl=sl, tt=tt: e.activation(
            mu_row[:, sl], ctx.ps[tt][0:1, :], AF.Copy, scale=1.0 / D),
            reads=[ctx.psb[tt]], writes=[mub])
        msq = sb2[0:1, 0:512]
        msqb = ctx.buf("SB2_l3msq")
        ctx.op("act", lambda e, sl=sl, msq=msq: e.activation(msq, mu_row[:, sl], AF.Square),
               reads=[mub], writes=[msqb])
        ctx.op("dve", lambda e, sl=sl, tt=tt, msq=msq: e.scalar_tensor_tensor(
            var_row[:, sl], ctx.ps[4 + tt][0:1, :], 1.0 / D, msq, ALU.mult, ALU.subtract),
            reads=[ctx.psb[4 + tt], msqb], writes=[varb])
    ctx.op("act", lambda e: e.activation(var_row[:, :], var_row[:, :], AF.Sqrt,
                                         bias=cf32_part(ctx, "eps1e5", 1, 0, 1)),
           reads=[varb], writes=[varb])
    ctx.op("dve", lambda e: e.reciprocal(var_row[:, :], var_row[:, :]),
           reads=[varb], writes=[varb])
    bcast_row_to_plane(ctx, mu_row, mub, "SF1", ctx.buf("SF1_l3mu"))
    bcast_row_to_plane(ctx, var_row, varb, "SF2", ctx.buf("SF2_l3rs"))

    lng = sb1[0:8, 0:1024]
    lngb = ctx.buf("SB1_l3lg")
    ctx.op("pool", lambda e: e.dma_start(out=lng, in_=W["L3_ffn_ln_g"][0][:, :]),
           reads=[W["L3_ffn_ln_g"][1]], writes=[lngb], kind="dma")
    lnb = sb1[0:8, 1024:2048]
    lnbb = ctx.buf("SB1_l3lb")
    ctx.op("pool", lambda e: e.dma_start(out=lnb, in_=W["L3_ffn_ln_b"][0][:, :]),
           reads=[W["L3_ffn_ln_b"][1]], writes=[lnbb], kind="dma")
    xln_d = ctx.dram_scratch("l3_xln", [D, T], BF16).ap()
    xln_db = ctx.buf("l3xln", tensor="l3xln")
    for c in range(DC):
        XN = ctx.sb[f"XN{c}"]
        for tt in range(TT):
            tsl = slice(tt * 512, (tt + 1) * 512)
            pg, pbb = ctx.psb[0 + tt % 2], ctx.psb[2 + tt % 2]
            ctx.op("pe", lambda e, c=c, tt=tt, i=0 + tt % 2: e.matmul(
                ctx.ps[i][:, :], lng[0:8, c * 128:(c + 1) * 128],
                w8[0:8, tt * 512:(tt + 1) * 512], start=True, stop=True),
                reads=[lngb, oh_b], writes=[pg])
            ctx.op("pe", lambda e, c=c, tt=tt, i=2 + tt % 2: e.matmul(
                ctx.ps[i][:, :], lnb[0:8, c * 128 - 1024 + 1024:(c + 1) * 128 - 1024 + 1024],
                w8[0:8, tt * 512:(tt + 1) * 512], start=True, stop=True),
                reads=[lnbb, oh_b], writes=[pbb])
            t1 = sb2[:, (tt % 2) * 512:(tt % 2) * 512 + 512]
            t1b = ctx.buf(f"SB2_l3t{tt % 2}")
            ctx.op("dve", lambda e, XN=XN, t1=t1, tsl=tsl: e.tensor_tensor(
                t1, XN[:, tsl], SF1[:, tsl], ALU.subtract),
                reads=[ctx.buf(f"XN{c}"), ctx.buf("SF1_l3mu")], writes=[t1b])
            ctx.op("dve", lambda e, t1=t1, tsl=tsl: e.tensor_tensor(
                t1, t1, SF2[:, tsl], ALU.mult),
                reads=[t1b, ctx.buf("SF2_l3rs")], writes=[t1b])
            ctx.op("dve", lambda e, t1=t1, i=0 + tt % 2: e.tensor_tensor(
                t1, t1, ctx.ps[i][:, :], ALU.mult),
                reads=[t1b, pg], writes=[t1b])
            t2 = sb3[:, (tt % 2) * 512:(tt % 2) * 512 + 512]
            t2b = ctx.buf(f"SB3_l3t{tt % 2}")
            ctx.op("dve", lambda e, t1=t1, t2=t2, i=2 + tt % 2: e.tensor_tensor(
                t2, t1, ctx.ps[i][:, :], ALU.add),
                reads=[t1b, pbb], writes=[t2b])
            ctx.op("sp", lambda e, c=c, tsl=tsl, t2=t2: e.dma_start(
                out=xln_d[c * 128:(c + 1) * 128, tsl], in_=t2),
                reads=[t2b], writes=[xln_db], kind="dma")

    b2m = sb1[0:8, 0:1024]
    b2mb = ctx.buf("SB1_l3b2")
    for tt in range(TT):
        tsl = slice(tt * 512, (tt + 1) * 512)
        tslots = []
        for kc in range(DC):
            tens = sb0 if kc < 4 else sb1
            apo = tens[:, (kc % 4) * 512:(kc % 4) * 512 + 512]
            bo = ctx.buf(("SB0" if kc < 4 else "SB1") + f"_l3x{kc % 4}")
            tslots.append((apo, bo))
            ctx.op("sp", lambda e, kc=kc, apo=apo, tsl=tsl: e.dma_start(
                out=apo, in_=xln_d[kc * 128:(kc + 1) * 128, tsl]),
                reads=[xln_db], writes=[bo], kind="dma")
        O_slots = []
        for dm in range(DC):
            tens = SF1 if dm < 4 else SF2
            apo = tens[:, (dm % 4) * 512:(dm % 4) * 512 + 512]
            bo = ctx.buf(("SF1" if dm < 4 else "SF2") + f"_l3O{dm % 4}")
            O_slots.append((apo, bo))
            ctx.op("dve", lambda e, apo=apo: e.memset(apo, 0.0), writes=[bo])
        for e_i in range(8):
            if tt == 0 and e_i == 0:
                ctx._l3b1 = ColPool(ctx).alloc("l3b1", 32) if False else cp.alloc("l3b1", 32)
            b1_off, b1_b = load_feature_cols(ctx, cp, "l3b1",
                                             W["L3_ffn_b1"][0][e_i, :], W["L3_ffn_b1"][1],
                                             4096, reuse=ctx._l3b1)
            pw = ctx.psb[5]
            ctx.op("pe", lambda e, e_i=e_i, tt=tt: e.matmul(
                ctx.ps[5][:, :], lc_bf(ctx, LC["sel8"], 8, e_i * 128, 128),
                w8[0:8, tt * 512:(tt + 1) * 512], start=True, stop=True),
                reads=[oh_b, LC["sel8"][2]], writes=[pw])
            wbc = sb2[:, 1024:1536]
            wbcb = ctx.buf("SB2_l3oh")
            ctx.op("act", lambda e, wbc=wbc: e.activation(wbc, ctx.ps[5][:, :], AF.Copy),
                   reads=[pw], writes=[wbcb])
            for m in range(32):
                w1t = wa if m % 2 == 0 else ctx.sb["WC"]
                wab = ctx.buf(("WA" if m % 2 == 0 else "WC") + "_l3w1")
                cast_load(ctx, w1t[:, 0:1024],
                          W["L3_ffn_w1"][0][e_i, :, m * 128:(m + 1) * 128], wab,
                          W["L3_ffn_w1"][1], 8)
                iU = m % 2
                def umm(e, iU=iU, w1t=w1t):
                    inst = None
                    for kc in range(DC):
                        inst = e.matmul(ctx.ps[iU][:, :], w1t[:, kc * 128:(kc + 1) * 128],
                                        tslots[kc][0], start=(kc == 0), stop=(kc == DC - 1))
                    return inst
                ctx.op("pe", umm, reads=[wab] + [b for _, b in tslots], writes=[ctx.psb[iU]])
                hw = sb2[:, (m % 2) * 512:(m % 2) * 512 + 512]
                hwb = ctx.buf(f"SB2_l3h{m % 2}")
                cc = b1_off + m
                ctx.op("act", lambda e, hw=hw, iU=iU, cc=cc: e.activation(
                    hw, ctx.ps[iU][:, :], AF.Relu, bias=scl[:, cc:cc + 1]),
                    reads=[ctx.psb[iU], b1_b], writes=[hwb])
                hslot = ctx.sb[f"BIG{m // 4}"][:, (m % 4) * 512:(m % 4) * 512 + 512]
                hslot_b = ctx.buf(f"BIG{m // 4}_l3h{m % 4}")
                ctx.op("dve", lambda e, hslot=hslot, hw=hw, wbc=wbc: e.tensor_tensor(
                    hslot, hw, wbc, ALU.mult),
                    reads=[hwb, wbcb], writes=[hslot_b])
            for dm in range(DC):
                ta, tb2 = (wa, ctx.sb["WB"]) if dm % 2 == 0 else (ctx.sb["WC"], ctx.sb["WD"])
                na, nb2 = ("WA", "WB") if dm % 2 == 0 else ("WC", "WD")
                wcb2 = ctx.buf(na + "_l3w2")
                cast_load(ctx, ta[:, :],
                          W["L3_ffn_w2"][0][e_i, 0:2048, dm * 128:(dm + 1) * 128],
                          wcb2, W["L3_ffn_w2"][1], 16)
                wcb3 = ctx.buf(nb2 + "_l3w2")
                cast_load(ctx, tb2[:, :],
                          W["L3_ffn_w2"][0][e_i, 2048:4096, dm * 128:(dm + 1) * 128],
                          wcb3, W["L3_ffn_w2"][1], 16)
                iD = 6 + dm % 2
                def dmm(e, iD=iD, ta=ta, tb2=tb2):
                    inst = None
                    for kc in range(32):
                        tgt = ta if kc < 16 else tb2
                        inst = e.matmul(ctx.ps[iD][:, :],
                                        tgt[:, (kc % 16) * 128:(kc % 16 + 1) * 128],
                                        ctx.sb[f"BIG{kc // 4}"][:, (kc % 4) * 512:(kc % 4) * 512 + 512],
                                        start=(kc == 0), stop=(kc == 31))
                    return inst
                ctx.op("pe", dmm,
                       reads=[wcb2, wcb3] + [ctx.buf(f"BIG{kc // 4}_l3h{kc % 4}") for kc in range(32)],
                       writes=[ctx.psb[iD]])
                ctx.op("dve", lambda e, dm=dm, iD=iD: e.tensor_tensor(
                    O_slots[dm][0], O_slots[dm][0], ctx.ps[iD][:, :], ALU.add),
                    reads=[ctx.psb[iD], O_slots[dm][1]], writes=[O_slots[dm][1]])
        ctx.op("pool", lambda e: e.dma_start(out=b2m, in_=W["L3_ffn_b2"][0][:, :]),
               reads=[W["L3_ffn_b2"][1]], writes=[b2mb], kind="dma")
        for dm in range(DC):
            pb2 = ctx.psb[5]
            ctx.op("pe", lambda e, dm=dm, tt=tt: e.matmul(
                ctx.ps[5][:, :], b2m[0:8, dm * 128:(dm + 1) * 128],
                w8[0:8, tt * 512:(tt + 1) * 512], start=True, stop=True),
                reads=[b2mb, oh_b], writes=[pb2])
            ctx.op("dve", lambda e, dm=dm: e.tensor_tensor(
                O_slots[dm][0], O_slots[dm][0], ctx.ps[5][:, :], ALU.add),
                reads=[pb2, O_slots[dm][1]], writes=[O_slots[dm][1]])
            t3 = sb2[:, 0:512]
            t3b = ctx.buf("SB2_l3g1")
            ctx.op("dve", lambda e, dm=dm, t3=t3, tsl=tsl: e.tensor_tensor(
                t3, O_slots[dm][0], SF0[:, tsl], ALU.mult),
                reads=[O_slots[dm][1], ctx.buf("SF0_l3al")], writes=[t3b])
            XN = ctx.sb[f"XN{dm}"]
            t4 = sb2[:, 512:1024]
            t4b = ctx.buf("SB2_l3g2")
            ctx.op("dve", lambda e, t4=t4, t3=t3, XN=XN, tsl=tsl: e.tensor_tensor(
                t4, t3, XN[:, tsl], ALU.add),
                reads=[t3b, ctx.buf(f"XN{dm}")], writes=[t4b])
            G = ctx.sb[f"G{dm}"]
            gb = ctx.buf(f"G{dm}_{tt}")
            ctx.op("act", lambda e, G=G, tsl=tsl, t4=t4: e.activation(G[:, tsl], t4, AF.Silu),
                   reads=[t4b], writes=[gb])



def emit_l3_attn(ctx, W, cp, LC, xd_in, xdin_buf, xd_out, xdout_buf):
    cbf_b, cf_b = ctx.buf("cbf"), ctx.buf("cf32")
    SF0, SF1, SF2 = ctx.sb["SF0"], ctx.sb["SF1"], ctx.sb["SF2"]
    sb0, sb1, sb2, sb3 = ctx.sb["SB0"], ctx.sb["SB1"], ctx.sb["SB2"], ctx.sb["SB3"]
    wa, wb_, wc, wdt = ctx.sb["WA"], ctx.sb["WB"], ctx.sb["WC"], ctx.sb["WD"]
    scl = ctx.sb["SCL"]
    tok = ctx.sb["TOK8"]
    R64, R16A, R16B = ctx.sb["R64"], ctx.sb["R16A"], ctx.sb["R16B"]

    lng_off, lng_b = load_feature_cols(ctx, cp, "l3lg", W["L3_attn_ln_g"][0],
                                       W["L3_attn_ln_g"][1], D)
    lnb_off, lnb_b = load_feature_cols(ctx, cp, "l3lb", W["L3_attn_ln_b"][0],
                                       W["L3_attn_ln_b"][1], D)
    bin_off, bin_b = load_feature_cols(ctx, cp, "l3bin", W["L3_attn_b_in"][0],
                                       W["L3_attn_b_in"][1], 4096)
    bdt_off, bdt_b = load_feature_cols(ctx, cp, "l3bdt", W["L3_attn_b_dt"][0],
                                       W["L3_attn_b_dt"][1], 2048)
    dd_off, dd_b = load_feature_cols(ctx, cp, "l3D", W["L3_attn_D"][0],
                                     W["L3_attn_D"][1], 2048)
    bo_off, bo_b = load_feature_cols(ctx, cp, "l3bo", W["L3_attn_b_out"][0],
                                     W["L3_attn_b_out"][1], D)
    bd_off, bd_b = cp.alloc("l3bd", 1)
    bB_off, bB_b = cp.alloc("l3bB", 1)
    bC_off, bC_b = cp.alloc("l3bC", 1)
    rw = ctx.sb["RWC"]
    rwb2 = ctx.buf("RWC_l3sb")
    for nm, off_, b_, n_ in [("L3_attn_b_delta", bd_off, bd_b, 64),
                             ("L3_attn_b_B", bB_off, bB_b, 16),
                             ("L3_attn_b_C", bC_off, bC_b, 16)]:
        ctx.op("sp", lambda e, nm=nm, n_=n_: e.dma_start(out=rw[0:1, 0:n_], in_=W[nm][0][:]),
               reads=[W[nm][1]], writes=[rwb2], kind="dma")
        ip = 6
        ctx.op("pe", lambda e, n_=n_: e.transpose(
            ctx.ps[6][0:n_, 0:1], rw[0:1, 0:n_], cf32_part(ctx, "id128f", 1, 0, 1)),
            reads=[rwb2, cf_b], writes=[ctx.psb[6]])
        ctx.op("dve", lambda e, off_=off_, n_=n_: e.tensor_copy(
            scl[0:n_, off_:off_ + 1], ctx.ps[6][0:n_, 0:1]),
            reads=[ctx.psb[6]], writes=[b_])

    sqb = [ctx.buf("SB2_sq0"), ctx.buf("SB2_sq1")]
    for tt in range(TT):
        for c in range(DC):
            XN = ctx.sb[f"XN{c}"]
            qa = sb2[:, (c % 2) * 512:(c % 2) * 512 + 512]
            qb = sqb[c % 2]
            ctx.op("act", lambda e, XN=XN, qa=qa, tt=tt: e.activation(
                qa, XN[:, tt * 512:(tt + 1) * 512], AF.Square),
                reads=[ctx.buf(f"XN{c}")], writes=[qb])
            ctx.op("pe", lambda e, XN=XN, c=c, tt=tt: e.matmul(
                ctx.ps[tt][0:1, :], cbf(ctx, "ones_col"), XN[:, tt * 512:(tt + 1) * 512],
                start=(c == 0), stop=(c == DC - 1)),
                reads=[ctx.buf(f"XN{c}"), cbf_b], writes=[ctx.psb[tt]])
            ctx.op("pe", lambda e, qa=qa, c=c, tt=tt: e.matmul(
                ctx.ps[4 + tt][0:1, :], cbf(ctx, "ones_col"), qa,
                start=(c == 0), stop=(c == DC - 1)),
                reads=[qb, cbf_b], writes=[ctx.psb[4 + tt]])
    mu_row, var_row = ctx.sb["RWA"], ctx.sb["RWB"]
    mub, varb = ctx.buf("RWA_l3amu"), ctx.buf("RWB_l3av")
    for tt in range(4):
        sl = slice(tt * 512, (tt + 1) * 512)
        ctx.op("act", lambda e, sl=sl, tt=tt: e.activation(
            mu_row[:, sl], ctx.ps[tt][0:1, :], AF.Copy, scale=1.0 / D),
            reads=[ctx.psb[tt]], writes=[mub])
        msq = sb2[0:1, 0:512]
        msqb = ctx.buf("SB2_l3amsq")
        ctx.op("act", lambda e, sl=sl, msq=msq: e.activation(msq, mu_row[:, sl], AF.Square),
               reads=[mub], writes=[msqb])
        ctx.op("dve", lambda e, sl=sl, tt=tt, msq=msq: e.scalar_tensor_tensor(
            var_row[:, sl], ctx.ps[4 + tt][0:1, :], 1.0 / D, msq, ALU.mult, ALU.subtract),
            reads=[ctx.psb[4 + tt], msqb], writes=[varb])
    ctx.op("act", lambda e: e.activation(var_row[:, :], var_row[:, :], AF.Sqrt,
                                         bias=cf32_part(ctx, "eps1e5", 1, 0, 1)),
           reads=[varb], writes=[varb])
    ctx.op("dve", lambda e: e.reciprocal(var_row[:, :], var_row[:, :]),
           reads=[varb], writes=[varb])
    bcast_row_to_plane(ctx, mu_row, mub, "SF1", ctx.buf("SF1_l3amu"))
    bcast_row_to_plane(ctx, var_row, varb, "SF2", ctx.buf("SF2_l3ars"))
    for c in range(DC):
        XN = ctx.sb[f"XN{c}"]
        t1 = sb2[:, 0:T]
        t1b = ctx.buf("SB2_l3xln")
        ctx.op("dve", lambda e, XN=XN, t1=t1: e.tensor_tensor(
            t1, XN[:, :], SF1[:, :], ALU.subtract),
            reads=[ctx.buf(f"XN{c}"), ctx.buf("SF1_l3amu")], writes=[t1b])
        ctx.op("dve", lambda e, t1=t1: e.tensor_tensor(t1, t1, SF2[:, :], ALU.mult),
               reads=[t1b, ctx.buf("SF2_l3ars")], writes=[t1b])
        XL = ctx.sb[f"BIG{c}"]
        xlb = ctx.buf(f"BIG{c}_xl")
        ctx.op("dve", lambda e, t1=t1, XL=XL, c=c: e.tensor_scalar(
            XL[:, :], t1, scl[:, lng_off + c:lng_off + c + 1],
            scl[:, lnb_off + c:lnb_off + c + 1], ALU.mult, ALU.add),
            reads=[t1b, lng_b, lnb_b], writes=[xlb])

    u_d = ctx.dram_scratch("l3_u", [2048, T], BF16).ap()
    u_db = ctx.buf("l3u", tensor="l3u")
    delta_d = ctx.dram_scratch("l3_delta", [2048, T], BF16).ap()
    delta_db = ctx.buf("l3delta", tensor="l3delta")
    du_d = ctx.dram_scratch("l3_du", [2048, T], BF16).ap()
    du_db = ctx.buf("l3du", tensor="l3du")
    y3_d = ctx.dram_scratch("l3_y3", [2048, T], BF16).ap()
    y3_db = ctx.buf("l3y3", tensor="l3y3")

    p1b = ctx.buf("R64_p1")
    bmb = ctx.buf("R16A_bm")
    cmb = ctx.buf("R16B_cm")
    ctx.op("dve", lambda e: e.memset(R64[:, :], 0.0), writes=[p1b])
    ctx.op("dve", lambda e: e.memset(R16A[:, :], 0.0), writes=[bmb])
    ctx.op("dve", lambda e: e.memset(R16B[:, :], 0.0), writes=[cmb])
    xl_reads = [ctx.buf(f"BIG{c}_xl") for c in range(DC)]
    for ec in range(16):
        wut = wa if ec % 2 == 0 else wc
        wab = ctx.buf(("WA" if ec % 2 == 0 else "WC") + "_l3win")
        cast_load(ctx, wut[:, 0:1024], W["L3_attn_W_in"][0][:, ec * 128:(ec + 1) * 128],
                  wab, W["L3_attn_W_in"][1], 8)
        wdb = ctx.buf("WD_l3wd")
        ctx.op("pool", lambda e, ec=ec: e.dma_start(
            out=wdt[:, 0:64], in_=W["L3_attn_W_delta"][0][ec * 128:(ec + 1) * 128, :]),
            reads=[W["L3_attn_W_delta"][1]], writes=[wdb], kind="dma")
        wbb2 = ctx.buf("WD_l3wB")
        ctx.op("pool", lambda e, ec=ec: e.dma_start(
            out=wdt[:, 64:80], in_=W["L3_attn_W_B"][0][ec * 128:(ec + 1) * 128, :]),
            reads=[W["L3_attn_W_B"][1]], writes=[wbb2], kind="dma")
        wcb2 = ctx.buf("WD_l3wC")
        ctx.op("pool", lambda e, ec=ec: e.dma_start(
            out=wdt[:, 80:96], in_=W["L3_attn_W_C"][0][ec * 128:(ec + 1) * 128, :]),
            reads=[W["L3_attn_W_C"][1]], writes=[wcb2], kind="dma")
        ub = ctx.buf("SB0_l3u")
        for tt in range(TT):
            tsl = slice(tt * 512, (tt + 1) * 512)
            iU = tt % 2
            def ummm(e, iU=iU, tsl=tsl, wut=wut):
                inst = None
                for kc in range(DC):
                    inst = e.matmul(ctx.ps[iU][:, :], wut[:, kc * 128:(kc + 1) * 128],
                                    ctx.sb[f"BIG{kc}"][:, tsl],
                                    start=(kc == 0), stop=(kc == DC - 1))
                return inst
            ctx.op("pe", ummm, reads=[wab] + xl_reads, writes=[ctx.psb[iU]])
            cc = bin_off + ec
            ctx.op("act", lambda e, tsl=tsl, iU=iU, cc=cc: e.activation(
                sb0[:, tsl], ctx.ps[iU][:, :], AF.Silu, bias=scl[:, cc:cc + 1]),
                reads=[ctx.psb[iU], bin_b], writes=[ub])
            pp = ctx.psb[2 + tt % 2]
            def pmm2(e, tsl=tsl, i=2 + tt % 2):
                e.matmul(ctx.ps[i][0:64, :], wdt[:, 0:64], sb0[:, tsl], start=True, stop=True)
                e.matmul(ctx.ps[i + 2 if False else 4][0:16, :], wdt[:, 64:80], sb0[:, tsl],
                         start=True, stop=True)
                return e.matmul(ctx.ps[5][0:16, :], wdt[:, 80:96], sb0[:, tsl],
                                start=True, stop=True)
            ctx.op("pe", pmm2, reads=[wdb, wbb2, wcb2, ub],
                   writes=[pp, ctx.psb[4], ctx.psb[5]])
            ctx.op("dve", lambda e, tsl=tsl, i=2 + tt % 2: e.tensor_tensor(
                R64[:, tsl], R64[:, tsl], ctx.ps[i][0:64, :], ALU.add),
                reads=[pp, p1b], writes=[p1b])
            ctx.op("dve", lambda e, tsl=tsl: e.tensor_tensor(
                R16A[0:16, tsl], R16A[0:16, tsl], ctx.ps[4][0:16, :], ALU.add),
                reads=[ctx.psb[4], bmb], writes=[bmb])
            ctx.op("dve", lambda e, tsl=tsl: e.tensor_tensor(
                R16B[0:16, tsl], R16B[0:16, tsl], ctx.ps[5][0:16, :], ALU.add),
                reads=[ctx.psb[5], cmb], writes=[cmb])
        ctx.op("sp", lambda e, ec=ec: e.dma_start(out=u_d[ec * 128:(ec + 1) * 128, :],
                                                  in_=sb0[:, :]),
               reads=[ub], writes=[u_db], kind="dma")
    ctx.op("dve", lambda e: e.tensor_scalar(R64[:, :], R64[:, :],
                                            scl[0:64, bd_off:bd_off + 1], None, ALU.add),
           reads=[bd_b, p1b], writes=[p1b])
    ctx.op("dve", lambda e: e.tensor_scalar(R16A[0:16, :], R16A[0:16, :],
                                            scl[0:16, bB_off:bB_off + 1], None, ALU.add),
           reads=[bB_b, bmb], writes=[bmb])
    ctx.op("dve", lambda e: e.tensor_scalar(R16B[0:16, :], R16B[0:16, :],
                                            scl[0:16, bC_off:bC_off + 1], None, ALU.add),
           reads=[bC_b, cmb], writes=[cmb])

    for ec in range(16):
        wdtb = ctx.buf("WD_l3wdt")
        ctx.op("pool", lambda e, ec=ec: e.dma_start(
            out=wdt[0:64, 128:256], in_=W["L3_attn_W_dt"][0][:, ec * 128:(ec + 1) * 128]),
            reads=[W["L3_attn_W_dt"][1]], writes=[wdtb], kind="dma")
        ub2 = ctx.buf("SB0_l3u2")
        ctx.op("sp", lambda e, ec=ec: e.dma_start(out=sb0[:, :],
                                                  in_=u_d[ec * 128:(ec + 1) * 128, :]),
               reads=[u_db], writes=[ub2], kind="dma")
        dlb = ctx.buf("SB1_l3dl")
        dub = ctx.buf("SB2_l3du")
        for tt in range(TT):
            tsl = slice(tt * 512, (tt + 1) * 512)
            iD = tt % 2
            ctx.op("pe", lambda e, tsl=tsl, iD=iD: e.matmul(
                ctx.ps[iD][:, :], wdt[0:64, 128:256], R64[0:64, tsl], start=True, stop=True),
                reads=[wdtb, p1b], writes=[ctx.psb[iD]])
            cc = bdt_off + ec
            t0 = SF0[:, 0:512]
            t0b = ctx.buf("SF0_l3t0")
            ctx.op("dve", lambda e, iD=iD, cc=cc, t0=t0: e.tensor_scalar(
                t0, ctx.ps[iD][:, :], scl[:, cc:cc + 1], None, ALU.add),
                reads=[ctx.psb[iD], bdt_b], writes=[t0b])
            ab_ = SF0[:, 512:1024]
            abb = ctx.buf("SF0_l3ab")
            ctx.op("act", lambda e, t0=t0, ab_=ab_: e.activation(ab_, t0, AF.Abs),
                   reads=[t0b], writes=[abb])
            ex_ = SF0[:, 1024:1536]
            exb = ctx.buf("SF0_l3ex")
            ctx.op("act", lambda e, ab_=ab_, ex_=ex_: e.activation(ex_, ab_, AF.Exp,
                                                                   scale=-1.0),
                   reads=[abb], writes=[exb])
            ln_ = SF0[:, 1536:2048]
            lnb2 = ctx.buf("SF0_l3ln")
            ctx.op("act", lambda e, ex_=ex_, ln_=ln_: e.activation(
                ln_, ex_, AF.Ln, bias=cf32_part(ctx, "one", 128, 0, 1)),
                reads=[exb], writes=[lnb2])
            r_ = SF2[:, 0:512]
            rb_ = ctx.buf("SF2_l3r")
            ctx.op("dve", lambda e, t0=t0, r_=r_: e.tensor_scalar(r_, t0, 0.0, None, ALU.max),
                   reads=[t0b], writes=[rb_])
            ctx.op("dve", lambda e, ln_=ln_, r_=r_, tsl=tsl: e.tensor_tensor(
                sb1[:, tsl], ln_, r_, ALU.add),
                reads=[lnb2, rb_], writes=[dlb])
            ctx.op("dve", lambda e, tsl=tsl: e.tensor_tensor(
                sb2[:, tsl], sb1[:, tsl], sb0[:, tsl], ALU.mult),
                reads=[dlb, ub2], writes=[dub])
        ctx.op("sp", lambda e, ec=ec: e.dma_start(out=delta_d[ec * 128:(ec + 1) * 128, :],
                                                  in_=sb1[:, :]),
               reads=[dlb], writes=[delta_db], kind="dma")
        ctx.op("sp", lambda e, ec=ec: e.dma_start(out=du_d[ec * 128:(ec + 1) * 128, :],
                                                  in_=sb2[:, :]),
               reads=[dub], writes=[du_db], kind="dma")

    for ec in range(16):
        dlb2 = ctx.buf("SB0_l3dl2")
        ctx.op("sp", lambda e, ec=ec: e.dma_start(out=sb0[:, :],
                                                  in_=delta_d[ec * 128:(ec + 1) * 128, :]),
               reads=[delta_db], writes=[dlb2], kind="dma")
        dub2 = ctx.buf("SB1_l3du2")
        ctx.op("sp", lambda e, ec=ec: e.dma_start(out=sb1[:, :],
                                                  in_=du_d[ec * 128:(ec + 1) * 128, :]),
               reads=[du_db], writes=[dub2], kind="dma")
        tkb = ctx.buf("TOK8_l3A")
        ctx.op("sp", lambda e, ec=ec: e.dma_start(
            out=tok[:, 16:32], in_=W["L3_attn_A_log"][0][ec * 128:(ec + 1) * 128, :]),
            reads=[W["L3_attn_A_log"][1]], writes=[tkb], kind="dma")
        ctx.op("act", lambda e: e.activation(tok[:, 32:48], tok[:, 16:32], AF.Exp),
               reads=[tkb], writes=[tkb])
        ctx.op("dve", lambda e: e.tensor_scalar(tok[:, 0:16], tok[:, 32:48], -1.0, None,
                                                ALU.mult),
               reads=[tkb], writes=[tkb])
        yab = ctx.buf("SF1_l3y")
        ctx.op("dve", lambda e: e.memset(SF1[:, :], 0.0), writes=[yab])
        for n in range(16):
            for tt in range(TT):
                tsl = slice(tt * 512, (tt + 1) * 512)
                iB = 4 + tt % 2
                ctx.op("pe", lambda e, n=n, tsl=tsl, iB=iB: e.matmul(
                    ctx.ps[iB][:, :], lc_bf(ctx, LC["sel16"], 16, n * 128, 128),
                    R16A[0:16, tsl], start=True, stop=True),
                    reads=[bmb, LC["sel16"][2]], writes=[ctx.psb[iB]])
                a_ = SF0[:, 0:512]
                a_b = ctx.buf("SF0_l3a")
                ctx.op("act", lambda e, n=n, tsl=tsl, a_=a_: e.activation(
                    a_, sb0[:, tsl], AF.Exp, scale=tok[:, n:n + 1]),
                    reads=[dlb2, tkb], writes=[a_b])
                b_2 = SF0[:, 512:1024]
                b_2b = ctx.buf("SF0_l3b")
                ctx.op("dve", lambda e, tsl=tsl, iB=iB, b_2=b_2: e.tensor_tensor(
                    b_2, sb1[:, tsl], ctx.ps[iB][:, :], ALU.mult),
                    reads=[dub2, ctx.psb[iB]], writes=[b_2b])
                h_ = SF0[:, 1024 + (tt % 2) * 512:1024 + (tt % 2) * 512 + 512]
                h_b = ctx.buf(f"SF0_l3hh{tt % 2}")
                if tt == 0:
                    init = 0.0
                else:
                    prev = SF0[:, 1024 + ((tt - 1) % 2) * 512:1024 + ((tt - 1) % 2) * 512 + 512]
                    init = prev[:, 511:512]
                ctx.op("dve", lambda e, a_=a_, b_2=b_2, h_=h_, init=init: e.tensor_tensor_scan(
                    h_, a_, b_2, init, ALU.mult, ALU.add),
                    reads=[a_b, b_2b] + ([ctx.buf(f"SF0_l3hh{(tt - 1) % 2}")] if tt > 0 else []),
                    writes=[h_b])
                iC = 6 + tt % 2
                ctx.op("pe", lambda e, n=n, tsl=tsl, iC=iC: e.matmul(
                    ctx.ps[iC][:, :], lc_bf(ctx, LC["sel16"], 16, n * 128, 128),
                    R16B[0:16, tsl], start=True, stop=True),
                    reads=[cmb, LC["sel16"][2]], writes=[ctx.psb[iC]])
                t_ = SF2[:, 0:512]
                t_b = ctx.buf("SF2_l3t")
                ctx.op("dve", lambda e, h_=h_, iC=iC, t_=t_: e.tensor_tensor(
                    t_, h_, ctx.ps[iC][:, :], ALU.mult),
                    reads=[h_b, ctx.psb[iC]], writes=[t_b])
                ctx.op("pool", lambda e, tsl=tsl, t_=t_: e.tensor_tensor(
                    SF1[:, tsl], SF1[:, tsl], t_, ALU.add),
                    reads=[t_b, yab], writes=[yab])
        ub3 = ctx.buf("SB2_l3u3")
        ctx.op("sp", lambda e, ec=ec: e.dma_start(out=sb2[:, :],
                                                  in_=u_d[ec * 128:(ec + 1) * 128, :]),
               reads=[u_db], writes=[ub3], kind="dma")
        y2b = ctx.buf("SF2_l3y2")
        cc = dd_off + ec // 8 * 8 + ec % 8 if False else dd_off + ec
        ctx.op("dve", lambda e, cc=cc: e.scalar_tensor_tensor(
            SF2[:, :], sb2[:, :], scl[:, cc:cc + 1], SF1[:, :], ALU.mult, ALU.add),
            reads=[ub3, dd_b, yab], writes=[y2b])
        wgt2 = wa if ec % 2 == 0 else wc
        wab2 = ctx.buf(("WA" if ec % 2 == 0 else "WC") + "_l3g")
        cast_load(ctx, wgt2[:, 0:1024],
                  W["L3_attn_W_in"][0][:, 2048 + ec * 128:2048 + (ec + 1) * 128],
                  wab2, W["L3_attn_W_in"][1], 8)
        y3b = ctx.buf("SB1_l3y3")
        for tt in range(TT):
            tsl = slice(tt * 512, (tt + 1) * 512)
            iG = tt % 2
            def gmm(e, iG=iG, tsl=tsl, wgt2=wgt2):
                inst = None
                for kc in range(DC):
                    inst = e.matmul(ctx.ps[iG][:, :], wgt2[:, kc * 128:(kc + 1) * 128],
                                    ctx.sb[f"BIG{kc}"][:, tsl],
                                    start=(kc == 0), stop=(kc == DC - 1))
                return inst
            ctx.op("pe", gmm, reads=[wab2] + xl_reads, writes=[ctx.psb[iG]])
            cc2 = bin_off + 16 + ec
            sg = sb3[:, (tt % 2) * 512:(tt % 2) * 512 + 512]
            sgb = ctx.buf(f"SB3_l3sg{tt % 2}")
            ctx.op("act", lambda e, iG=iG, cc2=cc2, sg=sg: e.activation(
                sg, ctx.ps[iG][:, :], AF.Silu, bias=scl[:, cc2:cc2 + 1]),
                reads=[ctx.psb[iG], bin_b], writes=[sgb])
            ctx.op("dve", lambda e, tsl=tsl, sg=sg: e.tensor_tensor(
                sb1[:, tsl], SF2[:, tsl], sg, ALU.mult),
                reads=[y2b, sgb], writes=[y3b])
        ctx.op("sp", lambda e, ec=ec: e.dma_start(out=y3_d[ec * 128:(ec + 1) * 128, :],
                                                  in_=sb1[:, :]),
               reads=[y3b], writes=[y3_db], kind="dma")

    for qt in range(TT):
        qsl = slice(qt * 512, (qt + 1) * 512)
        y_slots = []
        for kc in range(16):
            tens = ctx.sb[f"BIG{kc // 4}"]
            apo = tens[:, (kc % 4) * 512:(kc % 4) * 512 + 512]
            bo = ctx.buf(f"BIG{kc // 4}_l3y{kc % 4}")
            y_slots.append((apo, bo))
            ctx.op("sp", lambda e, apo=apo, kc=kc, qsl=qsl: e.dma_start(
                out=apo, in_=y3_d[kc * 128:(kc + 1) * 128, qsl]),
                reads=[y3_db], writes=[bo], kind="dma")
        for mo in range(DC):
            wot = wa if mo % 2 == 0 else wc
            wob = ctx.buf(("WA" if mo % 2 == 0 else "WC") + "_l3wo")
            cast_load(ctx, wot[:, :], W["L3_attn_W_out"][0][:, mo * 128:(mo + 1) * 128],
                      wob, W["L3_attn_W_out"][1], 16)
            iY = mo % 2
            pY = ctx.psb[iY]
            def womm(e, iY=iY, y_slots=y_slots, wot=wot):
                inst = None
                for kc in range(16):
                    inst = e.matmul(ctx.ps[iY][:, :], wot[:, kc * 128:(kc + 1) * 128],
                                    y_slots[kc][0], start=(kc == 0), stop=(kc == 15))
                return inst
            ctx.op("pe", womm, reads=[wob] + [b for _, b in y_slots], writes=[pY])
            XN = ctx.sb[f"XN{mo}"]
            st_ = sb3[:, 0:512]
            stb = ctx.buf("SB3_l3cs")
            cc = bo_off + mo
            ctx.op("dve", lambda e, st_=st_, iY=iY, cc=cc, XN=XN, qsl=qsl: e.scalar_tensor_tensor(
                st_, ctx.ps[iY][:, :], scl[:, cc:cc + 1], XN[:, qsl], ALU.add, ALU.add),
                reads=[pY, bo_b, ctx.buf(f"XN{mo}")], writes=[stb])
            G = ctx.sb[f"G{mo}"]
            tt_ = sb3[:, 512:1024]
            ttb = ctx.buf("SB3_l3ct")
            ctx.op("dve", lambda e, tt_=tt_, st_=st_, G=G, qsl=qsl: e.tensor_tensor(
                tt_, st_, G[:, qsl], ALU.mult),
                reads=[stb, ctx.buf(f"G{mo}_{qt}")], writes=[ttb])
            xst = SF1[:, (mo % 2) * 512:(mo % 2) * 512 + 512]
            xstb = ctx.buf(f"SF1_x{mo % 2}")
            ctx.op("sp", lambda e, xst=xst, mo=mo, qsl=qsl: e.dma_start(
                out=xst, in_=xd_in[mo * 128:(mo + 1) * 128, qsl]),
                reads=[xdin_buf], writes=[xstb], kind="dma")
            xon = SF2[:, (mo % 2) * 512:(mo % 2) * 512 + 512]
            xonb = ctx.buf(f"SF2_xo{mo % 2}")
            ctx.op("dve", lambda e, xon=xon, xst=xst, tt_=tt_: e.tensor_tensor(
                xon, xst, tt_, ALU.add),
                reads=[xstb, ttb], writes=[xonb])
            ctx.op("sp", lambda e, xon=xon, mo=mo, qsl=qsl: e.dma_start(
                out=xd_out[mo * 128:(mo + 1) * 128, qsl], in_=xon),
                reads=[xonb], writes=[xdout_buf], kind="dma")


def emit_layer3(ctx, W, xd_in, xdin_buf, xd_out, xdout_buf):
    cp = ColPool(ctx)
    emit_ln_xn(ctx, xd_in, xdin_buf)
    LC1 = load_lconsts(ctx, names_bf=["sel8"])
    emit_l3_ffn(ctx, W, cp, LC1)
    LC2 = load_lconsts(ctx, names_bf=["sel16"])
    emit_l3_attn(ctx, W, cp, LC2, xd_in, xdin_buf, xd_out, xdout_buf)


LAYER_FNS[3] = emit_layer3
LAYER_INPUTS[3] = {
    "L3_attn_ln_g": [D], "L3_attn_ln_b": [D], "L3_attn_W_in": [D, 4096],
    "L3_attn_b_in": [4096], "L3_attn_W_delta": [2048, 64], "L3_attn_b_delta": [64],
    "L3_attn_W_dt": [64, 2048], "L3_attn_b_dt": [2048], "L3_attn_W_B": [2048, 16],
    "L3_attn_b_B": [16], "L3_attn_W_C": [2048, 16], "L3_attn_b_C": [16],
    "L3_attn_A_log": [2048, 16], "L3_attn_D": [2048], "L3_attn_W_out": [2048, D],
    "L3_attn_b_out": [D],
    "L3_ffn_centroids": [8, D], "L3_ffn_ln_g": [8, D], "L3_ffn_ln_b": [8, D],
    "L3_ffn_w1": [8, D, 4096], "L3_ffn_b1": [8, 4096], "L3_ffn_w2": [8, 4096, D],
    "L3_ffn_b2": [8, D],
}
